# version 04
# baseline (speedup 1.0000x reference)
import contextlib
import numpy as np
import ml_dtypes
import concourse.bass as bass
import concourse.mybir as mybir
from concourse.bass_utils import run_bass_kernel_spmd

F32 = mybir.dt.float32
BF16 = mybir.dt.bfloat16
AF = mybir.ActivationFunctionType
ALU = mybir.AluOpType
AX = mybir.AxisListType
BF = ml_dtypes.bfloat16

B, L, D = 32, 2048, 1024
NCORE = 8
BPC = B // NCORE
NH, HD, CH = 4, 256, 64
I3 = 2816
PT = 256
NG = PT // 128
NCHU = PT // CH
NPAIR = 2
NPIECE = L // PT
DC = D // 128
KC3 = I3 // 128
NB3 = I3 // 16
OSL = [(i * 256, 256) for i in range(11)]
EPS = np.float32(1e-5)
NQPOOL = 8
SDPOOL = [f"sd{i}" for i in range(NQPOOL)]
GDPOOL = [f"gd{i}" for i in range(NQPOOL)]
QSEM = {"vector": "vv", "scalar": "aa", "tensor": "tt"}


class Bld:
    def __init__(self, dbg=False, nseq=BPC, npiece=NPIECE, serial=True):
        self.nc = bass.Bass("TRN2", target_bir_lowering=False,
                            detect_race_conditions=False)
        self.dbg = dbg
        self.nseq = nseq
        self.npiece = npiece
        self.serial = serial
        self.cnt = {}
        self.ev = {}
        self.hand = {}
        self.dbg_bufs = {}
        self.rr_sd = 0
        self.rr_gd = 0
        self.cursem = None
        self.watermark = {}

    def mark(self, inst, sem, label=None, step=1):
        self.cnt[sem] += step
        inst.then_inc(self.hand[sem], step)
        if label:
            self.ev[label] = (sem, self.cnt[sem])
        return (sem, self.cnt[sem])

    def cur(self, sem):
        return (sem, self.cnt[sem])

    def wait(self, eng, *evs):
        for e in evs:
            sem, val = self.ev[e] if isinstance(e, str) else e
            if val > 0:
                eng.wait_ge(self.hand[sem], val)

    def barrier_evs(self):
        return [self.cur(s) for s in
                ("vv", "aa", "tt", *SDPOOL, *GDPOOL)]

    _GUARD_OPS = frozenset((
        "tensor_tensor", "tensor_scalar", "tensor_copy", "tensor_reduce",
        "memset", "scalar_tensor_tensor", "tensor_tensor_scan", "activation",
        "copy", "tensor_add", "tensor_mul", "tensor_sub", "tensor_max",
        "select", "reciprocal", "iota"))

    class _Guard:
        def __init__(self, eng, bld, sem):
            self._eng = eng
            self._bld = bld
            self._sem = sem

        def __getattr__(self, name):
            attr = getattr(self._eng, name)
            if name in Bld._GUARD_OPS:
                def wrap(*args, **kw):
                    inst = attr(*args, **kw)
                    b = self._bld
                    b.cnt[self._sem] += 1
                    inst.then_inc(b.hand[self._sem], 1)
                    self._eng.wait_ge(b.hand[self._sem], b.cnt[self._sem])
                    return inst
                return wrap
            return attr

    def sec(self, engname, waits, fn, label=None, step=None):
        if engname == "sync":
            sem = SDPOOL[self.rr_sd % NQPOOL]
            self.rr_sd += 1
            step = 16
        elif engname == "gpsimd":
            sem = GDPOOL[self.rr_gd % NQPOOL]
            self.rr_gd += 1
            step = 16
        else:
            sem = QSEM[engname]
            if step is None:
                step = 1
        pr = self

        def body(eng):
            pr.cursem = sem
            if pr.serial:
                wm = pr.watermark.setdefault(engname, {})
                for s2 in sorted(pr.cnt):
                    c2 = pr.cnt[s2]
                    if c2 > wm.get(s2, 0):
                        eng.wait_ge(pr.hand[s2], c2)
                        wm[s2] = c2
            pr.wait(eng, *waits)
            if engname in ("vector", "scalar"):
                fn(Bld._Guard(eng, pr, sem))
                if label is not None:
                    pr.ev[label] = (sem, pr.cnt[sem])
            else:
                inst = fn(eng)
                if label is not None:
                    pr.mark(inst, sem, label, step)
        getattr(self.block, engname)(body)

    def dma(self, eng, qsem_ignored, out, in_, transpose=False, label=None):
        if transpose:
            inst = eng.dma_start_transpose(out, in_)
        else:
            inst = eng.dma_start(out=out, in_=in_)
        return self.mark(inst, self.cursem, label, 16)

    def declare(self):
        dram = self.nc.dram_tensor
        self.dx = dram("x", [BPC, L, D], F32, kind="ExternalInput").ap()
        self.dout = dram("out", [BPC, L, D], F32, kind="ExternalOutput").ap()
        self.dcc = dram("cc", [BPC, D], F32, kind="ExternalInput").ap()
        self.dwmod = dram("wmod", [D, 6 * D], F32, kind="ExternalInput").ap()
        self.dW = [dram(n, [D, D], F32, kind="ExternalInput").ap()
                   for n in ("Wq", "Wk", "Wv", "Wo")]
        self.dWb = dram("Wb", [D, NH], F32, kind="ExternalInput").ap()
        self.dwg = dram("wg", [I3, D], F32, kind="ExternalInput").ap()
        self.dwu = dram("wu", [I3, D], F32, kind="ExternalInput").ap()
        self.dwd = dram("wd", [D, I3], F32, kind="ExternalInput").ap()
        self.dcdiag = dram("cdiag", [12, DC, 128, 128], F32, kind="ExternalInput").ap()
        self.dgcol = dram("gcol", [128, 2], F32, kind="ExternalInput").ap()
        self.deye = dram("eye", [128, 128], F32, kind="ExternalInput").ap()
        self.deyeb = dram("eyeb", [128, 128], BF16, kind="ExternalInput").ap()
        self.dmask3 = dram("mask3", [3, 128, 128], BF16, kind="ExternalInput").ap()
        self.dhsel = dram("hsel", [NH, NH * 128], F32, kind="ExternalInput").ap()
        self.dcolsel = dram("colsel", [128, NH * NH], BF16, kind="ExternalInput").ap()
        self.dones4 = dram("ones4", [NH, 1], F32, kind="ExternalInput").ap()
        self.damask = dram("amask", [1, BPC * CH], F32, kind="ExternalInput").ap()
        self.swattn = dram("swattn", [4, D, D], BF16, kind="Internal").ap()
        self.swg = dram("swgq", [I3, D], BF16, kind="Internal").ap()
        self.swu = dram("swuq", [I3, D], BF16, kind="Internal").ap()
        self.swd = dram("swdq", [D, I3], BF16, kind="Internal").ap()
        self.smod = dram("smod", [BPC, 6, D], F32, kind="Internal").ap()
        if self.dbg:
            for n, sh, dt in [("d_h", [128, NG * D], BF16), ("d_hf", [128, DC * PT], BF16),
                              ("d_q", [128, DC * PT], BF16), ("d_k", [128, DC * PT], BF16),
                              ("d_v", [128, DC * PT], BF16), ("d_kb", [128, DC * PT], BF16),
                              ("d_m", [128, NPAIR * 128], F32),
                              ("d_tt", [128, NPAIR * 128], BF16),
                              ("d_w", [128, DC * PT], BF16), ("d_u", [128, DC * PT], BF16),
                              ("d_pp", [128, 2 * DC * NCHU], F32),
                              ("d_o", [128, DC * PT], BF16),
                              ("d_x2", [128, NG * D], F32), ("d_h2q", [128, NG * D], BF16),
                              ("d_hm", [128, I3], BF16), ("d_hmq", [128, I3], BF16),
                              ("d_ab", [128, PT], BF16),
                              ("d_mod", [128, BPC * 6 * DC], F32)]:
                self.dbg_bufs[n] = dram(n, sh, dt, kind="ExternalOutput").ap()

    def alloc(self, es):
        nc = self.nc
        T_ = lambda n, sh, dt: es.enter_context(nc.sbuf_tensor(n, sh, dt))
        self.eye_b = T_("eye_b", [128, 128], BF16)
        self.eye_f = T_("eye_f", [128, 128], F32)
        self.mask3 = T_("mask3_s", [128, 3 * 128], BF16)
        self.hsel = T_("hsel_s", [NH, NH * 128], F32)
        self.colsel = T_("colsel_s", [128, NH * NH], BF16)
        self.ones4 = T_("ones4_s", [NH, 1], F32)
        self.amask = T_("amask_s", [1, BPC * CH], F32)
        self.gcol = T_("gcol_s", [128, 2], F32)
        self.diag = T_("diag", [128, 12 * DC * 128], BF16)
        self.wb_sb = T_("wb_sb", [128, DC * NH], BF16)
        self.cT = T_("cT", [128, DC * BPC], BF16)
        self.modc = T_("modc", [128, BPC * 6 * DC], F32)
        self.modt = T_("modt", [128, 6 * D], BF16)

        self.wql = T_("wql", [128, DC * D], BF16)
        self.wfs = T_("wfs", [128, DC * 256], BF16)
        self.wfs2 = T_("wfs2", [128, DC * 256], BF16)
        self.wds = T_("wds", [128, KC3 * 256], BF16)
        self.x_t = T_("x_t", [128, NG * D], F32)
        self.sq_t = T_("sq_t", [128, D], F32)
        self.st = T_("st", [128, 32], F32)
        self.h_t = T_("h_t", [128, NG * D], BF16)
        self.h_f = T_("h_f", [128, DC * PT], BF16)
        self.p_f = T_("p_f", [128, DC * (PT + 3)], BF16)
        self.ptail = T_("ptail", [128, 3 * DC * 3], BF16)
        self.q_f = T_("q_f", [128, DC * PT], BF16)
        self.k_f = T_("k_f", [128, DC * PT], BF16)
        self.kb_f = T_("kb_f", [128, DC * PT], BF16)
        self.v_f = T_("v_f", [128, DC * PT], BF16)
        self.kb_t = T_("kb_t", [128, NG * D], BF16)
        self.w_f = T_("w_f", [128, DC * PT], BF16)
        self.u_f = T_("u_f", [128, DC * PT], BF16)
        self.sc_f = T_("sc_f", [128, DC * PT], F32)
        self.stgA = self.x_t
        self.stgB = self.sc_f
        self.a_b = T_("a_b", [128, PT], BF16)
        self.m_s = T_("m_s", [128, NPAIR * 128], F32)
        self.mt_s = T_("mt_s", [128, NPAIR * 128], F32)
        self.mn_s = T_("mn_s", [128, NPAIR * 128], F32)
        self.pwn = [T_("pwn0", [128, NPAIR * 128], F32),
                    T_("pwn1", [128, NPAIR * 128], F32)]
        self.pwt = [T_(f"pwt{i}", [128, NPAIR * 128], F32) for i in range(3)]
        self.pw = {-2: self.pwt[0], -4: self.pwt[1], -8: self.pwt[2],
                   -16: self.pwt[0], -32: self.pwt[1]}
        self.tc0 = T_("tc0", [128, NPAIR * 128], F32)
        self.tc1 = self.tc0
        self.tt_s = T_("tt_s", [128, NPAIR * 128], BF16)
        self.rq = T_("rq", [NH, PT], F32)
        self.rk = T_("rk", [NH, PT], F32)
        self.be = T_("be", [NH, PT], F32)
        self.ro = T_("ro", [NH, PT], F32)
        self.r1 = T_("r1", [NH, PT], F32)
        self.ar = T_("ar", [1, PT], F32)
        self.pp = T_("pp", [128, 2 * DC * NCHU], F32)
        self.sd = T_("sd", [128, DC * (NCHU + 1)], F32)
        self.sg_t = T_("sg_t", [128, I3], BF16)
        self.up_t = T_("up_t", [128, I3], BF16)
        self.h3_f = T_("h3_f", [128, KC3 * 128], BF16)
        self.hq_t = self.h_t
        self.hq_f = self.k_f
        self.ss_t = T_("ss_t", [128, NB3], F32)
        self.rs_t = T_("rs_t", [128, NB3], F32)
        self.qa_s = T_("qa_s", [128, I3], BF16)
        self.qc_s = self.kb_f[:, 0:1024]
        self.qh_s = self.kb_f[:, 1024:2048]
        P_ = lambda n, sh: es.enter_context(nc.psum_tensor(n, sh, F32))
        self.psA = P_("psA", [128, 512])
        self.psB = P_("psB", [128, 512])
        self.psC = P_("psC", [128, 512])
        self.psG = P_("psG", [128, 512])
        self.psT = P_("psT", [128, 512])
        self.psX = P_("psX", [128, 512])
        self.psR = P_("psR", [16, 512])
        self.psR2 = P_("psR2", [16, 512])
        for s in ("vv", "aa", "tt", *SDPOOL, *GDPOOL):
            self.hand[s] = es.enter_context(nc.semaphore(s))
            self.cnt[s] = 0
        self.block = es.enter_context(nc.Block())

    def ladder(self, v, X, out, ss, rs, FD):
        nb = FD // 16
        qa, qc, qh = self.qa_s[:, 0:FD], self.qc_s[:, 0:FD], self.qh_s[:, 0:FD]
        x3 = X.rearrange("p (n b) -> p n b", b=16)
        rs3 = rs[:, 0:nb, None].broadcast_to([128, nb, 16])
        ss3 = ss[:, 0:nb, None].broadcast_to([128, nb, 16])
        v.tensor_tensor(qa.rearrange("p (n b) -> p n b", b=16), x3, rs3, ALU.mult)
        v.scalar_tensor_tensor(qa, qa, -1.0, qa, ALU.mult, ALU.max)
        v.tensor_scalar(qc, qa, 0.25, None, ALU.is_gt)
        v.scalar_tensor_tensor(qc, qa, 0.75, qc, ALU.is_gt, ALU.add)
        v.scalar_tensor_tensor(qc, qa, 1.25, qc, ALU.is_gt, ALU.add)
        v.scalar_tensor_tensor(qc, qa, 1.75, qc, ALU.is_gt, ALU.add)
        v.tensor_scalar(qh, qa, 2.5, None, ALU.is_gt)
        v.scalar_tensor_tensor(qh, qa, 3.5, qh, ALU.is_gt, ALU.add)
        v.scalar_tensor_tensor(qh, qa, 5.0, qh, ALU.is_gt, ALU.add)
        v.scalar_tensor_tensor(qh, qa, 5.0, qh, ALU.is_gt, ALU.add)
        v.scalar_tensor_tensor(qh, qc, 0.5, qh, ALU.mult, ALU.add)
        v.tensor_scalar(qc, X, 0.0, None, ALU.is_lt)
        v.scalar_tensor_tensor(qc, X, 0.0, qc, ALU.is_gt, ALU.subtract)
        v.tensor_tensor(qh, qh, qc, ALU.mult)
        return v.tensor_tensor(out.rearrange("p (n b) -> p n b", b=16),
                               qh.rearrange("p (n b) -> p n b", b=16), ss3, ALU.mult)

    def scales_dve(self, v, X, ss, FD):
        nb = FD // 16
        v.tensor_reduce(ss[:, 0:nb], X.rearrange("p (n b) -> p n b", b=16),
                        AX.X, ALU.max, apply_absolute_value=True)
        return v.tensor_scalar(ss[:, 0:nb], ss[:, 0:nb], 1.0 / 6.0, 1e-12,
                               ALU.mult, ALU.max)

    def recip_act(self, sc, rs, ss, nb):
        sc.activation(rs[:, 0:nb], ss[:, 0:nb], AF.Ln)
        return sc.activation(rs[:, 0:nb], rs[:, 0:nb], AF.Exp, scale=np.float32(-1.0))

    def p0(self):
        S = self.sec

        def lds(sy):
            for dst, src in ((self.eye_f[:], self.deye), (self.eye_b[:], self.deyeb),
                             (self.mask3.rearrange("p (i q) -> p i q", q=128),
                              self.dmask3.rearrange("i p q -> p i q")),
                             (self.hsel[:], self.dhsel), (self.colsel[:], self.dcolsel),
                             (self.ones4[:], self.dones4), (self.amask[:], self.damask),
                             (self.gcol[:], self.dgcol)):
                self.dma(sy, None, dst, src)
            return sy.dma_start(out=self.stgA[0:BPC, 0:D], in_=self.dcc)
        S("sync", [], lds, "c0")
        S("scalar", ["c0"], lambda sc: sc.activation(
            self.stgA[0:BPC, 0:D], self.stgA[0:BPC, 0:D], AF.Silu), "c2")
        for dc in range(DC):
            w = ["c2"] if dc == 0 else [f"ctv{dc - 1}"]
            S("tensor", w, lambda t, dc=dc: t.transpose(
                self.psT[:, 0:BPC], self.stgA[0:BPC, dc * 128:(dc + 1) * 128],
                self.eye_f[0:BPC, 0:BPC]), f"ctr{dc}")
            S("vector", [f"ctr{dc}"], lambda v, dc=dc: v.tensor_copy(
                self.cT[:, dc * BPC:(dc + 1) * BPC], self.psT[:, 0:BPC]), f"ctv{dc}")

        for i in range(32):
            wi, kc = divmod(i, DC)
            w = [f"wcv{i - 2}"] if i >= 2 else [f"ctv{DC - 1}"]
            S("sync", w, lambda sy, wi=wi, i=i, kc=kc: sy.dma_start(
                out=(self.stgA if i % 2 == 0 else self.stgB)[:, 0:D],
                in_=self.dW[wi][kc * 128:(kc + 1) * 128, :]), f"wld{i}")

            def cast_w(v, wi=wi, kc=kc, i=i):
                if i >= 4:
                    self.wait(v, f"wst{i - 4}")
                src = (self.stgA if i % 2 == 0 else self.stgB)[:, 0:D]
                dst = self.wql[:, (i % 4) * 1024:(i % 4) * 1024 + D]
                if wi == 3:
                    return v.tensor_scalar(dst, src, self.gcol[:, kc % 2:kc % 2 + 1],
                                           None, ALU.mult)
                return v.tensor_copy(dst, src)
            S("vector", [f"wld{i}"], cast_w, f"wcv{i}")
            S("gpsimd", [f"wcv{i}"], lambda g, wi=wi, kc=kc, i=i: g.dma_start(
                out=self.swattn[wi, kc * 128:(kc + 1) * 128, :],
                in_=self.wql[:, (i % 4) * 1024:(i % 4) * 1024 + D]), f"wst{i}")

        for i in range(12):
            w = [f"dgc{i - 2}"] if i >= 2 else ["wcv31"]
            S("sync", w, lambda sy, i=i: sy.dma_start(
                out=(self.stgA if i % 2 == 0 else self.stgB)[:, 0:DC * 128]
                .rearrange("p (c q) -> p c q", q=128),
                in_=self.dcdiag[i].rearrange("c p q -> p c q")), f"dgl{i}")
            S("vector", [f"dgl{i}"], lambda v, i=i: v.tensor_copy(
                self.diag[:, i * DC * 128:(i + 1) * DC * 128],
                (self.stgA if i % 2 == 0 else self.stgB)[:, 0:DC * 128]), f"dgc{i}")

        S("sync", ["dgc11"], lambda sy: sy.dma_start(
            out=self.stgA[:, 0:DC * NH].rearrange("p (c h) -> p c h", c=DC),
            in_=self.dWb.rearrange("(c p) h -> p c h", p=128)), "wbl")
        S("vector", ["wbl"], lambda v: v.tensor_copy(self.wb_sb[:], self.stgA[:, 0:DC * NH]),
          "wbc")

        for idx in range(32):
            kc, sl = divmod(idx, 4)
            w = [f"mmt{idx - 2}"] if idx >= 2 else ["wbc"]
            S("sync", w, lambda sy, kc=kc, sl=sl, idx=idx: sy.dma_start(
                out=(self.stgA if idx % 2 == 0 else self.stgB)[:, 0:1536],
                in_=self.dwmod[kc * 128:(kc + 1) * 128, sl * 1536:(sl + 1) * 1536]),
              f"mml{idx}")
            w2 = [f"mml{idx}"]
            if idx < 4:
                w2 += [f"wst{j}" for j in range(28, 32)]
            S("vector", w2, lambda v, idx=idx: v.tensor_copy(
                self.wql[:, (idx % 2) * 1536:(idx % 2) * 1536 + 1536],
                (self.stgA if idx % 2 == 0 else self.stgB)[:, 0:1536]), f"mmc{idx}")

            def modmm(t, kc=kc, sl=sl, idx=idx):
                if idx >= 2:
                    self.wait(t, f"mme{idx - 2}")
                for ob in range(12):
                    oc = sl * 12 + ob
                    inst = t.matmul(self.psC[:, oc * NH:(oc + 1) * NH],
                                    self.wql[:, (idx % 2) * 1536 + ob * 128:
                                             (idx % 2) * 1536 + (ob + 1) * 128],
                                    self.cT[:, kc * BPC:(kc + 1) * BPC],
                                    start=True, stop=True)
                return inst
            S("tensor", [f"mmc{idx}"], modmm, f"mmt{idx}")

            def modacc(v, kc=kc, sl=sl, idx=idx):
                dst = self.modc[:, sl * 48:(sl + 1) * 48]
                srcp = self.psC[:, sl * 48:(sl + 1) * 48]
                if kc == 0:
                    return v.tensor_copy(dst, srcp)
                return v.tensor_tensor(dst, dst, srcp, ALU.add)
            S("vector", [f"mmt{idx}"], modacc, f"mme{idx}")

        def modev(v):
            v.tensor_copy(self.stgB[:, 0:192], self.modc[:, 0:192])
            v.tensor_copy(
                self.modc.rearrange("p (b m o) -> p b m o", b=BPC, m=6, o=DC),
                self.stgB[:, 0:192].rearrange("p (m o b) -> p b m o", m=6, o=DC, b=BPC))
            mc = self.modc.rearrange("p (b m o) -> p b m o", b=BPC, m=6, o=DC)
            for b in range(BPC):
                v.tensor_scalar(mc[:, b, 1, :], mc[:, b, 1, :], 1.0, None, ALU.add)
                v.tensor_scalar(mc[:, b, 4, :], mc[:, b, 4, :], 1.0, None, ALU.add)
                for pa in (2, 5):
                    gsl = mc[:, b, pa, :]
                    s1, s2 = self.st[:, 0:DC], self.st[:, DC:2 * DC]
                    v.tensor_scalar(s1, gsl, 0.0, 2.0, ALU.is_ge, ALU.mult)
                    v.tensor_scalar(s1, s1, -1.0, None, ALU.add)
                    v.scalar_tensor_tensor(s2, gsl, -1.0, gsl, ALU.mult, ALU.max)
                    v.tensor_scalar(s2, s2, 0.25, None, ALU.max)
                    inst = v.tensor_tensor(gsl, s1, s2, ALU.mult)
            return inst
        S("vector", ["mme31"], modev, "modr")
        if self.dbg:
            S("gpsimd", ["modr"], lambda g: g.dma_start(
                out=self.dbg_bufs["d_mod"], in_=self.modc[:]), "dmod")
        for b in range(BPC):
            w = ["modr"] if b == 0 else [f"modev{b - 1}"]
            S("tensor", w, lambda t, b=b: t.transpose(
                self.psT[0:48, 0:128], self.modc[:, b * 48:(b + 1) * 48], self.eye_f[:]),
              f"modtr{b}")

            def mev(v, b=b):
                if b > 0:
                    self.wait(v, f"modst{b - 1}")
                return v.tensor_copy(self.stgA[0:48, 0:128], self.psT[0:48, 0:128])
            S("vector", [f"modtr{b}"], mev, f"modev{b}")
            S("gpsimd", [f"modev{b}"], lambda g, b=b: g.dma_start(
                out=self.smod[b].rearrange("m (o q) -> (m o) q", q=128),
                in_=self.stgA[0:48, 0:128]), f"modst{b}")

        specs = [(self.dwg, self.swg, I3, D, [(0, 1024)]),
                 (self.dwu, self.swu, I3, D, [(0, 1024)]),
                 (self.dwd, self.swd, D, I3, [(0, 1024), (1024, 1024), (2048, 768)])]
        n = 0
        for dsrc, dscr, R, FT, chunks in specs:
            for rc in range(R // 128):
                for f0, F in chunks:
                    nb = F // 16
                    w = [f"fqv{n - 2}"] if n >= 2 else [f"modst{BPC - 1}"]
                    S("sync", w, lambda sy, dsrc=dsrc, rc=rc, n=n, f0=f0, F=F: sy.dma_start(
                        out=(self.stgA if n % 2 == 0 else self.stgB)[:, 0:F],
                        in_=dsrc[rc * 128:(rc + 1) * 128, f0:f0 + F]), f"fql{n}")

                    def fq_pre(v, n=n, F=F):
                        if n >= 1:
                            self.wait(v, f"fqw{n - 1}")
                        v.tensor_copy(self.sg_t[:, 0:F],
                                      (self.stgA if n % 2 == 0 else self.stgB)[:, 0:F])
                        return self.scales_dve(v, self.sg_t[:, 0:F], self.ss_t, F)
                    S("vector", [f"fql{n}"], fq_pre, f"fqs{n}")
                    S("scalar", [f"fqs{n}"], lambda sc, nb=nb: self.recip_act(
                        sc, self.rs_t, self.ss_t, nb), f"fqr{n}")
                    S("vector", [f"fqr{n}"], lambda v, F=F: self.ladder(
                        v, self.sg_t[:, 0:F], self.up_t[:, 0:F], self.ss_t, self.rs_t, F),
                      f"fqv{n}")
                    S("gpsimd", [f"fqv{n}"], lambda g, dscr=dscr, rc=rc, f0=f0, F=F:
                      g.dma_start(out=dscr[rc * 128:(rc + 1) * 128, f0:f0 + F],
                                  in_=self.up_t[:, 0:F]), f"fqw{n}")
                    n += 1

    def seq_setup(self, s):
        S = self.sec
        bar = self.barrier_evs()

        def mb(gq):
            inst = None
            for slot, pa in enumerate((1, 0, 4, 3, 2, 5)):
                if inst is not None:
                    self.mark(inst, self.cursem, None, 16)
                inst = gq.dma_start(out=self.modt[:, slot * D:(slot + 1) * D],
                                    in_=self.smod[s, pa][None, :].broadcast_to([128, D]))
            return inst
        S("gpsimd", bar, mb, f"q{s}mb")

        def z(v):
            v.memset(self.sd[:], 0.0)
            return v.memset(self.ptail[:], 0.0)
        S("vector", [f"q{s}mb"], z, f"q{s}z")

    def piece(self, s, pi):
        S = self.sec
        t0 = pi * PT
        g_ = f"s{s}p{pi}"
        dbg_on = self.dbg and s == 0 and pi == 0
        bar = self.barrier_evs()

        x3 = self.x_t.rearrange("p (g d) -> p g d", g=NG)

        h3 = self.h_t.rearrange("p (g d) -> p g d", g=NG)
        hf = self.h_f.rearrange("p (c t) -> p c t", c=DC)
        pf = self.p_f.rearrange("p (c t) -> p c t", c=DC)
        qf = self.q_f.rearrange("p (c t) -> p c t", c=DC)
        kf = self.k_f.rearrange("p (c t) -> p c t", c=DC)
        kbf = self.kb_f.rearrange("p (c t) -> p c t", c=DC)
        vf = self.v_f.rearrange("p (c t) -> p c t", c=DC)
        wf = self.w_f.rearrange("p (c t) -> p c t", c=DC)
        uf = self.u_f.rearrange("p (c t) -> p c t", c=DC)
        scf = self.sc_f.rearrange("p (c t) -> p c t", c=DC)
        prb = self.qa_s[:, 0:DC * PT].rearrange("p (c t) -> p c t", c=DC)
        pp4 = self.pp.rearrange("p (a c n) -> p a c n", a=2, c=DC)
        sd3 = self.sd.rearrange("p (c n) -> p c n", c=DC)
        pt3 = self.ptail.rearrange("p (w c e) -> p w c e", w=3, c=DC)

        def ldx(sy):
            self.dma(sy, None, x3,
                     self.dx[s, t0:t0 + PT, :].rearrange("(g p) d -> p g d", p=128),
                     label=f"{g_}_x")
            return sy.dma_start(out=self.wql.rearrange("p (c d) -> p c d", c=DC),
                                in_=self.swattn[0].rearrange("(c p) d -> p c d", p=128))
        S("sync", bar + [f"q{s}z"], ldx, f"{g_}_w0")

        def ln(pref, mslot, asl):
            for g in range(NG):
                S("scalar", [f"{pref}_lnin" if g == 0 else f"{pref}_vr{g - 1}"],
                  lambda sc, g=g: sc.activation(self.sq_t[:], x3[:, g, :], AF.Square),
                  f"{pref}_sq{g}")
                S("vector", [f"{pref}_sq{g}"], lambda v, g=g: v.tensor_reduce(
                    self.st[:, 2 + g:3 + g], self.sq_t[:, None, :], AX.X, ALU.add),
                  f"{pref}_vr{g}")

            def ln_var(v):
                v.tensor_reduce(self.st[:, 0:NG], x3, AX.X, ALU.add)
                v.tensor_scalar(self.st[:, 0:NG], self.st[:, 0:NG], 1.0 / D, None, ALU.mult)
                v.tensor_scalar(self.st[:, 2:2 + NG], self.st[:, 2:2 + NG], 1.0 / D,
                                None, ALU.mult)
                v.tensor_tensor(self.st[:, 4:4 + NG], self.st[:, 0:NG], self.st[:, 0:NG],
                                ALU.mult)
                return v.tensor_tensor(self.st[:, 4:4 + NG], self.st[:, 2:2 + NG],
                                       self.st[:, 4:4 + NG], ALU.subtract)
            S("vector", [f"{pref}_vr{NG - 1}"], ln_var, f"{pref}_var")

            def ln_rstd(sc):
                sc.activation(self.st[:, 6:6 + NG], self.st[:, 4:4 + NG], AF.Ln, bias=EPS)
                return sc.activation(self.st[:, 6:6 + NG], self.st[:, 6:6 + NG],
                                     AF.Exp, scale=np.float32(-0.5))
            S("scalar", [f"{pref}_var"], ln_rstd, f"{pref}_rstd")

            def ln_apply(v):
                for g in range(NG):
                    v.tensor_scalar(h3[:, g, :], x3[:, g, :], self.st[:, g:g + 1],
                                    self.st[:, 6 + g:7 + g], ALU.subtract, ALU.mult)
                v.tensor_tensor(h3, h3, self.modt[:, mslot * D:(mslot + 1) * D]
                                [:, None, :].broadcast_to([128, NG, D]), ALU.mult)
                return v.tensor_tensor(h3, h3, self.modt[:, asl * D:(asl + 1) * D]
                                       [:, None, :].broadcast_to([128, NG, D]), ALU.add)
            S("vector", [f"{pref}_rstd"], ln_apply, f"{pref}_h")

        self.ev[f"{g_}A_lnin"] = self.ev[f"{g_}_x"]
        ln(f"{g_}A", 0, 1)

        def hf_tr(g):
            if dbg_on:
                self.dma(g, None, self.dbg_bufs["d_h"], self.h_t[:])
            for gg in range(NG - 1):
                self.dma(g, None, hf[:, :, gg * 128:(gg + 1) * 128],
                         h3[:, gg, :].rearrange("p (c q) -> p c q", q=128),
                         transpose=True)
            return g.dma_start_transpose(
                hf[:, :, (NG - 1) * 128:NG * 128],
                h3[:, NG - 1, :].rearrange("p (c q) -> p c q", q=128))
        S("sync", [f"{g_}A_h"], hf_tr, f"{g_}_hf")

        for tgt in range(3):
            dstf = (qf, kf, vf)[tgt]
            for ocg in range(0, DC, 2):
                def pj(t, tgt=tgt, ocg=ocg):
                    for oc in (ocg, ocg + 1):
                        ps = (self.psA, self.psB)[oc % 2]
                        for kc in range(DC):
                            inst = t.matmul(
                                ps[:, 0:PT],
                                self.wql[:, kc * D + oc * 128:kc * D + (oc + 1) * 128],
                                hf[:, kc, :], start=(kc == 0), stop=(kc == DC - 1))
                    return inst
                w = [f"{g_}_w{tgt}", f"{g_}_hf"]
                if ocg >= 2:
                    w.append(f"{g_}_pe{tgt}_{ocg - 2}")
                elif tgt > 0:
                    w.append(f"{g_}_pe{tgt - 1}_{DC - 2}")
                S("tensor", w, pj, f"{g_}_pj{tgt}_{ocg}")

                def pev(v, tgt=tgt, ocg=ocg):
                    for oc in (ocg, ocg + 1):
                        ps = (self.psA, self.psB)[oc % 2]
                        inst = v.tensor_copy(pf[:, oc, 3:3 + PT], ps[:, 0:PT])
                    return inst
                S("vector", [f"{g_}_pj{tgt}_{ocg}"], pev, f"{g_}_pe{tgt}_{ocg}")

            S("vector", [f"{g_}_pe{tgt}_{DC - 2}"], lambda v, tgt=tgt: v.tensor_copy(
                pf[:, :, 0:3], pt3[:, tgt, :, :]), f"{g_}_ph{tgt}")
            if tgt < 3:
                S("sync", [f"{g_}_pj{tgt}_{DC - 2}"], lambda sy, tgt=tgt: sy.dma_start(
                    out=self.wql.rearrange("p (c d) -> p c d", c=DC),
                    in_=self.swattn[tgt + 1].rearrange("(c p) d -> p c d", p=128)),
                  f"{g_}_w{tgt + 1}")

            for ocg in range(0, DC, 2):
                def cv(t, tgt=tgt, ocg=ocg):
                    for oc in (ocg, ocg + 1):
                        ps = (self.psC, self.psG)[oc % 2]
                        for tau in range(4):
                            dg = self.diag[:, (tgt * 4 + tau) * DC * 128 + oc * 128:
                                           (tgt * 4 + tau) * DC * 128 + (oc + 1) * 128]
                            inst = t.matmul(ps[:, 0:PT], dg, pf[:, oc, tau:tau + PT],
                                            start=(tau == 0), stop=(tau == 3))
                    return inst
                w = [f"{g_}_pe{tgt}_{ocg}", f"{g_}_ph{tgt}"]
                if ocg >= 2:
                    w.append(f"{g_}_cs{tgt}_{ocg - 2}")
                elif tgt > 0:
                    w.append(f"{g_}_cs{tgt - 1}_{DC - 2}")
                S("tensor", w, cv, f"{g_}_cm{tgt}_{ocg}")

                def cse(sc, tgt=tgt, ocg=ocg, dstf=dstf):
                    for oc in (ocg, ocg + 1):
                        ps = (self.psC, self.psG)[oc % 2]
                        inst = sc.activation(dstf[:, oc, :], ps[:, 0:PT], AF.Silu)
                    return inst
                S("scalar", [f"{g_}_cm{tgt}_{ocg}"], cse, f"{g_}_cs{tgt}_{ocg}")

            S("vector", [f"{g_}_pe{tgt}_{DC - 2}"], lambda v, tgt=tgt: v.tensor_copy(
                pt3[:, tgt, :, :], pf[:, :, PT:PT + 3]), f"{g_}_pt{tgt}")

        def btm(t):
            for kc in range(DC):
                inst = t.matmul(self.psR[0:NH, 0:PT], self.wb_sb[:, kc * NH:(kc + 1) * NH],
                                hf[:, kc, :], start=(kc == 0), stop=(kc == DC - 1))
            return inst
        S("tensor", [f"{g_}_hf"], btm, f"{g_}_bt")
        S("scalar", [f"{g_}_bt"], lambda sc: sc.activation(
            self.be[:], self.psR[0:NH, 0:PT], AF.Tanh, scale=np.float32(0.5)), f"{g_}_be1")
        S("vector", [f"{g_}_be1"], lambda v: v.tensor_scalar(
            self.be[:], self.be[:], 0.5, 0.5, ALU.mult, ALU.add), f"{g_}_be")

        S("vector", [f"{g_}_cs0_{DC - 2}", f"{g_}_cs1_{DC - 2}"],
          lambda v: v.tensor_tensor(prb, qf, qf, ALU.mult), f"{g_}_pq")

        def rowmm(dst, deps, lab):
            def fn(t):
                for dc in range(DC):
                    inst = t.matmul(dst, self.colsel[:, (dc // 2) * NH:(dc // 2 + 1) * NH],
                                    prb[:, dc, :], start=(dc == 0), stop=(dc == DC - 1))
                return inst
            S("tensor", deps, fn, lab)
        rowmm(self.psR[0:NH, PT:2 * PT], [f"{g_}_pq"], f"{g_}_qq")
        S("vector", [f"{g_}_qq"], lambda v: v.tensor_tensor(prb, kf, kf, ALU.mult),
          f"{g_}_pk")
        rowmm(self.psR2[0:NH, 0:PT], [f"{g_}_pk"], f"{g_}_kk")
        S("vector", [f"{g_}_kk"], lambda v: v.tensor_tensor(prb, qf, kf, ALU.mult),
          f"{g_}_pqk")
        rowmm(self.psR2[0:NH, PT:2 * PT],
              [f"{g_}_pqk", f"{g_}_cs2_{DC - 2}"], f"{g_}_qk")

        def rqk(sc):
            sc.activation(self.rq[:], self.psR[0:NH, PT:2 * PT], AF.Ln)
            sc.activation(self.rq[:], self.rq[:], AF.Exp, scale=np.float32(-0.5))
            sc.activation(self.rk[:], self.psR2[0:NH, 0:PT], AF.Ln)
            return sc.activation(self.rk[:], self.rk[:], AF.Exp, scale=np.float32(-0.5))
        S("scalar", [f"{g_}_kk"], rqk, f"{g_}_rqk")

        def arp(v):
            v.tensor_tensor(self.r1[:], self.psR2[0:NH, PT:2 * PT], self.rq[:], ALU.mult)
            return v.tensor_tensor(self.r1[:], self.r1[:], self.rk[:], ALU.mult)
        S("vector", [f"{g_}_qk", f"{g_}_rqk"], arp, f"{g_}_arp")
        S("tensor", [f"{g_}_arp"], lambda t: t.matmul(
            self.psT[0:1, 0:PT], self.ones4[:], self.r1[:],
            start=True, stop=True), f"{g_}_arm")

        def armask(v):
            am = self.amask[0:1, s * CH:(s + 1) * CH][:, None, :].broadcast_to([1, NCHU, CH])
            return v.tensor_tensor(self.ar.rearrange("o (n c) -> o n c", c=CH),
                                   self.psT[0:1, 0:PT]
                                   .rearrange("o (n c) -> o n c", c=CH), am, ALU.mult)
        S("vector", [f"{g_}_arm"], armask, f"{g_}_ar")


        def mk_bcap(name, rows_ap, deps, fn):
            for hp in range(2):
                def bc(t, hp=hp, rows_ap=rows_ap):
                    for hh in range(2):
                        inst = t.matmul(self.psX[:, hh * PT:(hh + 1) * PT],
                                        self.hsel[0:NH, (hp * 2 + hh) * 128:
                                                  (hp * 2 + hh + 1) * 128],
                                        rows_ap, start=True, stop=True)
                    return inst
                w = list(deps) if hp == 0 else [f"{g_}_{name}ap0"]
                S("tensor", w, bc, f"{g_}_{name}bc{hp}")
                S("vector", [f"{g_}_{name}bc{hp}"], lambda v, hp=hp: fn(v, hp),
                  f"{g_}_{name}ap{hp}")

        def ap_gen(bufs):
            def fn(v, hp):
                for dc in range(4 * hp, 4 * hp + 4):
                    px = self.psX[:, (dc // 2 - 2 * hp) * PT:(dc // 2 - 2 * hp + 1) * PT]
                    for dst, src in bufs:
                        inst = v.tensor_tensor(dst[:, dc, :], src[:, dc, :], px, ALU.mult)
                return inst
            return fn

        mk_bcap("rk", self.rk[:], [f"{g_}_rqk", f"{g_}_pqk"], ap_gen([(kf, kf)]))
        mk_bcap("rq", self.rq[:], [f"{g_}_rkap1"], ap_gen([(qf, qf)]))
        mk_bcap("be", self.be[:], [f"{g_}_rqap1", f"{g_}_be", f"{g_}_cs2_{DC - 2}"],
                ap_gen([(kbf, kf), (vf, vf)]))

        def abm(t):
            ones_row = self.ones4[0:1, 0:1].broadcast_to([1, 128])
            return t.matmul(self.psX[:, 0:PT], ones_row, self.ar[:], start=True, stop=True)
        S("tensor", [f"{g_}_ar", f"{g_}_beap1"], abm, f"{g_}_abm")
        S("vector", [f"{g_}_abm"], lambda v: v.tensor_copy(self.a_b[:], self.psX[:, 0:PT]),
          f"{g_}_ab")

        if dbg_on:
            def dq(g):
                for nm, buf in (("d_q", self.q_f), ("d_k", self.k_f), ("d_v", self.v_f),
                                ("d_kb", self.kb_f)):
                    self.dma(g, None, self.dbg_bufs[nm], buf[:])
                return g.dma_start(out=self.dbg_bufs["d_ab"], in_=self.a_b[:])
            S("gpsimd", [f"{g_}_beap1", f"{g_}_ab"], dq, f"{g_}_dq")

        def tok_tr(g):
            kbt4 = self.kb_t.rearrange("p (g c q) -> p g c q", g=NG, q=128)
            vbt4 = self.h_f.rearrange("p (g c q) -> p g c q", g=NG, q=128)
            for dc in range(DC):
                self.dma(g, None, kbt4[:, :, dc, :], kbf[:, dc, :], transpose=True)
            for dc in range(DC - 1):
                self.dma(g, None, vbt4[:, :, dc, :], vf[:, dc, :], transpose=True)
            return g.dma_start_transpose(vbt4[:, :, DC - 1, :], vf[:, DC - 1, :])
        S("sync", [f"{g_}_beap1"], tok_tr, f"{g_}_tok")

        def grams(t):
            for pr_i in range(NPAIR):
                for dc in range(DC):
                    t.matmul(self.psG[:, pr_i * 128:(pr_i + 1) * 128],
                             kbf[:, dc, pr_i * 128:(pr_i + 1) * 128],
                             kf[:, dc, pr_i * 128:(pr_i + 1) * 128],
                             start=(dc == 0), stop=(dc == DC - 1))
                for dc in range(DC):
                    inst = t.matmul(self.psG[:, 256 + pr_i * 128:256 + (pr_i + 1) * 128],
                                    kf[:, dc, pr_i * 128:(pr_i + 1) * 128],
                                    kbf[:, dc, pr_i * 128:(pr_i + 1) * 128],
                                    start=(dc == 0), stop=(dc == DC - 1))
            return inst
        S("tensor", [f"{g_}_beap1", f"{g_}_ar"], grams, f"{g_}_gr")

        def masks(v):
            mk = lambda i: self.mask3[:, i * 128:(i + 1) * 128][:, None, :].broadcast_to([128, NPAIR, 128])
            g2 = self.psG[:, 0:256].rearrange("p (r q) -> p r q", q=128)
            gt2 = self.psG[:, 256:512].rearrange("p (r q) -> p r q", q=128)
            v.tensor_tensor(self.m_s.rearrange("p (r q) -> p r q", q=128), g2, mk(0),
                            ALU.mult)
            v.tensor_tensor(self.mn_s.rearrange("p (r q) -> p r q", q=128), gt2, mk(2),
                            ALU.mult)
            return v.tensor_tensor(self.mt_s.rearrange("p (r q) -> p r q", q=128), gt2,
                                   mk(1), ALU.mult)
        S("vector", [f"{g_}_gr"], masks, f"{g_}_msk")
        if dbg_on:
            S("gpsimd", [f"{g_}_msk"], lambda g: g.dma_start(
                out=self.dbg_bufs["d_m"], in_=self.m_s[:]), f"{g_}_dm")

        prev_lab = f"{g_}_msk"
        natof = {1: self.m_s, 2: self.pwn[0], 4: self.pwn[1], 8: self.pwn[0],
                 16: self.pwn[1]}
        mtof = {1: self.mt_s}
        for lv in (2, 4, 8, 16, 32):
            mtof[lv] = self.pw[-lv]
        cur = self.m_s
        nap = 0
        for step, lv in (("sq", 2), ("sq", 4), ("ap", 2), ("sq", 8), ("ap", 4),
                         ("sq", 16), ("ap", 8), ("sq", 32), ("ap", 16), ("ap", 32),
                         ("ap", 0)):
            if step == "sq":
                prev, prevt = natof[lv // 2], mtof[lv // 2]
                last = lv == 32

                def pwmm(t, prev=prev, prevt=prevt, last=last):
                    for pr_i in range(NPAIR):
                        sl = slice(pr_i * 128, (pr_i + 1) * 128)
                        if not last:
                            t.matmul(self.psT[:, sl], prevt[:, sl], prev[:, sl],
                                     start=True, stop=True)
                        inst = t.matmul(
                            self.psT[:, 256 + pr_i * 128:256 + (pr_i + 1) * 128],
                            prev[:, sl], prevt[:, sl], start=True, stop=True)
                    return inst
                S("tensor", [prev_lab], pwmm, f"{g_}_pw{lv}")

                def pwe(sc, lv=lv, last=last):
                    if not last:
                        sc.copy(natof[lv][:], self.psT[:, 0:256])
                    return sc.copy(mtof[lv][:], self.psT[:, 256:512])
                S("scalar", [f"{g_}_pw{lv}"], pwe, f"{g_}_pwe{lv}")
                prev_lab = f"{g_}_pwe{lv}"
            else:
                lhs = mtof[lv] if lv else self.mn_s

                def tcm(t, lhs=lhs, cur=cur):
                    for pr_i in range(NPAIR):
                        sl = slice(pr_i * 128, (pr_i + 1) * 128)
                        t.matmul(self.psT[:, sl], lhs[:, sl], cur[:, sl],
                                 start=True, stop=False)
                        inst = t.matmul(self.psT[:, sl], self.eye_f[:], cur[:, sl],
                                        start=False, stop=True)
                    return inst
                S("tensor", [prev_lab], tcm, f"{g_}_tc{nap}")
                S("scalar", [f"{g_}_tc{nap}"],
                  lambda sc: sc.copy(self.tc0[:], self.psT[:, 0:256]),
                  f"{g_}_tce{nap}")
                prev_lab = f"{g_}_tce{nap}"
                cur = self.tc0
                nap += 1

        def ttr(t):
            for pr_i in range(NPAIR):
                inst = t.transpose(self.psT[:, 256 + pr_i * 128:256 + (pr_i + 1) * 128],
                                   self.tc1[:, pr_i * 128:(pr_i + 1) * 128], self.eye_f[:])
            return inst
        S("tensor", [prev_lab], ttr, f"{g_}_ttr")

        def ttev(v):
            return v.scalar_tensor_tensor(
                self.tt_s.rearrange("p (r q) -> p r q", q=128),
                self.psT[:, 256:512].rearrange("p (r q) -> p r q", q=128), -3.0,
                self.eye_b[:, None, :].broadcast_to([128, NPAIR, 128]),
                ALU.mult, ALU.add)
        S("vector", [f"{g_}_ttr"], ttev, f"{g_}_tt")
        if dbg_on:
            S("gpsimd", [f"{g_}_tt"], lambda g: g.dma_start(
                out=self.dbg_bufs["d_tt"], in_=self.tt_s[:]), f"{g_}_dtt")

        kbt3 = self.kb_t.rearrange("p (g d) -> p g d", g=NG)
        vbt3 = self.h_f.rearrange("p (g d) -> p g d", g=NG)
        for ri in range(8):
            ten, dc0 = ri // 4, (ri % 4) * 2
            src3 = kbt3 if ten == 0 else vbt3
            ps = (self.psG, self.psC)[ri % 2]

            def wumm(t, src3=src3, dc0=dc0, ps=ps):
                for dq in range(2):
                    for pr_i in range(NPAIR):
                        inst = t.matmul(
                            ps[:, dq * 256 + pr_i * 128:dq * 256 + (pr_i + 1) * 128],
                            src3[:, pr_i, (dc0 + dq) * 128:(dc0 + dq + 1) * 128],
                            self.tt_s[:, pr_i * 128:(pr_i + 1) * 128],
                            start=True, stop=True)
                return inst
            w = [f"{g_}_tt", f"{g_}_tok"]
            if ri >= 2:
                w.append(f"{g_}_wue{ri - 2}")
            S("tensor", w, wumm, f"{g_}_wu{ri}")
            dstf = wf if ten == 0 else uf
            S("scalar", [f"{g_}_wu{ri}"], lambda sc, dstf=dstf, dc0=dc0, ps=ps: sc.copy(
                dstf[:, dc0:dc0 + 2, :].rearrange("p c t -> p (c t)"), ps[:, 0:512]),
              f"{g_}_wue{ri}")

        if dbg_on:
            def dwu(g):
                self.dma(g, None, self.dbg_bufs["d_w"], self.w_f[:])
                return g.dma_start(out=self.dbg_bufs["d_u"], in_=self.u_f[:])
            S("gpsimd", [f"{g_}_wue7"], dwu, f"{g_}_dwu")

        def p1p2(v):
            v.tensor_tensor(scf, kf, uf, ALU.mult)
            v.tensor_reduce(pp4[:, 0, :, :],
                            self.sc_f.rearrange("p (c n t) -> p c n t", c=DC, n=NCHU),
                            AX.X, ALU.add)
            v.tensor_tensor(scf, kf, wf, ALU.mult)
            v.tensor_reduce(pp4[:, 1, :, :],
                            self.sc_f.rearrange("p (c n t) -> p c n t", c=DC, n=NCHU),
                            AX.X, ALU.add)
            return v.tensor_scalar(pp4[:, 1, :, :], pp4[:, 1, :, :], -1.0, 1.0,
                                   ALU.mult, ALU.add)
        S("vector", [f"{g_}_wue7", f"{g_}_rkap1"], p1p2, f"{g_}_pp")
        if dbg_on:
            S("gpsimd", [f"{g_}_pp"], lambda g: g.dma_start(
                out=self.dbg_bufs["d_pp"], in_=self.pp[:]), f"{g_}_dpp")

        def scan(v):
            for dc in range(DC):
                inst = v.tensor_tensor_scan(sd3[:, dc, 1:NCHU + 1],
                                            pp4[:, 1, dc, :], pp4[:, 0, dc, :],
                                            sd3[:, dc, 0:1], ALU.mult, ALU.add)
            return inst
        S("vector", [f"{g_}_pp"], scan, f"{g_}_scan")

        def opass(v):
            ab3 = self.a_b[:, None, :].broadcast_to([128, DC, PT])
            v.tensor_tensor(scf, ab3, wf, ALU.mult)
            v.tensor_tensor(qf, qf, scf, ALU.subtract)
            v.tensor_tensor(uf, ab3, uf, ALU.mult)
            sdb = sd3[:, :, 0:NCHU][:, :, :, None].broadcast_to([128, DC, NCHU, CH])
            v.tensor_tensor(self.sc_f.rearrange("p (c n t) -> p c n t", c=DC, n=NCHU),
                            qf.rearrange("p c (n t) -> p c n t", n=NCHU), sdb, ALU.mult)
            v.tensor_tensor(uf, scf, uf, ALU.add)
            return v.tensor_copy(sd3[:, :, 0], sd3[:, :, NCHU])
        S("vector", [f"{g_}_scan", f"{g_}_ab", f"{g_}_rqap1"], opass, f"{g_}_o")
        if dbg_on:
            S("gpsimd", [f"{g_}_o"], lambda g: g.dma_start(
                out=self.dbg_bufs["d_o"], in_=self.u_f[:]), f"{g_}_do")

        S("vector", [f"{g_}_o"], lambda v: v.tensor_tensor(prb, uf, uf, ALU.mult),
          f"{g_}_po")
        rowmm(self.psR[0:NH, 0:PT], [f"{g_}_po"], f"{g_}_osq")

        def orstd(sc):
            sc.activation(self.ro[:], self.psR[0:NH, 0:PT], AF.Ln,
                          bias=EPS, scale=np.float32(1.0 / HD))
            return sc.activation(self.ro[:], self.ro[:], AF.Exp, scale=np.float32(-0.5))
        S("scalar", [f"{g_}_osq"], orstd, f"{g_}_ro")

        mk_bcap("ro", self.ro[:], [f"{g_}_ro", f"{g_}_abm"], ap_gen([(uf, uf)]))

        for gi in range(NG):
            for os_i in range(2):
                def wom(t, gi=gi, os_i=os_i):
                    for kc in range(DC):
                        inst = t.matmul(
                            (self.psA, self.psB)[os_i][:, 0:512],
                            uf[:, kc, gi * 128:(gi + 1) * 128],
                            self.wql[:, kc * D + os_i * 512:kc * D + (os_i + 1) * 512],
                            start=(kc == 0), stop=(kc == DC - 1))
                    return inst
                w = [f"{g_}_roap1", f"{g_}_w3"]
                if gi > 0:
                    w.append(f"{g_}_x2_{gi - 1}_{os_i}")
                S("tensor", w, wom, f"{g_}_wo{gi}_{os_i}")

                def x2ev(v, gi=gi, os_i=os_i):
                    ps = (self.psA, self.psB)[os_i]
                    gab = self.modt[:, 4 * D + os_i * 512:4 * D + (os_i + 1) * 512]
                    v.tensor_tensor(self.sc_f[:, 0:512], ps[:, 0:512], gab, ALU.mult)
                    return v.tensor_tensor(x3[:, gi, os_i * 512:(os_i + 1) * 512],
                                           self.sc_f[:, 0:512],
                                           x3[:, gi, os_i * 512:(os_i + 1) * 512],
                                           ALU.add)
                S("vector", [f"{g_}_wo{gi}_{os_i}"], x2ev, f"{g_}_x2_{gi}_{os_i}")

        if dbg_on:
            S("gpsimd", [f"{g_}_x2_{NG - 1}_1"], lambda g: g.dma_start(
                out=self.dbg_bufs["d_x2"], in_=self.x_t[:]), f"{g_}_dx2")

        self.ev[f"{g_}B_lnin"] = self.ev[f"{g_}_x2_{NG - 1}_1"]
        ln(f"{g_}B", 2, 3)

        S("vector", [f"{g_}B_h"], lambda v: self.scales_dve(
            v, self.h_t[:], self.ss_t, NG * D), f"{g_}_h2s")
        S("scalar", [f"{g_}_h2s"], lambda sc: self.recip_act(
            sc, self.rs_t, self.ss_t, NG * D // 16), f"{g_}_h2r")
        def h2lad(v):
            for ch in range(2):
                inst = self.ladder(v, self.h_t[:, ch * 1024:(ch + 1) * 1024],
                                   self.hq_t[:, ch * 1024:(ch + 1) * 1024],
                                   self.ss_t[:, ch * 64:], self.rs_t[:, ch * 64:], 1024)
            return inst
        S("vector", [f"{g_}_h2r"], h2lad, f"{g_}_h2q")
        if dbg_on:
            S("gpsimd", [f"{g_}_h2q"], lambda g: g.dma_start(
                out=self.dbg_bufs["d_h2q"], in_=self.hq_t[:]), f"{g_}_dh2q")

        hq3 = self.hq_t.rearrange("p (g d) -> p g d", g=NG)
        hqf = self.hq_f.rearrange("p (c t) -> p c t", c=DC)
        def hqf_tr(g):
            for gg in range(NG - 1):
                self.dma(g, None, hqf[:, :, gg * 128:(gg + 1) * 128],
                         hq3[:, gg, :].rearrange("p (c q) -> p c q", q=128),
                         transpose=True)
            return g.dma_start_transpose(
                hqf[:, :, (NG - 1) * 128:NG * 128],
                hq3[:, NG - 1, :].rearrange("p (c q) -> p c q", q=128))
        S("sync", [f"{g_}_h2q"], hqf_tr, f"{g_}_hqf")

        h3f = self.h3_f.rearrange("p (c t) -> p c t", c=KC3)
        for gi in range(NG):
            gt = f"{g_}_f{gi}"
            for osi, (o0, ow) in enumerate(OSL):
                nt = ow // 128

                def ldw(sy, o0=o0, ow=ow, nt=nt):
                    w1 = self.wfs[:, 0:DC * ow].rearrange("p (c a q) -> p c a q",
                                                          a=nt, q=128)
                    w2 = self.wfs2[:, 0:DC * ow].rearrange("p (c a q) -> p c a q",
                                                           a=nt, q=128)
                    inst = None
                    for a in range(nt):
                        self.dma(sy, None, w1[:, :, a, :],
                                 self.swg[o0 + a * 128:o0 + (a + 1) * 128, :],
                                 transpose=True)
                        if a < nt - 1:
                            self.dma(sy, None, w2[:, :, a, :],
                                     self.swu[o0 + a * 128:o0 + (a + 1) * 128, :],
                                     transpose=True)
                        else:
                            inst = sy.dma_start_transpose(
                                w2[:, :, a, :],
                                self.swu[o0 + a * 128:o0 + (a + 1) * 128, :])
                    return inst
                w = [f"{g_}_hqf"]
                if osi >= 1:
                    w.append(f"{gt}_gmm{osi - 1}")
                elif gi == 1:
                    w.append(f"{g_}_f0_gmm{len(OSL) - 1}")
                S("sync", w, ldw, f"{gt}_wu{osi}")

                def gumm(t, o0=o0, ow=ow, nt=nt, gi=gi, osi=osi):
                    wv = self.wfs[:, 0:DC * ow].rearrange("p (c a q) -> p c (a q)",
                                                          a=nt, q=128)
                    wv2 = self.wfs2[:, 0:DC * ow].rearrange("p (c a q) -> p c (a q)",
                                                            a=nt, q=128)
                    for kc in range(DC):
                        t.matmul(self.psA[:, 0:ow], hqf[:, kc, gi * 128:(gi + 1) * 128],
                                 wv[:, kc, :], start=(kc == 0), stop=(kc == DC - 1))
                    for kc in range(DC):
                        inst = t.matmul(self.psB[:, 0:ow],
                                        hqf[:, kc, gi * 128:(gi + 1) * 128],
                                        wv2[:, kc, :], start=(kc == 0), stop=(kc == DC - 1))
                    return inst
                w = [f"{gt}_wu{osi}"]
                if osi >= 1:
                    w += [f"{gt}_gu{osi - 1}", f"{gt}_sg{osi - 1}"]
                elif gi == 0:
                    w.append(f"{g_}_x2_{NG - 1}_1")
                else:
                    w.append(f"{g_}_f0_hm")
                S("tensor", w, gumm, f"{gt}_gmm{osi}")
                S("scalar", [f"{gt}_gmm{osi}"], lambda sc, o0=o0, ow=ow: sc.activation(
                    self.sg_t[:, o0:o0 + ow], self.psA[:, 0:ow], AF.Silu), f"{gt}_sg{osi}")
                S("vector", [f"{gt}_gmm{osi}"], lambda v, o0=o0, ow=ow: v.tensor_copy(
                    self.up_t[:, o0:o0 + ow], self.psB[:, 0:ow]), f"{gt}_gu{osi}")

            S("vector", [f"{gt}_gu{len(OSL) - 1}", f"{gt}_sg{len(OSL) - 1}"],
              lambda v: v.tensor_tensor(self.sg_t[:], self.sg_t[:], self.up_t[:],
                                        ALU.mult), f"{gt}_hm")
            if dbg_on and gi == 0:
                S("gpsimd", [f"{gt}_hm"], lambda g: g.dma_start(
                    out=self.dbg_bufs["d_hm"], in_=self.sg_t[:]), f"{gt}_dhm")
            S("vector", [f"{gt}_hm"], lambda v: self.scales_dve(
                v, self.sg_t[:], self.ss_t, I3), f"{gt}_hms")
            S("scalar", [f"{gt}_hms"], lambda sc: self.recip_act(
                sc, self.rs_t, self.ss_t, NB3), f"{gt}_hmr")
            def hmlad(v):
                for f0, F in ((0, 1024), (1024, 1024), (2048, 768)):
                    inst = self.ladder(v, self.sg_t[:, f0:f0 + F],
                                       self.up_t[:, f0:f0 + F],
                                       self.ss_t[:, f0 // 16:], self.rs_t[:, f0 // 16:],
                                       F)
                return inst
            S("vector", [f"{gt}_hmr"], hmlad, f"{gt}_hmq")
            if dbg_on and gi == 0:
                S("gpsimd", [f"{gt}_hmq"], lambda g: g.dma_start(
                    out=self.dbg_bufs["d_hmq"], in_=self.up_t[:]), f"{gt}_dhmq")

            w = [f"{gt}_hmq"] + ([f"{g_}_f0_dmm3"] if gi == 1 else [])
            S("sync", w, lambda g: g.dma_start_transpose(
                h3f, self.up_t.rearrange("p (c q) -> p c q", q=128)), f"{gt}_h3f")

            for osi in range(4):
                def ldwd(sy, osi=osi):
                    wv = self.wds[:, 0:KC3 * 256].rearrange("p (c a q) -> p c a q",
                                                            a=2, q=128)
                    self.dma(sy, None, wv[:, :, 0, :],
                             self.swd[osi * 256:osi * 256 + 128, :], transpose=True)
                    return sy.dma_start_transpose(
                        wv[:, :, 1, :], self.swd[osi * 256 + 128:(osi + 1) * 256, :])
                w = [f"{gt}_h3f"]
                if osi > 0:
                    w.append(f"{gt}_dmm{osi - 1}")
                elif gi > 0:
                    w.append(f"{g_}_f0_dmm3")
                S("sync", w, ldwd, f"{gt}_wd{osi}")

                def dnm(t, osi=osi):
                    wv = self.wds[:, 0:KC3 * 256].rearrange("p (c a q) -> p c (a q)",
                                                            a=2, q=128)
                    for kc in range(KC3):
                        inst = t.matmul(self.psC[:, 0:256], h3f[:, kc, :], wv[:, kc, :],
                                        start=(kc == 0), stop=(kc == KC3 - 1))
                    return inst
                w = [f"{gt}_wd{osi}"]
                if osi >= 1:
                    w.append(f"{gt}_dn{osi - 1}")
                elif gi == 0:
                    w.append(f"{g_}_wue7")
                else:
                    w.append(f"{g_}_f0_dn3")
                S("tensor", w, dnm, f"{gt}_dmm{osi}")

                def outev(v, osi=osi, gi=gi):
                    gbb = self.modt[:, 5 * D + osi * 256:5 * D + (osi + 1) * 256]
                    v.tensor_tensor(self.sc_f[:, 0:256], self.psC[:, 0:256], gbb, ALU.mult)
                    gq, rr = divmod(osi * 256, 1024)
                    return v.tensor_tensor(x3[:, gi, osi * 256:(osi + 1) * 256],
                                           self.sc_f[:, 0:256],
                                           x3[:, gi, osi * 256:(osi + 1) * 256], ALU.add)
                S("vector", [f"{gt}_dmm{osi}"], outev, f"{gt}_dn{osi}")

        S("gpsimd", [f"{g_}_f0_dn3", f"{g_}_f1_dn3"], lambda g: g.dma_start(
            out=self.dout[s, t0:t0 + PT, :].rearrange("(g p) d -> p g d", p=128),
            in_=x3), f"{g_}_out")

    def build(self):
        with contextlib.ExitStack() as es:
            self.declare()
            self.alloc(es)
            self.p0()
            for s in range(self.nseq):
                self.seq_setup(s)
                for pi in range(self.npiece):
                    self.piece(s, pi)
            fin = [self.cur(s) for s in
                   ("vv", "aa", "tt", *SDPOOL, *GDPOOL)]
            pr = self

            def final(sy):
                pr.wait(sy, *fin)
            self.block.sync(final)
        return self.nc


def _consts():
    eye = np.eye(128, dtype=np.float32)
    m_tril = np.zeros((128, 128), np.float32)
    m_triu = np.zeros((128, 128), np.float32)
    for blk in range(2):
        for i in range(64):
            jj = np.arange(64)
            m_tril[blk * 64 + i, blk * 64 + jj[jj < i]] = 1.0
            m_triu[blk * 64 + i, blk * 64 + jj[jj > i]] = 1.0
    mask3 = np.stack([m_tril, m_triu, -m_triu])
    hsel = np.zeros((NH, NH * 128), np.float32)
    for h in range(NH):
        hsel[h, h * 128:(h + 1) * 128] = 1.0
    colsel = np.zeros((128, NH * NH), np.float32)
    for h in range(NH):
        colsel[:, h * NH + h] = 1.0
    ones4 = np.ones((NH, 1), np.float32)
    return eye, mask3, hsel, colsel, ones4


def _diag_const(conv_q, conv_k, conv_v):
    cd = np.zeros((12, DC, 128, 128), np.float32)
    for ti, cw in enumerate((conv_q, conv_k, conv_v)):
        w = np.asarray(cw, np.float32)[:, 0, :]
        for tau in range(4):
            for dc in range(DC):
                np.fill_diagonal(cd[ti * 4 + tau, dc], w[dc * 128:(dc + 1) * 128, tau])
    return cd


_NC_CACHE = {}


def _in_maps(L0_attn_x, L0_attn_Wq, L0_attn_Wk, L0_attn_Wv, L0_attn_Wb,
             L0_attn_conv_q, L0_attn_conv_k, L0_attn_conv_v, L0_attn_g,
             L0_attn_Wo, L0_ffn_w_gate, L0_ffn_w_up, L0_ffn_w_down,
             L0_c, L0_wmod):
    x = np.ascontiguousarray(np.asarray(L0_attn_x, np.float32))
    eye, mask3, hsel, colsel, ones4 = _consts()
    cdiag = _diag_const(L0_attn_conv_q, L0_attn_conv_k, L0_attn_conv_v)
    g = np.asarray(L0_attn_g, np.float32)
    gcol = np.ascontiguousarray(g.reshape(2, 128).T)
    base = {
        "wmod": np.ascontiguousarray(np.asarray(L0_wmod, np.float32)),
        "Wq": np.ascontiguousarray(np.asarray(L0_attn_Wq, np.float32)),
        "Wk": np.ascontiguousarray(np.asarray(L0_attn_Wk, np.float32)),
        "Wv": np.ascontiguousarray(np.asarray(L0_attn_Wv, np.float32)),
        "Wo": np.ascontiguousarray(np.asarray(L0_attn_Wo, np.float32)),
        "Wb": np.ascontiguousarray(np.asarray(L0_attn_Wb, np.float32)),
        "wg": np.ascontiguousarray(np.asarray(L0_ffn_w_gate, np.float32)),
        "wu": np.ascontiguousarray(np.asarray(L0_ffn_w_up, np.float32)),
        "wd": np.ascontiguousarray(np.asarray(L0_ffn_w_down, np.float32)),
        "cdiag": cdiag, "gcol": gcol, "eye": eye,
        "eyeb": eye.astype(BF), "mask3": mask3.astype(BF),
        "hsel": hsel, "colsel": colsel.astype(BF),
        "ones4": ones4,
    }
    cc = np.asarray(L0_c, np.float32)
    in_maps = []
    for ci in range(NCORE):
        m = dict(base)
        m["x"] = np.ascontiguousarray(x[ci * BPC:(ci + 1) * BPC])
        m["cc"] = np.ascontiguousarray(cc[ci * BPC:(ci + 1) * BPC])
        am = np.zeros((1, BPC * CH), np.float32)
        for sloc in range(BPC):
            gb = ci * BPC + sloc
            am[0, sloc * CH:sloc * CH + gb + 1] = 1.0
        m["amask"] = am
        in_maps.append(m)
    return in_maps


def kernel(L0_attn_x, L0_attn_Wq, L0_attn_Wk, L0_attn_Wv, L0_attn_Wb,
           L0_attn_conv_q, L0_attn_conv_k, L0_attn_conv_v, L0_attn_g, L0_attn_Wo,
           L0_ffn_w_gate, L0_ffn_w_up, L0_ffn_w_down, L0_c, L0_wmod):
    in_maps = _in_maps(L0_attn_x, L0_attn_Wq, L0_attn_Wk, L0_attn_Wv,
                       L0_attn_Wb, L0_attn_conv_q, L0_attn_conv_k,
                       L0_attn_conv_v, L0_attn_g, L0_attn_Wo, L0_ffn_w_gate,
                       L0_ffn_w_up, L0_ffn_w_down, L0_c, L0_wmod)
    if "nc" not in _NC_CACHE:
        _NC_CACHE["nc"] = Bld().build()
    nc = _NC_CACHE["nc"]
    res = run_bass_kernel_spmd(nc, in_maps, core_ids=list(range(NCORE)))
    return np.concatenate([r["out"] for r in res.results], axis=0)


# version 05
# speedup vs baseline: 1.0494x; 1.0494x over previous
import contextlib
import numpy as np
import ml_dtypes
import concourse.bass as bass
import concourse.mybir as mybir
from concourse.bass_utils import run_bass_kernel_spmd

F32 = mybir.dt.float32
BF16 = mybir.dt.bfloat16
AF = mybir.ActivationFunctionType
ALU = mybir.AluOpType
AX = mybir.AxisListType
BF = ml_dtypes.bfloat16

B, L, D = 32, 2048, 1024
NCORE = 8
BPC = B // NCORE
NH, HD, CH = 4, 256, 64
I3 = 2816
PT = 256
NG = PT // 128
NCHU = PT // CH
NPAIR = 2
NPIECE = L // PT
DC = D // 128
KC3 = I3 // 128
NB3 = I3 // 16
OSL = [(i * 256, 256) for i in range(11)]
EPS = np.float32(1e-5)
NQPOOL = 8
SDPOOL = [f"sd{i}" for i in range(NQPOOL)]
GDPOOL = [f"gd{i}" for i in range(NQPOOL)]
QSEM = {"vector": "vv", "scalar": "aa", "tensor": "tt"}


class Bld:
    def __init__(self, dbg=False, nseq=BPC, npiece=NPIECE, serial=True):
        self.nc = bass.Bass("TRN2", target_bir_lowering=False,
                            detect_race_conditions=False)
        self.dbg = dbg
        self.nseq = nseq
        self.npiece = npiece
        self.serial = serial
        self.cnt = {}
        self.ev = {}
        self.hand = {}
        self.dbg_bufs = {}
        self.rr_sd = 0
        self.rr_gd = 0
        self.cursem = None
        self.watermark = {}

    def mark(self, inst, sem, label=None, step=1):
        self.cnt[sem] += step
        inst.then_inc(self.hand[sem], step)
        if label:
            self.ev[label] = (sem, self.cnt[sem])
        return (sem, self.cnt[sem])

    def cur(self, sem):
        return (sem, self.cnt[sem])

    def wait(self, eng, *evs):
        for e in evs:
            sem, val = self.ev[e] if isinstance(e, str) else e
            if val > 0:
                eng.wait_ge(self.hand[sem], val)

    def barrier_evs(self):
        return [self.cur(s) for s in
                ("vv", "aa", "tt", *SDPOOL, *GDPOOL)]

    _GUARD_OPS = frozenset((
        "tensor_tensor", "tensor_scalar", "tensor_copy", "tensor_reduce",
        "memset", "scalar_tensor_tensor", "tensor_tensor_scan", "activation",
        "copy", "tensor_add", "tensor_mul", "tensor_sub", "tensor_max",
        "select", "reciprocal", "iota"))

    class _Guard:
        def __init__(self, eng, bld, sem):
            self._eng = eng
            self._bld = bld
            self._sem = sem
            self._last = None

        def __getattr__(self, name):
            attr = getattr(self._eng, name)
            if name in Bld._GUARD_OPS:
                def wrap(*args, **kw):
                    inst = attr(*args, **kw)
                    out_ap = args[0]
                    nfree = 1
                    try:
                        for s in out_ap.shape[1:]:
                            nfree *= int(s)
                    except Exception:
                        nfree = 0
                    if nfree < 256:
                        b = self._bld
                        b.cnt[self._sem] += 1
                        inst.then_inc(b.hand[self._sem], 1)
                        self._eng.wait_ge(b.hand[self._sem], b.cnt[self._sem])
                        self._last = (inst, True)
                    else:
                        self._last = (inst, False)
                    return inst
                return wrap
            return attr

    def sec(self, engname, waits, fn, label=None, step=None):
        if engname == "sync":
            sem = SDPOOL[self.rr_sd % NQPOOL]
            self.rr_sd += 1
            step = 16
        elif engname == "gpsimd":
            sem = GDPOOL[self.rr_gd % NQPOOL]
            self.rr_gd += 1
            step = 16
        else:
            sem = QSEM[engname]
            if step is None:
                step = 1
        pr = self

        def body(eng):
            pr.cursem = sem
            if pr.serial:
                wm = pr.watermark.setdefault(engname, {})
                for s2 in sorted(pr.cnt):
                    c2 = pr.cnt[s2]
                    if c2 > wm.get(s2, 0):
                        eng.wait_ge(pr.hand[s2], c2)
                        wm[s2] = c2
            pr.wait(eng, *waits)
            if engname in ("vector", "scalar"):
                gd = Bld._Guard(eng, pr, sem)
                fn(gd)
                if gd._last is not None and not gd._last[1]:
                    pr.cnt[sem] += 1
                    gd._last[0].then_inc(pr.hand[sem], 1)
                if label is not None:
                    pr.ev[label] = (sem, pr.cnt[sem])
            else:
                inst = fn(eng)
                if label is not None:
                    pr.mark(inst, sem, label, step)
        getattr(self.block, engname)(body)

    def dma(self, eng, qsem_ignored, out, in_, transpose=False, label=None):
        if transpose:
            inst = eng.dma_start_transpose(out, in_)
        else:
            inst = eng.dma_start(out=out, in_=in_)
        return self.mark(inst, self.cursem, label, 16)

    def declare(self):
        dram = self.nc.dram_tensor
        self.dx = dram("x", [BPC, L, D], F32, kind="ExternalInput").ap()
        self.dout = dram("out", [BPC, L, D], F32, kind="ExternalOutput").ap()
        self.dcc = dram("cc", [BPC, D], F32, kind="ExternalInput").ap()
        self.dwmod = dram("wmod", [D, 6 * D], F32, kind="ExternalInput").ap()
        self.dW = [dram(n, [D, D], F32, kind="ExternalInput").ap()
                   for n in ("Wq", "Wk", "Wv", "Wo")]
        self.dWb = dram("Wb", [D, NH], F32, kind="ExternalInput").ap()
        self.dwg = dram("wg", [I3, D], F32, kind="ExternalInput").ap()
        self.dwu = dram("wu", [I3, D], F32, kind="ExternalInput").ap()
        self.dwd = dram("wd", [D, I3], F32, kind="ExternalInput").ap()
        self.dcdiag = dram("cdiag", [12, DC, 128, 128], F32, kind="ExternalInput").ap()
        self.dgcol = dram("gcol", [128, 2], F32, kind="ExternalInput").ap()
        self.deye = dram("eye", [128, 128], F32, kind="ExternalInput").ap()
        self.deyeb = dram("eyeb", [128, 128], BF16, kind="ExternalInput").ap()
        self.dmask3 = dram("mask3", [3, 128, 128], BF16, kind="ExternalInput").ap()
        self.dhsel = dram("hsel", [NH, NH * 128], F32, kind="ExternalInput").ap()
        self.dcolsel = dram("colsel", [128, NH * NH], BF16, kind="ExternalInput").ap()
        self.dones4 = dram("ones4", [NH, 1], F32, kind="ExternalInput").ap()
        self.damask = dram("amask", [1, BPC * CH], F32, kind="ExternalInput").ap()
        self.swattn = dram("swattn", [4, D, D], BF16, kind="Internal").ap()
        self.swg = dram("swgq", [I3, D], BF16, kind="Internal").ap()
        self.swu = dram("swuq", [I3, D], BF16, kind="Internal").ap()
        self.swd = dram("swdq", [D, I3], BF16, kind="Internal").ap()
        self.smod = dram("smod", [BPC, 6, D], F32, kind="Internal").ap()
        if self.dbg:
            for n, sh, dt in [("d_h", [128, NG * D], BF16), ("d_hf", [128, DC * PT], BF16),
                              ("d_q", [128, DC * PT], BF16), ("d_k", [128, DC * PT], BF16),
                              ("d_v", [128, DC * PT], BF16), ("d_kb", [128, DC * PT], BF16),
                              ("d_m", [128, NPAIR * 128], F32),
                              ("d_tt", [128, NPAIR * 128], BF16),
                              ("d_w", [128, DC * PT], BF16), ("d_u", [128, DC * PT], BF16),
                              ("d_pp", [128, 2 * DC * NCHU], F32),
                              ("d_o", [128, DC * PT], BF16),
                              ("d_x2", [128, NG * D], F32), ("d_h2q", [128, NG * D], BF16),
                              ("d_hm", [128, I3], BF16), ("d_hmq", [128, I3], BF16),
                              ("d_ab", [128, PT], BF16),
                              ("d_mod", [128, BPC * 6 * DC], F32)]:
                self.dbg_bufs[n] = dram(n, sh, dt, kind="ExternalOutput").ap()

    def alloc(self, es):
        nc = self.nc
        T_ = lambda n, sh, dt: es.enter_context(nc.sbuf_tensor(n, sh, dt))
        self.eye_b = T_("eye_b", [128, 128], BF16)
        self.eye_f = T_("eye_f", [128, 128], F32)
        self.mask3 = T_("mask3_s", [128, 3 * 128], BF16)
        self.hsel = T_("hsel_s", [NH, NH * 128], F32)
        self.colsel = T_("colsel_s", [128, NH * NH], BF16)
        self.ones4 = T_("ones4_s", [NH, 1], F32)
        self.amask = T_("amask_s", [1, BPC * CH], F32)
        self.gcol = T_("gcol_s", [128, 2], F32)
        self.diag = T_("diag", [128, 12 * DC * 128], BF16)
        self.wb_sb = T_("wb_sb", [128, DC * NH], BF16)
        self.cT = T_("cT", [128, DC * BPC], BF16)
        self.modc = T_("modc", [128, BPC * 6 * DC], F32)
        self.modt = T_("modt", [128, 6 * D], BF16)

        self.wql = T_("wql", [128, DC * D], BF16)
        self.wfs = T_("wfs", [128, DC * 256], BF16)
        self.wfs2 = T_("wfs2", [128, DC * 256], BF16)
        self.wds = T_("wds", [128, KC3 * 256], BF16)
        self.x_t = T_("x_t", [128, NG * D], F32)
        self.sq_t = T_("sq_t", [128, D], F32)
        self.st = T_("st", [128, 32], F32)
        self.h_t = T_("h_t", [128, NG * D], BF16)
        self.h_f = T_("h_f", [128, DC * PT], BF16)
        self.p_f = T_("p_f", [128, DC * (PT + 3)], BF16)
        self.ptail = T_("ptail", [128, 3 * DC * 3], BF16)
        self.q_f = T_("q_f", [128, DC * PT], BF16)
        self.k_f = T_("k_f", [128, DC * PT], BF16)
        self.kb_f = T_("kb_f", [128, DC * PT], BF16)
        self.v_f = T_("v_f", [128, DC * PT], BF16)
        self.kb_t = T_("kb_t", [128, NG * D], BF16)
        self.w_f = T_("w_f", [128, DC * PT], BF16)
        self.u_f = T_("u_f", [128, DC * PT], BF16)
        self.sc_f = T_("sc_f", [128, DC * PT], F32)
        self.stgA = self.x_t
        self.stgB = self.sc_f
        self.a_b = T_("a_b", [128, PT], BF16)
        self.m_s = T_("m_s", [128, NPAIR * 128], F32)
        self.mt_s = T_("mt_s", [128, NPAIR * 128], F32)
        self.mn_s = T_("mn_s", [128, NPAIR * 128], F32)
        self.pwn = [T_("pwn0", [128, NPAIR * 128], F32),
                    T_("pwn1", [128, NPAIR * 128], F32)]
        self.pwt = [T_(f"pwt{i}", [128, NPAIR * 128], F32) for i in range(3)]
        self.pw = {-2: self.pwt[0], -4: self.pwt[1], -8: self.pwt[2],
                   -16: self.pwt[0], -32: self.pwt[1]}
        self.tc0 = T_("tc0", [128, NPAIR * 128], F32)
        self.tc1 = self.tc0
        self.tt_s = T_("tt_s", [128, NPAIR * 128], BF16)
        self.rq = T_("rq", [NH, PT], F32)
        self.rk = T_("rk", [NH, PT], F32)
        self.be = T_("be", [NH, PT], F32)
        self.ro = T_("ro", [NH, PT], F32)
        self.r1 = T_("r1", [NH, PT], F32)
        self.ar = T_("ar", [1, PT], F32)
        self.pp = T_("pp", [128, 2 * DC * NCHU], F32)
        self.sd = T_("sd", [128, DC * (NCHU + 1)], F32)
        self.sg_t = T_("sg_t", [128, I3], BF16)
        self.up_t = T_("up_t", [128, I3], BF16)
        self.h3_f = T_("h3_f", [128, KC3 * 128], BF16)
        self.hq_t = self.h_t
        self.hq_f = self.k_f
        self.ss_t = T_("ss_t", [128, NB3], F32)
        self.rs_t = T_("rs_t", [128, NB3], F32)
        self.qa_s = T_("qa_s", [128, I3], BF16)
        self.qc_s = self.kb_f[:, 0:1024]
        self.qh_s = self.kb_f[:, 1024:2048]
        P_ = lambda n, sh: es.enter_context(nc.psum_tensor(n, sh, F32))
        self.psA = P_("psA", [128, 512])
        self.psB = P_("psB", [128, 512])
        self.psC = P_("psC", [128, 512])
        self.psG = P_("psG", [128, 512])
        self.psT = P_("psT", [128, 512])
        self.psX = P_("psX", [128, 512])
        self.psR = P_("psR", [16, 512])
        self.psR2 = P_("psR2", [16, 512])
        for s in ("vv", "aa", "tt", *SDPOOL, *GDPOOL):
            self.hand[s] = es.enter_context(nc.semaphore(s))
            self.cnt[s] = 0
        self.block = es.enter_context(nc.Block())

    def ladder(self, v, X, out, ss, rs, FD):
        nb = FD // 16
        qa, qc, qh = self.qa_s[:, 0:FD], self.qc_s[:, 0:FD], self.qh_s[:, 0:FD]
        x3 = X.rearrange("p (n b) -> p n b", b=16)
        rs3 = rs[:, 0:nb, None].broadcast_to([128, nb, 16])
        ss3 = ss[:, 0:nb, None].broadcast_to([128, nb, 16])
        v.tensor_tensor(qa.rearrange("p (n b) -> p n b", b=16), x3, rs3, ALU.mult)
        v.scalar_tensor_tensor(qa, qa, -1.0, qa, ALU.mult, ALU.max)
        v.tensor_scalar(qc, qa, 0.25, None, ALU.is_gt)
        v.scalar_tensor_tensor(qc, qa, 0.75, qc, ALU.is_gt, ALU.add)
        v.scalar_tensor_tensor(qc, qa, 1.25, qc, ALU.is_gt, ALU.add)
        v.scalar_tensor_tensor(qc, qa, 1.75, qc, ALU.is_gt, ALU.add)
        v.tensor_scalar(qh, qa, 2.5, None, ALU.is_gt)
        v.scalar_tensor_tensor(qh, qa, 3.5, qh, ALU.is_gt, ALU.add)
        v.scalar_tensor_tensor(qh, qa, 5.0, qh, ALU.is_gt, ALU.add)
        v.scalar_tensor_tensor(qh, qa, 5.0, qh, ALU.is_gt, ALU.add)
        v.scalar_tensor_tensor(qh, qc, 0.5, qh, ALU.mult, ALU.add)
        v.tensor_scalar(qc, X, 0.0, None, ALU.is_lt)
        v.scalar_tensor_tensor(qc, X, 0.0, qc, ALU.is_gt, ALU.subtract)
        v.tensor_tensor(qh, qh, qc, ALU.mult)
        return v.tensor_tensor(out.rearrange("p (n b) -> p n b", b=16),
                               qh.rearrange("p (n b) -> p n b", b=16), ss3, ALU.mult)

    def scales_dve(self, v, X, ss, FD):
        nb = FD // 16
        v.tensor_reduce(ss[:, 0:nb], X.rearrange("p (n b) -> p n b", b=16),
                        AX.X, ALU.max, apply_absolute_value=True)
        return v.tensor_scalar(ss[:, 0:nb], ss[:, 0:nb], 1.0 / 6.0, 1e-12,
                               ALU.mult, ALU.max)

    def recip_act(self, sc, rs, ss, nb):
        sc.activation(rs[:, 0:nb], ss[:, 0:nb], AF.Ln)
        return sc.activation(rs[:, 0:nb], rs[:, 0:nb], AF.Exp, scale=np.float32(-1.0))

    def p0(self):
        S = self.sec

        def lds(sy):
            for dst, src in ((self.eye_f[:], self.deye), (self.eye_b[:], self.deyeb),
                             (self.mask3.rearrange("p (i q) -> p i q", q=128),
                              self.dmask3.rearrange("i p q -> p i q")),
                             (self.hsel[:], self.dhsel), (self.colsel[:], self.dcolsel),
                             (self.ones4[:], self.dones4), (self.amask[:], self.damask),
                             (self.gcol[:], self.dgcol)):
                self.dma(sy, None, dst, src)
            return sy.dma_start(out=self.stgA[0:BPC, 0:D], in_=self.dcc)
        S("sync", [], lds, "c0")
        S("scalar", ["c0"], lambda sc: sc.activation(
            self.stgA[0:BPC, 0:D], self.stgA[0:BPC, 0:D], AF.Silu), "c2")
        for dc in range(DC):
            w = ["c2"] if dc == 0 else [f"ctv{dc - 1}"]
            S("tensor", w, lambda t, dc=dc: t.transpose(
                self.psT[:, 0:BPC], self.stgA[0:BPC, dc * 128:(dc + 1) * 128],
                self.eye_f[0:BPC, 0:BPC]), f"ctr{dc}")
            S("vector", [f"ctr{dc}"], lambda v, dc=dc: v.tensor_copy(
                self.cT[:, dc * BPC:(dc + 1) * BPC], self.psT[:, 0:BPC]), f"ctv{dc}")

        for i in range(32):
            wi, kc = divmod(i, DC)
            w = [f"wcv{i - 2}"] if i >= 2 else [f"ctv{DC - 1}"]
            S("sync", w, lambda sy, wi=wi, i=i, kc=kc: sy.dma_start(
                out=(self.stgA if i % 2 == 0 else self.stgB)[:, 0:D],
                in_=self.dW[wi][kc * 128:(kc + 1) * 128, :]), f"wld{i}")

            def cast_w(v, wi=wi, kc=kc, i=i):
                if i >= 4:
                    self.wait(v, f"wst{i - 4}")
                src = (self.stgA if i % 2 == 0 else self.stgB)[:, 0:D]
                dst = self.wql[:, (i % 4) * 1024:(i % 4) * 1024 + D]
                if wi == 3:
                    return v.tensor_scalar(dst, src, self.gcol[:, kc % 2:kc % 2 + 1],
                                           None, ALU.mult)
                return v.tensor_copy(dst, src)
            S("vector", [f"wld{i}"], cast_w, f"wcv{i}")
            S("gpsimd", [f"wcv{i}"], lambda g, wi=wi, kc=kc, i=i: g.dma_start(
                out=self.swattn[wi, kc * 128:(kc + 1) * 128, :],
                in_=self.wql[:, (i % 4) * 1024:(i % 4) * 1024 + D]), f"wst{i}")

        for i in range(12):
            w = [f"dgc{i - 2}"] if i >= 2 else ["wcv31"]
            S("sync", w, lambda sy, i=i: sy.dma_start(
                out=(self.stgA if i % 2 == 0 else self.stgB)[:, 0:DC * 128]
                .rearrange("p (c q) -> p c q", q=128),
                in_=self.dcdiag[i].rearrange("c p q -> p c q")), f"dgl{i}")
            S("vector", [f"dgl{i}"], lambda v, i=i: v.tensor_copy(
                self.diag[:, i * DC * 128:(i + 1) * DC * 128],
                (self.stgA if i % 2 == 0 else self.stgB)[:, 0:DC * 128]), f"dgc{i}")

        S("sync", ["dgc11"], lambda sy: sy.dma_start(
            out=self.stgA[:, 0:DC * NH].rearrange("p (c h) -> p c h", c=DC),
            in_=self.dWb.rearrange("(c p) h -> p c h", p=128)), "wbl")
        S("vector", ["wbl"], lambda v: v.tensor_copy(self.wb_sb[:], self.stgA[:, 0:DC * NH]),
          "wbc")

        for idx in range(32):
            kc, sl = divmod(idx, 4)
            w = [f"mmt{idx - 2}"] if idx >= 2 else ["wbc"]
            S("sync", w, lambda sy, kc=kc, sl=sl, idx=idx: sy.dma_start(
                out=(self.stgA if idx % 2 == 0 else self.stgB)[:, 0:1536],
                in_=self.dwmod[kc * 128:(kc + 1) * 128, sl * 1536:(sl + 1) * 1536]),
              f"mml{idx}")
            w2 = [f"mml{idx}"]
            if idx < 4:
                w2 += [f"wst{j}" for j in range(28, 32)]
            S("vector", w2, lambda v, idx=idx: v.tensor_copy(
                self.wql[:, (idx % 2) * 1536:(idx % 2) * 1536 + 1536],
                (self.stgA if idx % 2 == 0 else self.stgB)[:, 0:1536]), f"mmc{idx}")

            def modmm(t, kc=kc, sl=sl, idx=idx):
                if idx >= 2:
                    self.wait(t, f"mme{idx - 2}")
                for ob in range(12):
                    oc = sl * 12 + ob
                    inst = t.matmul(self.psC[:, oc * NH:(oc + 1) * NH],
                                    self.wql[:, (idx % 2) * 1536 + ob * 128:
                                             (idx % 2) * 1536 + (ob + 1) * 128],
                                    self.cT[:, kc * BPC:(kc + 1) * BPC],
                                    start=True, stop=True)
                return inst
            S("tensor", [f"mmc{idx}"], modmm, f"mmt{idx}")

            def modacc(v, kc=kc, sl=sl, idx=idx):
                dst = self.modc[:, sl * 48:(sl + 1) * 48]
                srcp = self.psC[:, sl * 48:(sl + 1) * 48]
                if kc == 0:
                    return v.tensor_copy(dst, srcp)
                return v.tensor_tensor(dst, dst, srcp, ALU.add)
            S("vector", [f"mmt{idx}"], modacc, f"mme{idx}")

        def modev(v):
            v.tensor_copy(self.stgB[:, 0:192], self.modc[:, 0:192])
            v.tensor_copy(
                self.modc.rearrange("p (b m o) -> p b m o", b=BPC, m=6, o=DC),
                self.stgB[:, 0:192].rearrange("p (m o b) -> p b m o", m=6, o=DC, b=BPC))
            mc = self.modc.rearrange("p (b m o) -> p b m o", b=BPC, m=6, o=DC)
            for b in range(BPC):
                v.tensor_scalar(mc[:, b, 1, :], mc[:, b, 1, :], 1.0, None, ALU.add)
                v.tensor_scalar(mc[:, b, 4, :], mc[:, b, 4, :], 1.0, None, ALU.add)
                for pa in (2, 5):
                    gsl = mc[:, b, pa, :]
                    s1, s2 = self.st[:, 0:DC], self.st[:, DC:2 * DC]
                    v.tensor_scalar(s1, gsl, 0.0, 2.0, ALU.is_ge, ALU.mult)
                    v.tensor_scalar(s1, s1, -1.0, None, ALU.add)
                    v.scalar_tensor_tensor(s2, gsl, -1.0, gsl, ALU.mult, ALU.max)
                    v.tensor_scalar(s2, s2, 0.25, None, ALU.max)
                    inst = v.tensor_tensor(gsl, s1, s2, ALU.mult)
            return inst
        S("vector", ["mme31"], modev, "modr")
        if self.dbg:
            S("gpsimd", ["modr"], lambda g: g.dma_start(
                out=self.dbg_bufs["d_mod"], in_=self.modc[:]), "dmod")
        for b in range(BPC):
            w = ["modr"] if b == 0 else [f"modev{b - 1}"]
            S("tensor", w, lambda t, b=b: t.transpose(
                self.psT[0:48, 0:128], self.modc[:, b * 48:(b + 1) * 48], self.eye_f[:]),
              f"modtr{b}")

            def mev(v, b=b):
                if b > 0:
                    self.wait(v, f"modst{b - 1}")
                return v.tensor_copy(self.stgA[0:48, 0:128], self.psT[0:48, 0:128])
            S("vector", [f"modtr{b}"], mev, f"modev{b}")
            S("gpsimd", [f"modev{b}"], lambda g, b=b: g.dma_start(
                out=self.smod[b].rearrange("m (o q) -> (m o) q", q=128),
                in_=self.stgA[0:48, 0:128]), f"modst{b}")

        specs = [(self.dwg, self.swg, I3, D, [(0, 1024)]),
                 (self.dwu, self.swu, I3, D, [(0, 1024)]),
                 (self.dwd, self.swd, D, I3, [(0, 1024), (1024, 1024), (2048, 768)])]
        n = 0
        for dsrc, dscr, R, FT, chunks in specs:
            for rc in range(R // 128):
                for f0, F in chunks:
                    nb = F // 16
                    w = [f"fqv{n - 2}"] if n >= 2 else [f"modst{BPC - 1}"]
                    S("sync", w, lambda sy, dsrc=dsrc, rc=rc, n=n, f0=f0, F=F: sy.dma_start(
                        out=(self.stgA if n % 2 == 0 else self.stgB)[:, 0:F],
                        in_=dsrc[rc * 128:(rc + 1) * 128, f0:f0 + F]), f"fql{n}")

                    def fq_pre(v, n=n, F=F):
                        if n >= 1:
                            self.wait(v, f"fqw{n - 1}")
                        v.tensor_copy(self.sg_t[:, 0:F],
                                      (self.stgA if n % 2 == 0 else self.stgB)[:, 0:F])
                        return self.scales_dve(v, self.sg_t[:, 0:F], self.ss_t, F)
                    S("vector", [f"fql{n}"], fq_pre, f"fqs{n}")
                    S("scalar", [f"fqs{n}"], lambda sc, nb=nb: self.recip_act(
                        sc, self.rs_t, self.ss_t, nb), f"fqr{n}")
                    S("vector", [f"fqr{n}"], lambda v, F=F: self.ladder(
                        v, self.sg_t[:, 0:F], self.up_t[:, 0:F], self.ss_t, self.rs_t, F),
                      f"fqv{n}")
                    S("gpsimd", [f"fqv{n}"], lambda g, dscr=dscr, rc=rc, f0=f0, F=F:
                      g.dma_start(out=dscr[rc * 128:(rc + 1) * 128, f0:f0 + F],
                                  in_=self.up_t[:, 0:F]), f"fqw{n}")
                    n += 1

    def seq_setup(self, s):
        S = self.sec
        bar = self.barrier_evs()

        def mb(gq):
            inst = None
            for slot, pa in enumerate((1, 0, 4, 3, 2, 5)):
                if inst is not None:
                    self.mark(inst, self.cursem, None, 16)
                inst = gq.dma_start(out=self.modt[:, slot * D:(slot + 1) * D],
                                    in_=self.smod[s, pa][None, :].broadcast_to([128, D]))
            return inst
        S("gpsimd", bar, mb, f"q{s}mb")

        def z(v):
            v.memset(self.sd[:], 0.0)
            return v.memset(self.ptail[:], 0.0)
        S("vector", [f"q{s}mb"], z, f"q{s}z")

    def piece(self, s, pi):
        S = self.sec
        t0 = pi * PT
        g_ = f"s{s}p{pi}"
        dbg_on = self.dbg and s == 0 and pi == 0
        bar = self.barrier_evs()

        x3 = self.x_t.rearrange("p (g d) -> p g d", g=NG)

        h3 = self.h_t.rearrange("p (g d) -> p g d", g=NG)
        hf = self.h_f.rearrange("p (c t) -> p c t", c=DC)
        pf = self.p_f.rearrange("p (c t) -> p c t", c=DC)
        qf = self.q_f.rearrange("p (c t) -> p c t", c=DC)
        kf = self.k_f.rearrange("p (c t) -> p c t", c=DC)
        kbf = self.kb_f.rearrange("p (c t) -> p c t", c=DC)
        vf = self.v_f.rearrange("p (c t) -> p c t", c=DC)
        wf = self.w_f.rearrange("p (c t) -> p c t", c=DC)
        uf = self.u_f.rearrange("p (c t) -> p c t", c=DC)
        scf = self.sc_f.rearrange("p (c t) -> p c t", c=DC)
        prb = self.qa_s[:, 0:DC * PT].rearrange("p (c t) -> p c t", c=DC)
        pp4 = self.pp.rearrange("p (a c n) -> p a c n", a=2, c=DC)
        sd3 = self.sd.rearrange("p (c n) -> p c n", c=DC)
        pt3 = self.ptail.rearrange("p (w c e) -> p w c e", w=3, c=DC)

        def ldx(sy):
            self.dma(sy, None, x3,
                     self.dx[s, t0:t0 + PT, :].rearrange("(g p) d -> p g d", p=128),
                     label=f"{g_}_x")
            return sy.dma_start(out=self.wql.rearrange("p (c d) -> p c d", c=DC),
                                in_=self.swattn[0].rearrange("(c p) d -> p c d", p=128))
        S("sync", bar + [f"q{s}z"], ldx, f"{g_}_w0")

        def ln(pref, mslot, asl):
            for g in range(NG):
                S("scalar", [f"{pref}_lnin" if g == 0 else f"{pref}_vr{g - 1}"],
                  lambda sc, g=g: sc.activation(self.sq_t[:], x3[:, g, :], AF.Square),
                  f"{pref}_sq{g}")
                S("vector", [f"{pref}_sq{g}"], lambda v, g=g: v.tensor_reduce(
                    self.st[:, 2 + g:3 + g], self.sq_t[:, None, :], AX.X, ALU.add),
                  f"{pref}_vr{g}")

            def ln_var(v):
                v.tensor_reduce(self.st[:, 0:NG], x3, AX.X, ALU.add)
                v.tensor_scalar(self.st[:, 0:NG], self.st[:, 0:NG], 1.0 / D, None, ALU.mult)
                v.tensor_scalar(self.st[:, 2:2 + NG], self.st[:, 2:2 + NG], 1.0 / D,
                                None, ALU.mult)
                v.tensor_tensor(self.st[:, 4:4 + NG], self.st[:, 0:NG], self.st[:, 0:NG],
                                ALU.mult)
                return v.tensor_tensor(self.st[:, 4:4 + NG], self.st[:, 2:2 + NG],
                                       self.st[:, 4:4 + NG], ALU.subtract)
            S("vector", [f"{pref}_vr{NG - 1}"], ln_var, f"{pref}_var")

            def ln_rstd(sc):
                sc.activation(self.st[:, 6:6 + NG], self.st[:, 4:4 + NG], AF.Ln, bias=EPS)
                return sc.activation(self.st[:, 6:6 + NG], self.st[:, 6:6 + NG],
                                     AF.Exp, scale=np.float32(-0.5))
            S("scalar", [f"{pref}_var"], ln_rstd, f"{pref}_rstd")

            def ln_apply(v):
                for g in range(NG):
                    v.tensor_scalar(h3[:, g, :], x3[:, g, :], self.st[:, g:g + 1],
                                    self.st[:, 6 + g:7 + g], ALU.subtract, ALU.mult)
                v.tensor_tensor(h3, h3, self.modt[:, mslot * D:(mslot + 1) * D]
                                [:, None, :].broadcast_to([128, NG, D]), ALU.mult)
                return v.tensor_tensor(h3, h3, self.modt[:, asl * D:(asl + 1) * D]
                                       [:, None, :].broadcast_to([128, NG, D]), ALU.add)
            S("vector", [f"{pref}_rstd"], ln_apply, f"{pref}_h")

        self.ev[f"{g_}A_lnin"] = self.ev[f"{g_}_x"]
        ln(f"{g_}A", 0, 1)

        def hf_tr(g):
            if dbg_on:
                self.dma(g, None, self.dbg_bufs["d_h"], self.h_t[:])
            for gg in range(NG - 1):
                self.dma(g, None, hf[:, :, gg * 128:(gg + 1) * 128],
                         h3[:, gg, :].rearrange("p (c q) -> p c q", q=128),
                         transpose=True)
            return g.dma_start_transpose(
                hf[:, :, (NG - 1) * 128:NG * 128],
                h3[:, NG - 1, :].rearrange("p (c q) -> p c q", q=128))
        S("sync", [f"{g_}A_h"], hf_tr, f"{g_}_hf")

        for tgt in range(3):
            dstf = (qf, kf, vf)[tgt]
            for ocg in range(0, DC, 2):
                def pj(t, tgt=tgt, ocg=ocg):
                    for oc in (ocg, ocg + 1):
                        ps = (self.psA, self.psB)[oc % 2]
                        for kc in range(DC):
                            inst = t.matmul(
                                ps[:, 0:PT],
                                self.wql[:, kc * D + oc * 128:kc * D + (oc + 1) * 128],
                                hf[:, kc, :], start=(kc == 0), stop=(kc == DC - 1))
                    return inst
                w = [f"{g_}_w{tgt}", f"{g_}_hf"]
                if ocg >= 2:
                    w.append(f"{g_}_pe{tgt}_{ocg - 2}")
                elif tgt > 0:
                    w.append(f"{g_}_pe{tgt - 1}_{DC - 2}")
                S("tensor", w, pj, f"{g_}_pj{tgt}_{ocg}")

                def pev(v, tgt=tgt, ocg=ocg):
                    for oc in (ocg, ocg + 1):
                        ps = (self.psA, self.psB)[oc % 2]
                        inst = v.tensor_copy(pf[:, oc, 3:3 + PT], ps[:, 0:PT])
                    return inst
                S("vector", [f"{g_}_pj{tgt}_{ocg}"], pev, f"{g_}_pe{tgt}_{ocg}")

            S("vector", [f"{g_}_pe{tgt}_{DC - 2}"], lambda v, tgt=tgt: v.tensor_copy(
                pf[:, :, 0:3], pt3[:, tgt, :, :]), f"{g_}_ph{tgt}")
            if tgt < 3:
                S("sync", [f"{g_}_pj{tgt}_{DC - 2}"], lambda sy, tgt=tgt: sy.dma_start(
                    out=self.wql.rearrange("p (c d) -> p c d", c=DC),
                    in_=self.swattn[tgt + 1].rearrange("(c p) d -> p c d", p=128)),
                  f"{g_}_w{tgt + 1}")

            for ocg in range(0, DC, 2):
                def cv(t, tgt=tgt, ocg=ocg):
                    for oc in (ocg, ocg + 1):
                        ps = (self.psC, self.psG)[oc % 2]
                        for tau in range(4):
                            dg = self.diag[:, (tgt * 4 + tau) * DC * 128 + oc * 128:
                                           (tgt * 4 + tau) * DC * 128 + (oc + 1) * 128]
                            inst = t.matmul(ps[:, 0:PT], dg, pf[:, oc, tau:tau + PT],
                                            start=(tau == 0), stop=(tau == 3))
                    return inst
                w = [f"{g_}_pe{tgt}_{ocg}", f"{g_}_ph{tgt}"]
                if ocg >= 2:
                    w.append(f"{g_}_cs{tgt}_{ocg - 2}")
                elif tgt > 0:
                    w.append(f"{g_}_cs{tgt - 1}_{DC - 2}")
                S("tensor", w, cv, f"{g_}_cm{tgt}_{ocg}")

                def cse(sc, tgt=tgt, ocg=ocg, dstf=dstf):
                    for oc in (ocg, ocg + 1):
                        ps = (self.psC, self.psG)[oc % 2]
                        inst = sc.activation(dstf[:, oc, :], ps[:, 0:PT], AF.Silu)
                    return inst
                S("scalar", [f"{g_}_cm{tgt}_{ocg}"], cse, f"{g_}_cs{tgt}_{ocg}")

            S("vector", [f"{g_}_pe{tgt}_{DC - 2}"], lambda v, tgt=tgt: v.tensor_copy(
                pt3[:, tgt, :, :], pf[:, :, PT:PT + 3]), f"{g_}_pt{tgt}")

        def btm(t):
            for kc in range(DC):
                inst = t.matmul(self.psR[0:NH, 0:PT], self.wb_sb[:, kc * NH:(kc + 1) * NH],
                                hf[:, kc, :], start=(kc == 0), stop=(kc == DC - 1))
            return inst
        S("tensor", [f"{g_}_hf"], btm, f"{g_}_bt")
        S("scalar", [f"{g_}_bt"], lambda sc: sc.activation(
            self.be[:], self.psR[0:NH, 0:PT], AF.Tanh, scale=np.float32(0.5)), f"{g_}_be1")
        S("vector", [f"{g_}_be1"], lambda v: v.tensor_scalar(
            self.be[:], self.be[:], 0.5, 0.5, ALU.mult, ALU.add), f"{g_}_be")

        S("vector", [f"{g_}_cs0_{DC - 2}", f"{g_}_cs1_{DC - 2}"],
          lambda v: v.tensor_tensor(prb, qf, qf, ALU.mult), f"{g_}_pq")

        def rowmm(dst, deps, lab):
            def fn(t):
                for dc in range(DC):
                    inst = t.matmul(dst, self.colsel[:, (dc // 2) * NH:(dc // 2 + 1) * NH],
                                    prb[:, dc, :], start=(dc == 0), stop=(dc == DC - 1))
                return inst
            S("tensor", deps, fn, lab)
        rowmm(self.psR[0:NH, PT:2 * PT], [f"{g_}_pq"], f"{g_}_qq")
        S("vector", [f"{g_}_qq"], lambda v: v.tensor_tensor(prb, kf, kf, ALU.mult),
          f"{g_}_pk")
        rowmm(self.psR2[0:NH, 0:PT], [f"{g_}_pk"], f"{g_}_kk")
        S("vector", [f"{g_}_kk"], lambda v: v.tensor_tensor(prb, qf, kf, ALU.mult),
          f"{g_}_pqk")
        rowmm(self.psR2[0:NH, PT:2 * PT],
              [f"{g_}_pqk", f"{g_}_cs2_{DC - 2}"], f"{g_}_qk")

        def rqk(sc):
            sc.activation(self.rq[:], self.psR[0:NH, PT:2 * PT], AF.Ln)
            sc.activation(self.rq[:], self.rq[:], AF.Exp, scale=np.float32(-0.5))
            sc.activation(self.rk[:], self.psR2[0:NH, 0:PT], AF.Ln)
            return sc.activation(self.rk[:], self.rk[:], AF.Exp, scale=np.float32(-0.5))
        S("scalar", [f"{g_}_kk"], rqk, f"{g_}_rqk")

        def arp(v):
            v.tensor_tensor(self.r1[:], self.psR2[0:NH, PT:2 * PT], self.rq[:], ALU.mult)
            return v.tensor_tensor(self.r1[:], self.r1[:], self.rk[:], ALU.mult)
        S("vector", [f"{g_}_qk", f"{g_}_rqk"], arp, f"{g_}_arp")
        S("tensor", [f"{g_}_arp"], lambda t: t.matmul(
            self.psT[0:1, 0:PT], self.ones4[:], self.r1[:],
            start=True, stop=True), f"{g_}_arm")

        def armask(v):
            am = self.amask[0:1, s * CH:(s + 1) * CH][:, None, :].broadcast_to([1, NCHU, CH])
            return v.tensor_tensor(self.ar.rearrange("o (n c) -> o n c", c=CH),
                                   self.psT[0:1, 0:PT]
                                   .rearrange("o (n c) -> o n c", c=CH), am, ALU.mult)
        S("vector", [f"{g_}_arm"], armask, f"{g_}_ar")


        def mk_bcap(name, rows_ap, deps, fn):
            for hp in range(2):
                def bc(t, hp=hp, rows_ap=rows_ap):
                    for hh in range(2):
                        inst = t.matmul(self.psX[:, hh * PT:(hh + 1) * PT],
                                        self.hsel[0:NH, (hp * 2 + hh) * 128:
                                                  (hp * 2 + hh + 1) * 128],
                                        rows_ap, start=True, stop=True)
                    return inst
                w = list(deps) if hp == 0 else [f"{g_}_{name}ap0"]
                S("tensor", w, bc, f"{g_}_{name}bc{hp}")
                S("vector", [f"{g_}_{name}bc{hp}"], lambda v, hp=hp: fn(v, hp),
                  f"{g_}_{name}ap{hp}")

        def ap_gen(bufs):
            def fn(v, hp):
                for dc in range(4 * hp, 4 * hp + 4):
                    px = self.psX[:, (dc // 2 - 2 * hp) * PT:(dc // 2 - 2 * hp + 1) * PT]
                    for dst, src in bufs:
                        inst = v.tensor_tensor(dst[:, dc, :], src[:, dc, :], px, ALU.mult)
                return inst
            return fn

        mk_bcap("rk", self.rk[:], [f"{g_}_rqk", f"{g_}_pqk"], ap_gen([(kf, kf)]))
        mk_bcap("rq", self.rq[:], [f"{g_}_rkap1"], ap_gen([(qf, qf)]))
        mk_bcap("be", self.be[:], [f"{g_}_rqap1", f"{g_}_be", f"{g_}_cs2_{DC - 2}"],
                ap_gen([(kbf, kf), (vf, vf)]))

        def abm(t):
            ones_row = self.ones4[0:1, 0:1].broadcast_to([1, 128])
            return t.matmul(self.psX[:, 0:PT], ones_row, self.ar[:], start=True, stop=True)
        S("tensor", [f"{g_}_ar", f"{g_}_beap1"], abm, f"{g_}_abm")
        S("vector", [f"{g_}_abm"], lambda v: v.tensor_copy(self.a_b[:], self.psX[:, 0:PT]),
          f"{g_}_ab")

        if dbg_on:
            def dq(g):
                for nm, buf in (("d_q", self.q_f), ("d_k", self.k_f), ("d_v", self.v_f),
                                ("d_kb", self.kb_f)):
                    self.dma(g, None, self.dbg_bufs[nm], buf[:])
                return g.dma_start(out=self.dbg_bufs["d_ab"], in_=self.a_b[:])
            S("gpsimd", [f"{g_}_beap1", f"{g_}_ab"], dq, f"{g_}_dq")

        def tok_tr(g):
            kbt4 = self.kb_t.rearrange("p (g c q) -> p g c q", g=NG, q=128)
            vbt4 = self.h_f.rearrange("p (g c q) -> p g c q", g=NG, q=128)
            for dc in range(DC):
                self.dma(g, None, kbt4[:, :, dc, :], kbf[:, dc, :], transpose=True)
            for dc in range(DC - 1):
                self.dma(g, None, vbt4[:, :, dc, :], vf[:, dc, :], transpose=True)
            return g.dma_start_transpose(vbt4[:, :, DC - 1, :], vf[:, DC - 1, :])
        S("sync", [f"{g_}_beap1"], tok_tr, f"{g_}_tok")

        def grams(t):
            for pr_i in range(NPAIR):
                for dc in range(DC):
                    t.matmul(self.psG[:, pr_i * 128:(pr_i + 1) * 128],
                             kbf[:, dc, pr_i * 128:(pr_i + 1) * 128],
                             kf[:, dc, pr_i * 128:(pr_i + 1) * 128],
                             start=(dc == 0), stop=(dc == DC - 1))
                for dc in range(DC):
                    inst = t.matmul(self.psG[:, 256 + pr_i * 128:256 + (pr_i + 1) * 128],
                                    kf[:, dc, pr_i * 128:(pr_i + 1) * 128],
                                    kbf[:, dc, pr_i * 128:(pr_i + 1) * 128],
                                    start=(dc == 0), stop=(dc == DC - 1))
            return inst
        S("tensor", [f"{g_}_beap1", f"{g_}_ar"], grams, f"{g_}_gr")

        def masks(v):
            mk = lambda i: self.mask3[:, i * 128:(i + 1) * 128][:, None, :].broadcast_to([128, NPAIR, 128])
            g2 = self.psG[:, 0:256].rearrange("p (r q) -> p r q", q=128)
            gt2 = self.psG[:, 256:512].rearrange("p (r q) -> p r q", q=128)
            v.tensor_tensor(self.m_s.rearrange("p (r q) -> p r q", q=128), g2, mk(0),
                            ALU.mult)
            v.tensor_tensor(self.mn_s.rearrange("p (r q) -> p r q", q=128), gt2, mk(2),
                            ALU.mult)
            return v.tensor_tensor(self.mt_s.rearrange("p (r q) -> p r q", q=128), gt2,
                                   mk(1), ALU.mult)
        S("vector", [f"{g_}_gr"], masks, f"{g_}_msk")
        if dbg_on:
            S("gpsimd", [f"{g_}_msk"], lambda g: g.dma_start(
                out=self.dbg_bufs["d_m"], in_=self.m_s[:]), f"{g_}_dm")

        prev_lab = f"{g_}_msk"
        natof = {1: self.m_s, 2: self.pwn[0], 4: self.pwn[1], 8: self.pwn[0],
                 16: self.pwn[1]}
        mtof = {1: self.mt_s}
        for lv in (2, 4, 8, 16, 32):
            mtof[lv] = self.pw[-lv]
        cur = self.m_s
        nap = 0
        for step, lv in (("sq", 2), ("sq", 4), ("ap", 2), ("sq", 8), ("ap", 4),
                         ("sq", 16), ("ap", 8), ("sq", 32), ("ap", 16), ("ap", 32),
                         ("ap", 0)):
            if step == "sq":
                prev, prevt = natof[lv // 2], mtof[lv // 2]
                last = lv == 32

                def pwmm(t, prev=prev, prevt=prevt, last=last):
                    for pr_i in range(NPAIR):
                        sl = slice(pr_i * 128, (pr_i + 1) * 128)
                        if not last:
                            t.matmul(self.psT[:, sl], prevt[:, sl], prev[:, sl],
                                     start=True, stop=True)
                        inst = t.matmul(
                            self.psT[:, 256 + pr_i * 128:256 + (pr_i + 1) * 128],
                            prev[:, sl], prevt[:, sl], start=True, stop=True)
                    return inst
                S("tensor", [prev_lab], pwmm, f"{g_}_pw{lv}")

                def pwe(sc, lv=lv, last=last):
                    if not last:
                        sc.copy(natof[lv][:], self.psT[:, 0:256])
                    return sc.copy(mtof[lv][:], self.psT[:, 256:512])
                S("scalar", [f"{g_}_pw{lv}"], pwe, f"{g_}_pwe{lv}")
                prev_lab = f"{g_}_pwe{lv}"
            else:
                lhs = mtof[lv] if lv else self.mn_s

                def tcm(t, lhs=lhs, cur=cur):
                    for pr_i in range(NPAIR):
                        sl = slice(pr_i * 128, (pr_i + 1) * 128)
                        t.matmul(self.psT[:, sl], lhs[:, sl], cur[:, sl],
                                 start=True, stop=False)
                        inst = t.matmul(self.psT[:, sl], self.eye_f[:], cur[:, sl],
                                        start=False, stop=True)
                    return inst
                S("tensor", [prev_lab], tcm, f"{g_}_tc{nap}")
                S("scalar", [f"{g_}_tc{nap}"],
                  lambda sc: sc.copy(self.tc0[:], self.psT[:, 0:256]),
                  f"{g_}_tce{nap}")
                prev_lab = f"{g_}_tce{nap}"
                cur = self.tc0
                nap += 1

        def ttr(t):
            for pr_i in range(NPAIR):
                inst = t.transpose(self.psT[:, 256 + pr_i * 128:256 + (pr_i + 1) * 128],
                                   self.tc1[:, pr_i * 128:(pr_i + 1) * 128], self.eye_f[:])
            return inst
        S("tensor", [prev_lab], ttr, f"{g_}_ttr")

        def ttev(v):
            return v.scalar_tensor_tensor(
                self.tt_s.rearrange("p (r q) -> p r q", q=128),
                self.psT[:, 256:512].rearrange("p (r q) -> p r q", q=128), -3.0,
                self.eye_b[:, None, :].broadcast_to([128, NPAIR, 128]),
                ALU.mult, ALU.add)
        S("vector", [f"{g_}_ttr"], ttev, f"{g_}_tt")
        if dbg_on:
            S("gpsimd", [f"{g_}_tt"], lambda g: g.dma_start(
                out=self.dbg_bufs["d_tt"], in_=self.tt_s[:]), f"{g_}_dtt")

        kbt3 = self.kb_t.rearrange("p (g d) -> p g d", g=NG)
        vbt3 = self.h_f.rearrange("p (g d) -> p g d", g=NG)
        for ri in range(8):
            ten, dc0 = ri // 4, (ri % 4) * 2
            src3 = kbt3 if ten == 0 else vbt3
            ps = (self.psG, self.psC)[ri % 2]

            def wumm(t, src3=src3, dc0=dc0, ps=ps):
                for dq in range(2):
                    for pr_i in range(NPAIR):
                        inst = t.matmul(
                            ps[:, dq * 256 + pr_i * 128:dq * 256 + (pr_i + 1) * 128],
                            src3[:, pr_i, (dc0 + dq) * 128:(dc0 + dq + 1) * 128],
                            self.tt_s[:, pr_i * 128:(pr_i + 1) * 128],
                            start=True, stop=True)
                return inst
            w = [f"{g_}_tt", f"{g_}_tok"]
            if ri >= 2:
                w.append(f"{g_}_wue{ri - 2}")
            S("tensor", w, wumm, f"{g_}_wu{ri}")
            dstf = wf if ten == 0 else uf
            S("scalar", [f"{g_}_wu{ri}"], lambda sc, dstf=dstf, dc0=dc0, ps=ps: sc.copy(
                dstf[:, dc0:dc0 + 2, :].rearrange("p c t -> p (c t)"), ps[:, 0:512]),
              f"{g_}_wue{ri}")

        if dbg_on:
            def dwu(g):
                self.dma(g, None, self.dbg_bufs["d_w"], self.w_f[:])
                return g.dma_start(out=self.dbg_bufs["d_u"], in_=self.u_f[:])
            S("gpsimd", [f"{g_}_wue7"], dwu, f"{g_}_dwu")

        def p1p2(v):
            v.tensor_tensor(scf, kf, uf, ALU.mult)
            v.tensor_reduce(pp4[:, 0, :, :],
                            self.sc_f.rearrange("p (c n t) -> p c n t", c=DC, n=NCHU),
                            AX.X, ALU.add)
            v.tensor_tensor(scf, kf, wf, ALU.mult)
            v.tensor_reduce(pp4[:, 1, :, :],
                            self.sc_f.rearrange("p (c n t) -> p c n t", c=DC, n=NCHU),
                            AX.X, ALU.add)
            return v.tensor_scalar(pp4[:, 1, :, :], pp4[:, 1, :, :], -1.0, 1.0,
                                   ALU.mult, ALU.add)
        S("vector", [f"{g_}_wue7", f"{g_}_rkap1"], p1p2, f"{g_}_pp")
        if dbg_on:
            S("gpsimd", [f"{g_}_pp"], lambda g: g.dma_start(
                out=self.dbg_bufs["d_pp"], in_=self.pp[:]), f"{g_}_dpp")

        def scan(v):
            for dc in range(DC):
                inst = v.tensor_tensor_scan(sd3[:, dc, 1:NCHU + 1],
                                            pp4[:, 1, dc, :], pp4[:, 0, dc, :],
                                            sd3[:, dc, 0:1], ALU.mult, ALU.add)
            return inst
        S("vector", [f"{g_}_pp"], scan, f"{g_}_scan")

        def opass(v):
            ab3 = self.a_b[:, None, :].broadcast_to([128, DC, PT])
            v.tensor_tensor(scf, ab3, wf, ALU.mult)
            v.tensor_tensor(qf, qf, scf, ALU.subtract)
            v.tensor_tensor(uf, ab3, uf, ALU.mult)
            sdb = sd3[:, :, 0:NCHU][:, :, :, None].broadcast_to([128, DC, NCHU, CH])
            v.tensor_tensor(self.sc_f.rearrange("p (c n t) -> p c n t", c=DC, n=NCHU),
                            qf.rearrange("p c (n t) -> p c n t", n=NCHU), sdb, ALU.mult)
            v.tensor_tensor(uf, scf, uf, ALU.add)
            return v.tensor_copy(sd3[:, :, 0], sd3[:, :, NCHU])
        S("vector", [f"{g_}_scan", f"{g_}_ab", f"{g_}_rqap1"], opass, f"{g_}_o")
        if dbg_on:
            S("gpsimd", [f"{g_}_o"], lambda g: g.dma_start(
                out=self.dbg_bufs["d_o"], in_=self.u_f[:]), f"{g_}_do")

        S("vector", [f"{g_}_o"], lambda v: v.tensor_tensor(prb, uf, uf, ALU.mult),
          f"{g_}_po")
        rowmm(self.psR[0:NH, 0:PT], [f"{g_}_po"], f"{g_}_osq")

        def orstd(sc):
            sc.activation(self.ro[:], self.psR[0:NH, 0:PT], AF.Ln,
                          bias=EPS, scale=np.float32(1.0 / HD))
            return sc.activation(self.ro[:], self.ro[:], AF.Exp, scale=np.float32(-0.5))
        S("scalar", [f"{g_}_osq"], orstd, f"{g_}_ro")

        mk_bcap("ro", self.ro[:], [f"{g_}_ro", f"{g_}_abm"], ap_gen([(uf, uf)]))

        for gi in range(NG):
            for os_i in range(2):
                def wom(t, gi=gi, os_i=os_i):
                    for kc in range(DC):
                        inst = t.matmul(
                            (self.psA, self.psB)[os_i][:, 0:512],
                            uf[:, kc, gi * 128:(gi + 1) * 128],
                            self.wql[:, kc * D + os_i * 512:kc * D + (os_i + 1) * 512],
                            start=(kc == 0), stop=(kc == DC - 1))
                    return inst
                w = [f"{g_}_roap1", f"{g_}_w3"]
                if gi > 0:
                    w.append(f"{g_}_x2_{gi - 1}_{os_i}")
                S("tensor", w, wom, f"{g_}_wo{gi}_{os_i}")

                def x2ev(v, gi=gi, os_i=os_i):
                    ps = (self.psA, self.psB)[os_i]
                    gab = self.modt[:, 4 * D + os_i * 512:4 * D + (os_i + 1) * 512]
                    v.tensor_tensor(self.sc_f[:, 0:512], ps[:, 0:512], gab, ALU.mult)
                    return v.tensor_tensor(x3[:, gi, os_i * 512:(os_i + 1) * 512],
                                           self.sc_f[:, 0:512],
                                           x3[:, gi, os_i * 512:(os_i + 1) * 512],
                                           ALU.add)
                S("vector", [f"{g_}_wo{gi}_{os_i}"], x2ev, f"{g_}_x2_{gi}_{os_i}")

        if dbg_on:
            S("gpsimd", [f"{g_}_x2_{NG - 1}_1"], lambda g: g.dma_start(
                out=self.dbg_bufs["d_x2"], in_=self.x_t[:]), f"{g_}_dx2")

        self.ev[f"{g_}B_lnin"] = self.ev[f"{g_}_x2_{NG - 1}_1"]
        ln(f"{g_}B", 2, 3)

        S("vector", [f"{g_}B_h"], lambda v: self.scales_dve(
            v, self.h_t[:], self.ss_t, NG * D), f"{g_}_h2s")
        S("scalar", [f"{g_}_h2s"], lambda sc: self.recip_act(
            sc, self.rs_t, self.ss_t, NG * D // 16), f"{g_}_h2r")
        def h2lad(v):
            for ch in range(2):
                inst = self.ladder(v, self.h_t[:, ch * 1024:(ch + 1) * 1024],
                                   self.hq_t[:, ch * 1024:(ch + 1) * 1024],
                                   self.ss_t[:, ch * 64:], self.rs_t[:, ch * 64:], 1024)
            return inst
        S("vector", [f"{g_}_h2r"], h2lad, f"{g_}_h2q")
        if dbg_on:
            S("gpsimd", [f"{g_}_h2q"], lambda g: g.dma_start(
                out=self.dbg_bufs["d_h2q"], in_=self.hq_t[:]), f"{g_}_dh2q")

        hq3 = self.hq_t.rearrange("p (g d) -> p g d", g=NG)
        hqf = self.hq_f.rearrange("p (c t) -> p c t", c=DC)
        def hqf_tr(g):
            for gg in range(NG - 1):
                self.dma(g, None, hqf[:, :, gg * 128:(gg + 1) * 128],
                         hq3[:, gg, :].rearrange("p (c q) -> p c q", q=128),
                         transpose=True)
            return g.dma_start_transpose(
                hqf[:, :, (NG - 1) * 128:NG * 128],
                hq3[:, NG - 1, :].rearrange("p (c q) -> p c q", q=128))
        S("sync", [f"{g_}_h2q"], hqf_tr, f"{g_}_hqf")

        h3f = self.h3_f.rearrange("p (c t) -> p c t", c=KC3)
        for gi in range(NG):
            gt = f"{g_}_f{gi}"
            for osi, (o0, ow) in enumerate(OSL):
                nt = ow // 128

                def ldw(sy, o0=o0, ow=ow, nt=nt):
                    w1 = self.wfs[:, 0:DC * ow].rearrange("p (c a q) -> p c a q",
                                                          a=nt, q=128)
                    w2 = self.wfs2[:, 0:DC * ow].rearrange("p (c a q) -> p c a q",
                                                           a=nt, q=128)
                    inst = None
                    for a in range(nt):
                        self.dma(sy, None, w1[:, :, a, :],
                                 self.swg[o0 + a * 128:o0 + (a + 1) * 128, :],
                                 transpose=True)
                        if a < nt - 1:
                            self.dma(sy, None, w2[:, :, a, :],
                                     self.swu[o0 + a * 128:o0 + (a + 1) * 128, :],
                                     transpose=True)
                        else:
                            inst = sy.dma_start_transpose(
                                w2[:, :, a, :],
                                self.swu[o0 + a * 128:o0 + (a + 1) * 128, :])
                    return inst
                w = [f"{g_}_hqf"]
                if osi >= 1:
                    w.append(f"{gt}_gmm{osi - 1}")
                elif gi == 1:
                    w.append(f"{g_}_f0_gmm{len(OSL) - 1}")
                S("sync", w, ldw, f"{gt}_wu{osi}")

                def gumm(t, o0=o0, ow=ow, nt=nt, gi=gi, osi=osi):
                    wv = self.wfs[:, 0:DC * ow].rearrange("p (c a q) -> p c (a q)",
                                                          a=nt, q=128)
                    wv2 = self.wfs2[:, 0:DC * ow].rearrange("p (c a q) -> p c (a q)",
                                                            a=nt, q=128)
                    for kc in range(DC):
                        t.matmul(self.psA[:, 0:ow], hqf[:, kc, gi * 128:(gi + 1) * 128],
                                 wv[:, kc, :], start=(kc == 0), stop=(kc == DC - 1))
                    for kc in range(DC):
                        inst = t.matmul(self.psB[:, 0:ow],
                                        hqf[:, kc, gi * 128:(gi + 1) * 128],
                                        wv2[:, kc, :], start=(kc == 0), stop=(kc == DC - 1))
                    return inst
                w = [f"{gt}_wu{osi}"]
                if osi >= 1:
                    w += [f"{gt}_gu{osi - 1}", f"{gt}_sg{osi - 1}"]
                elif gi == 0:
                    w.append(f"{g_}_x2_{NG - 1}_1")
                else:
                    w.append(f"{g_}_f0_hm")
                S("tensor", w, gumm, f"{gt}_gmm{osi}")
                S("scalar", [f"{gt}_gmm{osi}"], lambda sc, o0=o0, ow=ow: sc.activation(
                    self.sg_t[:, o0:o0 + ow], self.psA[:, 0:ow], AF.Silu), f"{gt}_sg{osi}")
                S("vector", [f"{gt}_gmm{osi}"], lambda v, o0=o0, ow=ow: v.tensor_copy(
                    self.up_t[:, o0:o0 + ow], self.psB[:, 0:ow]), f"{gt}_gu{osi}")

            S("vector", [f"{gt}_gu{len(OSL) - 1}", f"{gt}_sg{len(OSL) - 1}"],
              lambda v: v.tensor_tensor(self.sg_t[:], self.sg_t[:], self.up_t[:],
                                        ALU.mult), f"{gt}_hm")
            if dbg_on and gi == 0:
                S("gpsimd", [f"{gt}_hm"], lambda g: g.dma_start(
                    out=self.dbg_bufs["d_hm"], in_=self.sg_t[:]), f"{gt}_dhm")
            S("vector", [f"{gt}_hm"], lambda v: self.scales_dve(
                v, self.sg_t[:], self.ss_t, I3), f"{gt}_hms")
            S("scalar", [f"{gt}_hms"], lambda sc: self.recip_act(
                sc, self.rs_t, self.ss_t, NB3), f"{gt}_hmr")
            def hmlad(v):
                for f0, F in ((0, 1024), (1024, 1024), (2048, 768)):
                    inst = self.ladder(v, self.sg_t[:, f0:f0 + F],
                                       self.up_t[:, f0:f0 + F],
                                       self.ss_t[:, f0 // 16:], self.rs_t[:, f0 // 16:],
                                       F)
                return inst
            S("vector", [f"{gt}_hmr"], hmlad, f"{gt}_hmq")
            if dbg_on and gi == 0:
                S("gpsimd", [f"{gt}_hmq"], lambda g: g.dma_start(
                    out=self.dbg_bufs["d_hmq"], in_=self.up_t[:]), f"{gt}_dhmq")

            w = [f"{gt}_hmq"] + ([f"{g_}_f0_dmm3"] if gi == 1 else [])
            S("sync", w, lambda g: g.dma_start_transpose(
                h3f, self.up_t.rearrange("p (c q) -> p c q", q=128)), f"{gt}_h3f")

            for osi in range(4):
                def ldwd(sy, osi=osi):
                    wv = self.wds[:, 0:KC3 * 256].rearrange("p (c a q) -> p c a q",
                                                            a=2, q=128)
                    self.dma(sy, None, wv[:, :, 0, :],
                             self.swd[osi * 256:osi * 256 + 128, :], transpose=True)
                    return sy.dma_start_transpose(
                        wv[:, :, 1, :], self.swd[osi * 256 + 128:(osi + 1) * 256, :])
                w = [f"{gt}_h3f"]
                if osi > 0:
                    w.append(f"{gt}_dmm{osi - 1}")
                elif gi > 0:
                    w.append(f"{g_}_f0_dmm3")
                S("sync", w, ldwd, f"{gt}_wd{osi}")

                def dnm(t, osi=osi):
                    wv = self.wds[:, 0:KC3 * 256].rearrange("p (c a q) -> p c (a q)",
                                                            a=2, q=128)
                    for kc in range(KC3):
                        inst = t.matmul(self.psC[:, 0:256], h3f[:, kc, :], wv[:, kc, :],
                                        start=(kc == 0), stop=(kc == KC3 - 1))
                    return inst
                w = [f"{gt}_wd{osi}"]
                if osi >= 1:
                    w.append(f"{gt}_dn{osi - 1}")
                elif gi == 0:
                    w.append(f"{g_}_wue7")
                else:
                    w.append(f"{g_}_f0_dn3")
                S("tensor", w, dnm, f"{gt}_dmm{osi}")

                def outev(v, osi=osi, gi=gi):
                    gbb = self.modt[:, 5 * D + osi * 256:5 * D + (osi + 1) * 256]
                    v.tensor_tensor(self.sc_f[:, 0:256], self.psC[:, 0:256], gbb, ALU.mult)
                    gq, rr = divmod(osi * 256, 1024)
                    return v.tensor_tensor(x3[:, gi, osi * 256:(osi + 1) * 256],
                                           self.sc_f[:, 0:256],
                                           x3[:, gi, osi * 256:(osi + 1) * 256], ALU.add)
                S("vector", [f"{gt}_dmm{osi}"], outev, f"{gt}_dn{osi}")

        S("gpsimd", [f"{g_}_f0_dn3", f"{g_}_f1_dn3"], lambda g: g.dma_start(
            out=self.dout[s, t0:t0 + PT, :].rearrange("(g p) d -> p g d", p=128),
            in_=x3), f"{g_}_out")

    def build(self):
        with contextlib.ExitStack() as es:
            self.declare()
            self.alloc(es)
            self.p0()
            for s in range(self.nseq):
                self.seq_setup(s)
                for pi in range(self.npiece):
                    self.piece(s, pi)
            fin = [self.cur(s) for s in
                   ("vv", "aa", "tt", *SDPOOL, *GDPOOL)]
            pr = self

            def final(sy):
                pr.wait(sy, *fin)
            self.block.sync(final)
        return self.nc


def _consts():
    eye = np.eye(128, dtype=np.float32)
    m_tril = np.zeros((128, 128), np.float32)
    m_triu = np.zeros((128, 128), np.float32)
    for blk in range(2):
        for i in range(64):
            jj = np.arange(64)
            m_tril[blk * 64 + i, blk * 64 + jj[jj < i]] = 1.0
            m_triu[blk * 64 + i, blk * 64 + jj[jj > i]] = 1.0
    mask3 = np.stack([m_tril, m_triu, -m_triu])
    hsel = np.zeros((NH, NH * 128), np.float32)
    for h in range(NH):
        hsel[h, h * 128:(h + 1) * 128] = 1.0
    colsel = np.zeros((128, NH * NH), np.float32)
    for h in range(NH):
        colsel[:, h * NH + h] = 1.0
    ones4 = np.ones((NH, 1), np.float32)
    return eye, mask3, hsel, colsel, ones4


def _diag_const(conv_q, conv_k, conv_v):
    cd = np.zeros((12, DC, 128, 128), np.float32)
    for ti, cw in enumerate((conv_q, conv_k, conv_v)):
        w = np.asarray(cw, np.float32)[:, 0, :]
        for tau in range(4):
            for dc in range(DC):
                np.fill_diagonal(cd[ti * 4 + tau, dc], w[dc * 128:(dc + 1) * 128, tau])
    return cd


_NC_CACHE = {}


def _in_maps(L0_attn_x, L0_attn_Wq, L0_attn_Wk, L0_attn_Wv, L0_attn_Wb,
             L0_attn_conv_q, L0_attn_conv_k, L0_attn_conv_v, L0_attn_g,
             L0_attn_Wo, L0_ffn_w_gate, L0_ffn_w_up, L0_ffn_w_down,
             L0_c, L0_wmod):
    x = np.ascontiguousarray(np.asarray(L0_attn_x, np.float32))
    eye, mask3, hsel, colsel, ones4 = _consts()
    cdiag = _diag_const(L0_attn_conv_q, L0_attn_conv_k, L0_attn_conv_v)
    g = np.asarray(L0_attn_g, np.float32)
    gcol = np.ascontiguousarray(g.reshape(2, 128).T)
    base = {
        "wmod": np.ascontiguousarray(np.asarray(L0_wmod, np.float32)),
        "Wq": np.ascontiguousarray(np.asarray(L0_attn_Wq, np.float32)),
        "Wk": np.ascontiguousarray(np.asarray(L0_attn_Wk, np.float32)),
        "Wv": np.ascontiguousarray(np.asarray(L0_attn_Wv, np.float32)),
        "Wo": np.ascontiguousarray(np.asarray(L0_attn_Wo, np.float32)),
        "Wb": np.ascontiguousarray(np.asarray(L0_attn_Wb, np.float32)),
        "wg": np.ascontiguousarray(np.asarray(L0_ffn_w_gate, np.float32)),
        "wu": np.ascontiguousarray(np.asarray(L0_ffn_w_up, np.float32)),
        "wd": np.ascontiguousarray(np.asarray(L0_ffn_w_down, np.float32)),
        "cdiag": cdiag, "gcol": gcol, "eye": eye,
        "eyeb": eye.astype(BF), "mask3": mask3.astype(BF),
        "hsel": hsel, "colsel": colsel.astype(BF),
        "ones4": ones4,
    }
    cc = np.asarray(L0_c, np.float32)
    in_maps = []
    for ci in range(NCORE):
        m = dict(base)
        m["x"] = np.ascontiguousarray(x[ci * BPC:(ci + 1) * BPC])
        m["cc"] = np.ascontiguousarray(cc[ci * BPC:(ci + 1) * BPC])
        am = np.zeros((1, BPC * CH), np.float32)
        for sloc in range(BPC):
            gb = ci * BPC + sloc
            am[0, sloc * CH:sloc * CH + gb + 1] = 1.0
        m["amask"] = am
        in_maps.append(m)
    return in_maps


def kernel(L0_attn_x, L0_attn_Wq, L0_attn_Wk, L0_attn_Wv, L0_attn_Wb,
           L0_attn_conv_q, L0_attn_conv_k, L0_attn_conv_v, L0_attn_g, L0_attn_Wo,
           L0_ffn_w_gate, L0_ffn_w_up, L0_ffn_w_down, L0_c, L0_wmod):
    in_maps = _in_maps(L0_attn_x, L0_attn_Wq, L0_attn_Wk, L0_attn_Wv,
                       L0_attn_Wb, L0_attn_conv_q, L0_attn_conv_k,
                       L0_attn_conv_v, L0_attn_g, L0_attn_Wo, L0_ffn_w_gate,
                       L0_ffn_w_up, L0_ffn_w_down, L0_c, L0_wmod)
    if "nc" not in _NC_CACHE:
        _NC_CACHE["nc"] = Bld().build()
    nc = _NC_CACHE["nc"]
    res = run_bass_kernel_spmd(nc, in_maps, core_ids=list(range(NCORE)))
    return np.concatenate([r["out"] for r in res.results], axis=0)


# version 06 (final)
# speedup vs baseline: 1.2343x; 1.1762x over previous
import contextlib
import numpy as np
import ml_dtypes
import concourse.bass as bass
import concourse.mybir as mybir
from concourse.bass_utils import run_bass_kernel_spmd

F32 = mybir.dt.float32
BF16 = mybir.dt.bfloat16
AF = mybir.ActivationFunctionType
ALU = mybir.AluOpType
AX = mybir.AxisListType
BF = ml_dtypes.bfloat16

B, L, D = 32, 2048, 1024
NCORE = 8
BPC = B // NCORE
NH, HD, CH = 4, 256, 64
I3 = 2816
PT = 256
NG = PT // 128
NCHU = PT // CH
NPAIR = 2
NPIECE = L // PT
DC = D // 128
KC3 = I3 // 128
NB3 = I3 // 16
OSL = [(i * 256, 256) for i in range(11)]
EPS = np.float32(1e-5)
NQPOOL = 8
SDPOOL = [f"sd{i}" for i in range(NQPOOL)]
GDPOOL = [f"gd{i}" for i in range(NQPOOL)]
QSEM = {"vector": "vv", "scalar": "aa", "tensor": "tt"}


class Bld:
    def __init__(self, dbg=False, nseq=BPC, npiece=NPIECE, serial=True,
                 window=1):
        self.nc = bass.Bass("TRN2", target_bir_lowering=False,
                            detect_race_conditions=False)
        self.dbg = dbg
        self.nseq = nseq
        self.npiece = npiece
        self.serial = serial
        self.window = window
        self.snaps = []
        self.cnt = {}
        self.ev = {}
        self.hand = {}
        self.dbg_bufs = {}
        self.rr_sd = 0
        self.rr_gd = 0
        self.cursem = None
        self.watermark = {}

    def mark(self, inst, sem, label=None, step=1):
        self.cnt[sem] += step
        inst.then_inc(self.hand[sem], step)
        if label:
            self.ev[label] = (sem, self.cnt[sem])
        return (sem, self.cnt[sem])

    def cur(self, sem):
        return (sem, self.cnt[sem])

    def wait(self, eng, *evs):
        for e in evs:
            sem, val = self.ev[e] if isinstance(e, str) else e
            if val > 0:
                eng.wait_ge(self.hand[sem], val)

    def barrier_evs(self):
        return [self.cur(s) for s in
                ("vv", "aa", "tt", *SDPOOL, *GDPOOL)]

    _GUARD_OPS = frozenset((
        "tensor_tensor", "tensor_scalar", "tensor_copy", "tensor_reduce",
        "memset", "scalar_tensor_tensor", "tensor_tensor_scan", "activation",
        "copy", "tensor_add", "tensor_mul", "tensor_sub", "tensor_max",
        "select", "reciprocal", "iota"))

    class _Guard:
        def __init__(self, eng, bld, sem):
            self._eng = eng
            self._bld = bld
            self._sem = sem
            self._last = None

        def __getattr__(self, name):
            attr = getattr(self._eng, name)
            if name in Bld._GUARD_OPS:
                def wrap(*args, **kw):
                    inst = attr(*args, **kw)
                    out_ap = args[0]
                    nfree = 1
                    try:
                        for s in out_ap.shape[1:]:
                            nfree *= int(s)
                    except Exception:
                        nfree = 0
                    if nfree < 256:
                        b = self._bld
                        b.cnt[self._sem] += 1
                        inst.then_inc(b.hand[self._sem], 1)
                        self._eng.wait_ge(b.hand[self._sem], b.cnt[self._sem])
                        self._last = (inst, True)
                    else:
                        self._last = (inst, False)
                    return inst
                return wrap
            return attr

    def sec(self, engname, waits, fn, label=None, step=None):
        if engname == "sync":
            sem = SDPOOL[self.rr_sd % NQPOOL]
            self.rr_sd += 1
            step = 16
        elif engname == "gpsimd":
            sem = GDPOOL[self.rr_gd % NQPOOL]
            self.rr_gd += 1
            step = 16
        else:
            sem = QSEM[engname]
            if step is None:
                step = 1
        pr = self

        def body(eng):
            pr.cursem = sem
            if pr.serial:
                if pr.window == 0:
                    if engname in ("vector", "scalar", "tensor") and pr.snaps:
                        tgt = pr.snaps[-1]
                        wm = pr.watermark.setdefault(engname, {})
                        for s2 in ("vv", "aa", "tt"):
                            c2 = tgt.get(s2, 0)
                            if c2 > wm.get(s2, 0):
                                eng.wait_ge(pr.hand[s2], c2)
                                wm[s2] = c2
                elif len(pr.snaps) >= pr.window:
                    tgt = pr.snaps[len(pr.snaps) - pr.window]
                    wm = pr.watermark.setdefault(engname, {})
                    for s2 in sorted(tgt):
                        c2 = tgt[s2]
                        if c2 > wm.get(s2, 0):
                            eng.wait_ge(pr.hand[s2], c2)
                            wm[s2] = c2
            pr.wait(eng, *waits)
            if engname in ("vector", "scalar"):
                gd = Bld._Guard(eng, pr, sem)
                fn(gd)
                if gd._last is not None and not gd._last[1]:
                    pr.cnt[sem] += 1
                    gd._last[0].then_inc(pr.hand[sem], 1)
                if label is not None:
                    pr.ev[label] = (sem, pr.cnt[sem])
            else:
                inst = fn(eng)
                if label is not None:
                    pr.mark(inst, sem, label, step)
            pr.snaps.append(dict(pr.cnt))
        getattr(self.block, engname)(body)

    def dma(self, eng, qsem_ignored, out, in_, transpose=False, label=None):
        if transpose:
            inst = eng.dma_start_transpose(out, in_)
        else:
            inst = eng.dma_start(out=out, in_=in_)
        return self.mark(inst, self.cursem, label, 16)

    def declare(self):
        dram = self.nc.dram_tensor
        self.dx = dram("x", [BPC, L, D], F32, kind="ExternalInput").ap()
        self.dout = dram("out", [BPC, L, D], F32, kind="ExternalOutput").ap()
        self.dcc = dram("cc", [BPC, D], F32, kind="ExternalInput").ap()
        self.dwmod = dram("wmod", [D, 6 * D], F32, kind="ExternalInput").ap()
        self.dW = [dram(n, [D, D], F32, kind="ExternalInput").ap()
                   for n in ("Wq", "Wk", "Wv", "Wo")]
        self.dWb = dram("Wb", [D, NH], F32, kind="ExternalInput").ap()
        self.dwg = dram("wg", [I3, D], F32, kind="ExternalInput").ap()
        self.dwu = dram("wu", [I3, D], F32, kind="ExternalInput").ap()
        self.dwd = dram("wd", [D, I3], F32, kind="ExternalInput").ap()
        self.dcdiag = dram("cdiag", [12, DC, 128, 128], F32, kind="ExternalInput").ap()
        self.dgcol = dram("gcol", [128, 2], F32, kind="ExternalInput").ap()
        self.deye = dram("eye", [128, 128], F32, kind="ExternalInput").ap()
        self.deyeb = dram("eyeb", [128, 128], BF16, kind="ExternalInput").ap()
        self.dmask3 = dram("mask3", [3, 128, 128], BF16, kind="ExternalInput").ap()
        self.dhsel = dram("hsel", [NH, NH * 128], F32, kind="ExternalInput").ap()
        self.dcolsel = dram("colsel", [128, NH * NH], BF16, kind="ExternalInput").ap()
        self.dones4 = dram("ones4", [NH, 1], F32, kind="ExternalInput").ap()
        self.damask = dram("amask", [1, BPC * CH], F32, kind="ExternalInput").ap()
        self.swattn = dram("swattn", [4, D, D], BF16, kind="Internal").ap()
        self.swg = dram("swgq", [I3, D], BF16, kind="Internal").ap()
        self.swu = dram("swuq", [I3, D], BF16, kind="Internal").ap()
        self.swd = dram("swdq", [D, I3], BF16, kind="Internal").ap()
        self.smod = dram("smod", [BPC, 6, D], F32, kind="Internal").ap()
        if self.dbg:
            for n, sh, dt in [("d_h", [128, NG * D], BF16), ("d_hf", [128, DC * PT], BF16),
                              ("d_q", [128, DC * PT], BF16), ("d_k", [128, DC * PT], BF16),
                              ("d_v", [128, DC * PT], BF16), ("d_kb", [128, DC * PT], BF16),
                              ("d_m", [128, NPAIR * 128], F32),
                              ("d_tt", [128, NPAIR * 128], BF16),
                              ("d_w", [128, DC * PT], BF16), ("d_u", [128, DC * PT], BF16),
                              ("d_pp", [128, 2 * DC * NCHU], F32),
                              ("d_o", [128, DC * PT], BF16),
                              ("d_x2", [128, NG * D], F32), ("d_h2q", [128, NG * D], BF16),
                              ("d_hm", [128, I3], BF16), ("d_hmq", [128, I3], BF16),
                              ("d_ab", [128, PT], BF16),
                              ("d_mod", [128, BPC * 6 * DC], F32)]:
                self.dbg_bufs[n] = dram(n, sh, dt, kind="ExternalOutput").ap()

    def alloc(self, es):
        nc = self.nc
        T_ = lambda n, sh, dt: es.enter_context(nc.sbuf_tensor(n, sh, dt))
        self.eye_b = T_("eye_b", [128, 128], BF16)
        self.eye_f = T_("eye_f", [128, 128], F32)
        self.mask3 = T_("mask3_s", [128, 3 * 128], BF16)
        self.hsel = T_("hsel_s", [NH, NH * 128], F32)
        self.colsel = T_("colsel_s", [128, NH * NH], BF16)
        self.ones4 = T_("ones4_s", [NH, 1], F32)
        self.amask = T_("amask_s", [1, BPC * CH], F32)
        self.gcol = T_("gcol_s", [128, 2], F32)
        self.diag = T_("diag", [128, 12 * DC * 128], BF16)
        self.wb_sb = T_("wb_sb", [128, DC * NH], BF16)
        self.cT = T_("cT", [128, DC * BPC], BF16)
        self.modc = T_("modc", [128, BPC * 6 * DC], F32)
        self.modt = T_("modt", [128, 6 * D], BF16)

        self.wql = T_("wql", [128, DC * D], BF16)
        self.wfs = T_("wfs", [128, DC * 256], BF16)
        self.wfs2 = T_("wfs2", [128, DC * 256], BF16)
        self.wds = T_("wds", [128, KC3 * 256], BF16)
        self.x_t = T_("x_t", [128, NG * D], F32)
        self.sq_t = T_("sq_t", [128, D], F32)
        self.st = T_("st", [128, 32], F32)
        self.h_t = T_("h_t", [128, NG * D], BF16)
        self.h_f = T_("h_f", [128, DC * PT], BF16)
        self.p_f = T_("p_f", [128, DC * (PT + 3)], BF16)
        self.ptail = T_("ptail", [128, 3 * DC * 3], BF16)
        self.q_f = T_("q_f", [128, DC * PT], BF16)
        self.k_f = T_("k_f", [128, DC * PT], BF16)
        self.kb_f = T_("kb_f", [128, DC * PT], BF16)
        self.v_f = T_("v_f", [128, DC * PT], BF16)
        self.kb_t = T_("kb_t", [128, NG * D], BF16)
        self.w_f = T_("w_f", [128, DC * PT], BF16)
        self.u_f = T_("u_f", [128, DC * PT], BF16)
        self.sc_f = T_("sc_f", [128, DC * PT], F32)
        self.stgA = self.x_t
        self.stgB = self.sc_f
        self.a_b = T_("a_b", [128, PT], BF16)
        self.m_s = T_("m_s", [128, NPAIR * 128], F32)
        self.mt_s = T_("mt_s", [128, NPAIR * 128], F32)
        self.mn_s = T_("mn_s", [128, NPAIR * 128], F32)
        self.pwn = [T_("pwn0", [128, NPAIR * 128], F32),
                    T_("pwn1", [128, NPAIR * 128], F32)]
        self.pwt = [T_(f"pwt{i}", [128, NPAIR * 128], F32) for i in range(3)]
        self.pw = {-2: self.pwt[0], -4: self.pwt[1], -8: self.pwt[2],
                   -16: self.pwt[0], -32: self.pwt[1]}
        self.tc0 = T_("tc0", [128, NPAIR * 128], F32)
        self.tc1 = self.tc0
        self.tt_s = T_("tt_s", [128, NPAIR * 128], BF16)
        self.rq = T_("rq", [NH, PT], F32)
        self.rk = T_("rk", [NH, PT], F32)
        self.be = T_("be", [NH, PT], F32)
        self.ro = T_("ro", [NH, PT], F32)
        self.r1 = T_("r1", [NH, PT], F32)
        self.ar = T_("ar", [1, PT], F32)
        self.pp = T_("pp", [128, 2 * DC * NCHU], F32)
        self.sd = T_("sd", [128, DC * (NCHU + 1)], F32)
        self.sg_t = T_("sg_t", [128, I3], BF16)
        self.up_t = T_("up_t", [128, I3], BF16)
        self.h3_f = T_("h3_f", [128, KC3 * 128], BF16)
        self.hq_t = self.h_t
        self.hq_f = self.k_f
        self.ss_t = T_("ss_t", [128, NB3], F32)
        self.rs_t = T_("rs_t", [128, NB3], F32)
        self.qa_s = T_("qa_s", [128, I3], BF16)
        self.qc_s = self.kb_f[:, 0:1024]
        self.qh_s = self.kb_f[:, 1024:2048]
        P_ = lambda n, sh: es.enter_context(nc.psum_tensor(n, sh, F32))
        self.psA = P_("psA", [128, 512])
        self.psB = P_("psB", [128, 512])
        self.psC = P_("psC", [128, 512])
        self.psG = P_("psG", [128, 512])
        self.psT = P_("psT", [128, 512])
        self.psX = P_("psX", [128, 512])
        self.psR = P_("psR", [16, 512])
        self.psR2 = P_("psR2", [16, 512])
        for s in ("vv", "aa", "tt", *SDPOOL, *GDPOOL):
            self.hand[s] = es.enter_context(nc.semaphore(s))
            self.cnt[s] = 0
        self.block = es.enter_context(nc.Block())

    def ladder(self, v, X, out, ss, rs, FD):
        nb = FD // 16
        qa, qc, qh = self.qa_s[:, 0:FD], self.qc_s[:, 0:FD], self.qh_s[:, 0:FD]
        x3 = X.rearrange("p (n b) -> p n b", b=16)
        rs3 = rs[:, 0:nb, None].broadcast_to([128, nb, 16])
        ss3 = ss[:, 0:nb, None].broadcast_to([128, nb, 16])
        v.tensor_tensor(qa.rearrange("p (n b) -> p n b", b=16), x3, rs3, ALU.mult)
        v.scalar_tensor_tensor(qa, qa, -1.0, qa, ALU.mult, ALU.max)
        v.tensor_scalar(qc, qa, 0.25, None, ALU.is_gt)
        v.scalar_tensor_tensor(qc, qa, 0.75, qc, ALU.is_gt, ALU.add)
        v.scalar_tensor_tensor(qc, qa, 1.25, qc, ALU.is_gt, ALU.add)
        v.scalar_tensor_tensor(qc, qa, 1.75, qc, ALU.is_gt, ALU.add)
        v.tensor_scalar(qh, qa, 2.5, None, ALU.is_gt)
        v.scalar_tensor_tensor(qh, qa, 3.5, qh, ALU.is_gt, ALU.add)
        v.scalar_tensor_tensor(qh, qa, 5.0, qh, ALU.is_gt, ALU.add)
        v.scalar_tensor_tensor(qh, qa, 5.0, qh, ALU.is_gt, ALU.add)
        v.scalar_tensor_tensor(qh, qc, 0.5, qh, ALU.mult, ALU.add)
        v.tensor_scalar(qc, X, 0.0, None, ALU.is_lt)
        v.scalar_tensor_tensor(qc, X, 0.0, qc, ALU.is_gt, ALU.subtract)
        v.tensor_tensor(qh, qh, qc, ALU.mult)
        return v.tensor_tensor(out.rearrange("p (n b) -> p n b", b=16),
                               qh.rearrange("p (n b) -> p n b", b=16), ss3, ALU.mult)

    def scales_dve(self, v, X, ss, FD):
        nb = FD // 16
        v.tensor_reduce(ss[:, 0:nb], X.rearrange("p (n b) -> p n b", b=16),
                        AX.X, ALU.max, apply_absolute_value=True)
        return v.tensor_scalar(ss[:, 0:nb], ss[:, 0:nb], 1.0 / 6.0, 1e-12,
                               ALU.mult, ALU.max)

    def recip_act(self, sc, rs, ss, nb):
        sc.activation(rs[:, 0:nb], ss[:, 0:nb], AF.Ln)
        return sc.activation(rs[:, 0:nb], rs[:, 0:nb], AF.Exp, scale=np.float32(-1.0))

    def p0(self):
        S = self.sec

        def lds(sy):
            for dst, src in ((self.eye_f[:], self.deye), (self.eye_b[:], self.deyeb),
                             (self.mask3.rearrange("p (i q) -> p i q", q=128),
                              self.dmask3.rearrange("i p q -> p i q")),
                             (self.hsel[:], self.dhsel), (self.colsel[:], self.dcolsel),
                             (self.ones4[:], self.dones4), (self.amask[:], self.damask),
                             (self.gcol[:], self.dgcol)):
                self.dma(sy, None, dst, src)
            return sy.dma_start(out=self.stgA[0:BPC, 0:D], in_=self.dcc)
        S("sync", [], lds, "c0")
        S("scalar", ["c0"], lambda sc: sc.activation(
            self.stgA[0:BPC, 0:D], self.stgA[0:BPC, 0:D], AF.Silu), "c2")
        for dc in range(DC):
            w = ["c2"] if dc == 0 else [f"ctv{dc - 1}"]
            S("tensor", w, lambda t, dc=dc: t.transpose(
                self.psT[:, 0:BPC], self.stgA[0:BPC, dc * 128:(dc + 1) * 128],
                self.eye_f[0:BPC, 0:BPC]), f"ctr{dc}")
            S("vector", [f"ctr{dc}"], lambda v, dc=dc: v.tensor_copy(
                self.cT[:, dc * BPC:(dc + 1) * BPC], self.psT[:, 0:BPC]), f"ctv{dc}")

        for i in range(32):
            wi, kc = divmod(i, DC)
            w = [f"wcv{i - 2}"] if i >= 2 else [f"ctv{DC - 1}"]
            S("sync", w, lambda sy, wi=wi, i=i, kc=kc: sy.dma_start(
                out=(self.stgA if i % 2 == 0 else self.stgB)[:, 0:D],
                in_=self.dW[wi][kc * 128:(kc + 1) * 128, :]), f"wld{i}")

            def cast_w(v, wi=wi, kc=kc, i=i):
                if i >= 4:
                    self.wait(v, f"wst{i - 4}")
                src = (self.stgA if i % 2 == 0 else self.stgB)[:, 0:D]
                dst = self.wql[:, (i % 4) * 1024:(i % 4) * 1024 + D]
                if wi == 3:
                    return v.tensor_scalar(dst, src, self.gcol[:, kc % 2:kc % 2 + 1],
                                           None, ALU.mult)
                return v.tensor_copy(dst, src)
            S("vector", [f"wld{i}"], cast_w, f"wcv{i}")
            S("gpsimd", [f"wcv{i}"], lambda g, wi=wi, kc=kc, i=i: g.dma_start(
                out=self.swattn[wi, kc * 128:(kc + 1) * 128, :],
                in_=self.wql[:, (i % 4) * 1024:(i % 4) * 1024 + D]), f"wst{i}")

        for i in range(12):
            w = [f"dgc{i - 2}"] if i >= 2 else ["wcv31"]
            S("sync", w, lambda sy, i=i: sy.dma_start(
                out=(self.stgA if i % 2 == 0 else self.stgB)[:, 0:DC * 128]
                .rearrange("p (c q) -> p c q", q=128),
                in_=self.dcdiag[i].rearrange("c p q -> p c q")), f"dgl{i}")
            S("vector", [f"dgl{i}"], lambda v, i=i: v.tensor_copy(
                self.diag[:, i * DC * 128:(i + 1) * DC * 128],
                (self.stgA if i % 2 == 0 else self.stgB)[:, 0:DC * 128]), f"dgc{i}")

        S("sync", ["dgc11"], lambda sy: sy.dma_start(
            out=self.stgA[:, 0:DC * NH].rearrange("p (c h) -> p c h", c=DC),
            in_=self.dWb.rearrange("(c p) h -> p c h", p=128)), "wbl")
        S("vector", ["wbl"], lambda v: v.tensor_copy(self.wb_sb[:], self.stgA[:, 0:DC * NH]),
          "wbc")

        for idx in range(32):
            kc, sl = divmod(idx, 4)
            w = [f"mmt{idx - 2}"] if idx >= 2 else ["wbc"]
            S("sync", w, lambda sy, kc=kc, sl=sl, idx=idx: sy.dma_start(
                out=(self.stgA if idx % 2 == 0 else self.stgB)[:, 0:1536],
                in_=self.dwmod[kc * 128:(kc + 1) * 128, sl * 1536:(sl + 1) * 1536]),
              f"mml{idx}")
            w2 = [f"mml{idx}"]
            if idx < 4:
                w2 += [f"wst{j}" for j in range(28, 32)]
            S("vector", w2, lambda v, idx=idx: v.tensor_copy(
                self.wql[:, (idx % 2) * 1536:(idx % 2) * 1536 + 1536],
                (self.stgA if idx % 2 == 0 else self.stgB)[:, 0:1536]), f"mmc{idx}")

            def modmm(t, kc=kc, sl=sl, idx=idx):
                if idx >= 2:
                    self.wait(t, f"mme{idx - 2}")
                for ob in range(12):
                    oc = sl * 12 + ob
                    inst = t.matmul(self.psC[:, oc * NH:(oc + 1) * NH],
                                    self.wql[:, (idx % 2) * 1536 + ob * 128:
                                             (idx % 2) * 1536 + (ob + 1) * 128],
                                    self.cT[:, kc * BPC:(kc + 1) * BPC],
                                    start=True, stop=True)
                return inst
            S("tensor", [f"mmc{idx}"], modmm, f"mmt{idx}")

            def modacc(v, kc=kc, sl=sl, idx=idx):
                dst = self.modc[:, sl * 48:(sl + 1) * 48]
                srcp = self.psC[:, sl * 48:(sl + 1) * 48]
                if kc == 0:
                    return v.tensor_copy(dst, srcp)
                return v.tensor_tensor(dst, dst, srcp, ALU.add)
            S("vector", [f"mmt{idx}"], modacc, f"mme{idx}")

        def modev(v):
            v.tensor_copy(self.stgB[:, 0:192], self.modc[:, 0:192])
            v.tensor_copy(
                self.modc.rearrange("p (b m o) -> p b m o", b=BPC, m=6, o=DC),
                self.stgB[:, 0:192].rearrange("p (m o b) -> p b m o", m=6, o=DC, b=BPC))
            mc = self.modc.rearrange("p (b m o) -> p b m o", b=BPC, m=6, o=DC)
            for b in range(BPC):
                v.tensor_scalar(mc[:, b, 1, :], mc[:, b, 1, :], 1.0, None, ALU.add)
                v.tensor_scalar(mc[:, b, 4, :], mc[:, b, 4, :], 1.0, None, ALU.add)
                for pa in (2, 5):
                    gsl = mc[:, b, pa, :]
                    s1, s2 = self.st[:, 0:DC], self.st[:, DC:2 * DC]
                    v.tensor_scalar(s1, gsl, 0.0, 2.0, ALU.is_ge, ALU.mult)
                    v.tensor_scalar(s1, s1, -1.0, None, ALU.add)
                    v.scalar_tensor_tensor(s2, gsl, -1.0, gsl, ALU.mult, ALU.max)
                    v.tensor_scalar(s2, s2, 0.25, None, ALU.max)
                    inst = v.tensor_tensor(gsl, s1, s2, ALU.mult)
            return inst
        S("vector", ["mme31"], modev, "modr")
        if self.dbg:
            S("gpsimd", ["modr"], lambda g: g.dma_start(
                out=self.dbg_bufs["d_mod"], in_=self.modc[:]), "dmod")
        for b in range(BPC):
            w = ["modr"] if b == 0 else [f"modev{b - 1}"]
            S("tensor", w, lambda t, b=b: t.transpose(
                self.psT[0:48, 0:128], self.modc[:, b * 48:(b + 1) * 48], self.eye_f[:]),
              f"modtr{b}")

            def mev(v, b=b):
                if b > 0:
                    self.wait(v, f"modst{b - 1}")
                return v.tensor_copy(self.stgA[0:48, 0:128], self.psT[0:48, 0:128])
            S("vector", [f"modtr{b}"], mev, f"modev{b}")
            S("gpsimd", [f"modev{b}"], lambda g, b=b: g.dma_start(
                out=self.smod[b].rearrange("m (o q) -> (m o) q", q=128),
                in_=self.stgA[0:48, 0:128]), f"modst{b}")

        specs = [(self.dwg, self.swg, I3, D, [(0, 1024)]),
                 (self.dwu, self.swu, I3, D, [(0, 1024)]),
                 (self.dwd, self.swd, D, I3, [(0, 1024), (1024, 1024), (2048, 768)])]
        n = 0
        for dsrc, dscr, R, FT, chunks in specs:
            for rc in range(R // 128):
                for f0, F in chunks:
                    nb = F // 16
                    w = [f"fqv{n - 2}"] if n >= 2 else [f"modst{BPC - 1}"]
                    S("sync", w, lambda sy, dsrc=dsrc, rc=rc, n=n, f0=f0, F=F: sy.dma_start(
                        out=(self.stgA if n % 2 == 0 else self.stgB)[:, 0:F],
                        in_=dsrc[rc * 128:(rc + 1) * 128, f0:f0 + F]), f"fql{n}")

                    def fq_pre(v, n=n, F=F):
                        if n >= 1:
                            self.wait(v, f"fqw{n - 1}")
                        v.tensor_copy(self.sg_t[:, 0:F],
                                      (self.stgA if n % 2 == 0 else self.stgB)[:, 0:F])
                        return self.scales_dve(v, self.sg_t[:, 0:F], self.ss_t, F)
                    S("vector", [f"fql{n}"], fq_pre, f"fqs{n}")
                    S("scalar", [f"fqs{n}"], lambda sc, nb=nb: self.recip_act(
                        sc, self.rs_t, self.ss_t, nb), f"fqr{n}")
                    S("vector", [f"fqr{n}"], lambda v, F=F: self.ladder(
                        v, self.sg_t[:, 0:F], self.up_t[:, 0:F], self.ss_t, self.rs_t, F),
                      f"fqv{n}")
                    S("gpsimd", [f"fqv{n}"], lambda g, dscr=dscr, rc=rc, f0=f0, F=F:
                      g.dma_start(out=dscr[rc * 128:(rc + 1) * 128, f0:f0 + F],
                                  in_=self.up_t[:, 0:F]), f"fqw{n}")
                    n += 1

    def seq_setup(self, s):
        S = self.sec
        bar = self.barrier_evs()

        def mb(gq):
            inst = None
            for slot, pa in enumerate((1, 0, 4, 3, 2, 5)):
                if inst is not None:
                    self.mark(inst, self.cursem, None, 16)
                inst = gq.dma_start(out=self.modt[:, slot * D:(slot + 1) * D],
                                    in_=self.smod[s, pa][None, :].broadcast_to([128, D]))
            return inst
        S("gpsimd", bar, mb, f"q{s}mb")

        def z(v):
            v.memset(self.sd[:], 0.0)
            return v.memset(self.ptail[:], 0.0)
        S("vector", [f"q{s}mb"], z, f"q{s}z")

    def piece(self, s, pi):
        S = self.sec
        t0 = pi * PT
        g_ = f"s{s}p{pi}"
        dbg_on = self.dbg and s == 0 and pi == 0
        bar = self.barrier_evs()

        x3 = self.x_t.rearrange("p (g d) -> p g d", g=NG)

        h3 = self.h_t.rearrange("p (g d) -> p g d", g=NG)
        hf = self.h_f.rearrange("p (c t) -> p c t", c=DC)
        pf = self.p_f.rearrange("p (c t) -> p c t", c=DC)
        qf = self.q_f.rearrange("p (c t) -> p c t", c=DC)
        kf = self.k_f.rearrange("p (c t) -> p c t", c=DC)
        kbf = self.kb_f.rearrange("p (c t) -> p c t", c=DC)
        vf = self.v_f.rearrange("p (c t) -> p c t", c=DC)
        wf = self.w_f.rearrange("p (c t) -> p c t", c=DC)
        uf = self.u_f.rearrange("p (c t) -> p c t", c=DC)
        scf = self.sc_f.rearrange("p (c t) -> p c t", c=DC)
        prb = self.qa_s[:, 0:DC * PT].rearrange("p (c t) -> p c t", c=DC)
        pp4 = self.pp.rearrange("p (a c n) -> p a c n", a=2, c=DC)
        sd3 = self.sd.rearrange("p (c n) -> p c n", c=DC)
        pt3 = self.ptail.rearrange("p (w c e) -> p w c e", w=3, c=DC)

        def ldx(sy):
            self.dma(sy, None, x3,
                     self.dx[s, t0:t0 + PT, :].rearrange("(g p) d -> p g d", p=128),
                     label=f"{g_}_x")
            return sy.dma_start(out=self.wql.rearrange("p (c d) -> p c d", c=DC),
                                in_=self.swattn[0].rearrange("(c p) d -> p c d", p=128))
        S("sync", bar + [f"q{s}z"], ldx, f"{g_}_w0")

        def ln(pref, mslot, asl):
            for g in range(NG):
                S("scalar", [f"{pref}_lnin" if g == 0 else f"{pref}_vr{g - 1}"],
                  lambda sc, g=g: sc.activation(self.sq_t[:], x3[:, g, :], AF.Square),
                  f"{pref}_sq{g}")
                S("vector", [f"{pref}_sq{g}"], lambda v, g=g: v.tensor_reduce(
                    self.st[:, 2 + g:3 + g], self.sq_t[:, None, :], AX.X, ALU.add),
                  f"{pref}_vr{g}")

            def ln_var(v):
                v.tensor_reduce(self.st[:, 0:NG], x3, AX.X, ALU.add)
                v.tensor_scalar(self.st[:, 0:NG], self.st[:, 0:NG], 1.0 / D, None, ALU.mult)
                v.tensor_scalar(self.st[:, 2:2 + NG], self.st[:, 2:2 + NG], 1.0 / D,
                                None, ALU.mult)
                v.tensor_tensor(self.st[:, 4:4 + NG], self.st[:, 0:NG], self.st[:, 0:NG],
                                ALU.mult)
                return v.tensor_tensor(self.st[:, 4:4 + NG], self.st[:, 2:2 + NG],
                                       self.st[:, 4:4 + NG], ALU.subtract)
            S("vector", [f"{pref}_vr{NG - 1}"], ln_var, f"{pref}_var")

            def ln_rstd(sc):
                sc.activation(self.st[:, 6:6 + NG], self.st[:, 4:4 + NG], AF.Ln, bias=EPS)
                return sc.activation(self.st[:, 6:6 + NG], self.st[:, 6:6 + NG],
                                     AF.Exp, scale=np.float32(-0.5))
            S("scalar", [f"{pref}_var"], ln_rstd, f"{pref}_rstd")

            def ln_apply(v):
                for g in range(NG):
                    v.tensor_scalar(h3[:, g, :], x3[:, g, :], self.st[:, g:g + 1],
                                    self.st[:, 6 + g:7 + g], ALU.subtract, ALU.mult)
                v.tensor_tensor(h3, h3, self.modt[:, mslot * D:(mslot + 1) * D]
                                [:, None, :].broadcast_to([128, NG, D]), ALU.mult)
                return v.tensor_tensor(h3, h3, self.modt[:, asl * D:(asl + 1) * D]
                                       [:, None, :].broadcast_to([128, NG, D]), ALU.add)
            S("vector", [f"{pref}_rstd"], ln_apply, f"{pref}_h")

        self.ev[f"{g_}A_lnin"] = self.ev[f"{g_}_x"]
        ln(f"{g_}A", 0, 1)

        def hf_tr(g):
            if dbg_on:
                self.dma(g, None, self.dbg_bufs["d_h"], self.h_t[:])
            for gg in range(NG - 1):
                self.dma(g, None, hf[:, :, gg * 128:(gg + 1) * 128],
                         h3[:, gg, :].rearrange("p (c q) -> p c q", q=128),
                         transpose=True)
            return g.dma_start_transpose(
                hf[:, :, (NG - 1) * 128:NG * 128],
                h3[:, NG - 1, :].rearrange("p (c q) -> p c q", q=128))
        S("sync", [f"{g_}A_h"], hf_tr, f"{g_}_hf")

        for tgt in range(3):
            dstf = (qf, kf, vf)[tgt]
            for ocg in range(0, DC, 2):
                def pj(t, tgt=tgt, ocg=ocg):
                    for oc in (ocg, ocg + 1):
                        ps = (self.psA, self.psB)[oc % 2]
                        for kc in range(DC):
                            inst = t.matmul(
                                ps[:, 0:PT],
                                self.wql[:, kc * D + oc * 128:kc * D + (oc + 1) * 128],
                                hf[:, kc, :], start=(kc == 0), stop=(kc == DC - 1))
                    return inst
                w = [f"{g_}_w{tgt}", f"{g_}_hf"]
                if ocg >= 2:
                    w.append(f"{g_}_pe{tgt}_{ocg - 2}")
                elif tgt > 0:
                    w.append(f"{g_}_pe{tgt - 1}_{DC - 2}")
                S("tensor", w, pj, f"{g_}_pj{tgt}_{ocg}")

                def pev(v, tgt=tgt, ocg=ocg):
                    for oc in (ocg, ocg + 1):
                        ps = (self.psA, self.psB)[oc % 2]
                        inst = v.tensor_copy(pf[:, oc, 3:3 + PT], ps[:, 0:PT])
                    return inst
                S("vector", [f"{g_}_pj{tgt}_{ocg}"], pev, f"{g_}_pe{tgt}_{ocg}")

            S("vector", [f"{g_}_pe{tgt}_{DC - 2}"], lambda v, tgt=tgt: v.tensor_copy(
                pf[:, :, 0:3], pt3[:, tgt, :, :]), f"{g_}_ph{tgt}")
            if tgt < 3:
                S("sync", [f"{g_}_pj{tgt}_{DC - 2}"], lambda sy, tgt=tgt: sy.dma_start(
                    out=self.wql.rearrange("p (c d) -> p c d", c=DC),
                    in_=self.swattn[tgt + 1].rearrange("(c p) d -> p c d", p=128)),
                  f"{g_}_w{tgt + 1}")

            for ocg in range(0, DC, 2):
                def cv(t, tgt=tgt, ocg=ocg):
                    for oc in (ocg, ocg + 1):
                        ps = (self.psC, self.psG)[oc % 2]
                        for tau in range(4):
                            dg = self.diag[:, (tgt * 4 + tau) * DC * 128 + oc * 128:
                                           (tgt * 4 + tau) * DC * 128 + (oc + 1) * 128]
                            inst = t.matmul(ps[:, 0:PT], dg, pf[:, oc, tau:tau + PT],
                                            start=(tau == 0), stop=(tau == 3))
                    return inst
                w = [f"{g_}_pe{tgt}_{ocg}", f"{g_}_ph{tgt}"]
                if ocg >= 2:
                    w.append(f"{g_}_cs{tgt}_{ocg - 2}")
                elif tgt > 0:
                    w.append(f"{g_}_cs{tgt - 1}_{DC - 2}")
                S("tensor", w, cv, f"{g_}_cm{tgt}_{ocg}")

                def cse(sc, tgt=tgt, ocg=ocg, dstf=dstf):
                    for oc in (ocg, ocg + 1):
                        ps = (self.psC, self.psG)[oc % 2]
                        inst = sc.activation(dstf[:, oc, :], ps[:, 0:PT], AF.Silu)
                    return inst
                S("scalar", [f"{g_}_cm{tgt}_{ocg}"], cse, f"{g_}_cs{tgt}_{ocg}")

            S("vector", [f"{g_}_pe{tgt}_{DC - 2}"], lambda v, tgt=tgt: v.tensor_copy(
                pt3[:, tgt, :, :], pf[:, :, PT:PT + 3]), f"{g_}_pt{tgt}")

        def btm(t):
            for kc in range(DC):
                inst = t.matmul(self.psR[0:NH, 0:PT], self.wb_sb[:, kc * NH:(kc + 1) * NH],
                                hf[:, kc, :], start=(kc == 0), stop=(kc == DC - 1))
            return inst
        S("tensor", [f"{g_}_hf"], btm, f"{g_}_bt")
        S("scalar", [f"{g_}_bt"], lambda sc: sc.activation(
            self.be[:], self.psR[0:NH, 0:PT], AF.Tanh, scale=np.float32(0.5)), f"{g_}_be1")
        S("vector", [f"{g_}_be1"], lambda v: v.tensor_scalar(
            self.be[:], self.be[:], 0.5, 0.5, ALU.mult, ALU.add), f"{g_}_be")

        S("vector", [f"{g_}_cs0_{DC - 2}", f"{g_}_cs1_{DC - 2}"],
          lambda v: v.tensor_tensor(prb, qf, qf, ALU.mult), f"{g_}_pq")

        def rowmm(dst, deps, lab):
            def fn(t):
                for dc in range(DC):
                    inst = t.matmul(dst, self.colsel[:, (dc // 2) * NH:(dc // 2 + 1) * NH],
                                    prb[:, dc, :], start=(dc == 0), stop=(dc == DC - 1))
                return inst
            S("tensor", deps, fn, lab)
        rowmm(self.psR[0:NH, PT:2 * PT], [f"{g_}_pq"], f"{g_}_qq")
        S("vector", [f"{g_}_qq"], lambda v: v.tensor_tensor(prb, kf, kf, ALU.mult),
          f"{g_}_pk")
        rowmm(self.psR2[0:NH, 0:PT], [f"{g_}_pk"], f"{g_}_kk")
        S("vector", [f"{g_}_kk"], lambda v: v.tensor_tensor(prb, qf, kf, ALU.mult),
          f"{g_}_pqk")
        rowmm(self.psR2[0:NH, PT:2 * PT],
              [f"{g_}_pqk", f"{g_}_cs2_{DC - 2}"], f"{g_}_qk")

        def rqk(sc):
            sc.activation(self.rq[:], self.psR[0:NH, PT:2 * PT], AF.Ln)
            sc.activation(self.rq[:], self.rq[:], AF.Exp, scale=np.float32(-0.5))
            sc.activation(self.rk[:], self.psR2[0:NH, 0:PT], AF.Ln)
            return sc.activation(self.rk[:], self.rk[:], AF.Exp, scale=np.float32(-0.5))
        S("scalar", [f"{g_}_kk"], rqk, f"{g_}_rqk")

        def arp(v):
            v.tensor_tensor(self.r1[:], self.psR2[0:NH, PT:2 * PT], self.rq[:], ALU.mult)
            return v.tensor_tensor(self.r1[:], self.r1[:], self.rk[:], ALU.mult)
        S("vector", [f"{g_}_qk", f"{g_}_rqk"], arp, f"{g_}_arp")
        S("tensor", [f"{g_}_arp"], lambda t: t.matmul(
            self.psT[0:1, 0:PT], self.ones4[:], self.r1[:],
            start=True, stop=True), f"{g_}_arm")

        def armask(v):
            am = self.amask[0:1, s * CH:(s + 1) * CH][:, None, :].broadcast_to([1, NCHU, CH])
            return v.tensor_tensor(self.ar.rearrange("o (n c) -> o n c", c=CH),
                                   self.psT[0:1, 0:PT]
                                   .rearrange("o (n c) -> o n c", c=CH), am, ALU.mult)
        S("vector", [f"{g_}_arm"], armask, f"{g_}_ar")


        def mk_bcap(name, rows_ap, deps, fn):
            for hp in range(2):
                def bc(t, hp=hp, rows_ap=rows_ap):
                    for hh in range(2):
                        inst = t.matmul(self.psX[:, hh * PT:(hh + 1) * PT],
                                        self.hsel[0:NH, (hp * 2 + hh) * 128:
                                                  (hp * 2 + hh + 1) * 128],
                                        rows_ap, start=True, stop=True)
                    return inst
                w = list(deps) if hp == 0 else [f"{g_}_{name}ap0"]
                S("tensor", w, bc, f"{g_}_{name}bc{hp}")
                S("vector", [f"{g_}_{name}bc{hp}"], lambda v, hp=hp: fn(v, hp),
                  f"{g_}_{name}ap{hp}")

        def ap_gen(bufs):
            def fn(v, hp):
                for dc in range(4 * hp, 4 * hp + 4):
                    px = self.psX[:, (dc // 2 - 2 * hp) * PT:(dc // 2 - 2 * hp + 1) * PT]
                    for dst, src in bufs:
                        inst = v.tensor_tensor(dst[:, dc, :], src[:, dc, :], px, ALU.mult)
                return inst
            return fn

        mk_bcap("rk", self.rk[:], [f"{g_}_rqk", f"{g_}_pqk"], ap_gen([(kf, kf)]))
        mk_bcap("rq", self.rq[:], [f"{g_}_rkap1"], ap_gen([(qf, qf)]))
        mk_bcap("be", self.be[:], [f"{g_}_rqap1", f"{g_}_be", f"{g_}_cs2_{DC - 2}"],
                ap_gen([(kbf, kf), (vf, vf)]))

        def abm(t):
            ones_row = self.ones4[0:1, 0:1].broadcast_to([1, 128])
            return t.matmul(self.psX[:, 0:PT], ones_row, self.ar[:], start=True, stop=True)
        S("tensor", [f"{g_}_ar", f"{g_}_beap1"], abm, f"{g_}_abm")
        S("vector", [f"{g_}_abm"], lambda v: v.tensor_copy(self.a_b[:], self.psX[:, 0:PT]),
          f"{g_}_ab")

        if dbg_on:
            def dq(g):
                for nm, buf in (("d_q", self.q_f), ("d_k", self.k_f), ("d_v", self.v_f),
                                ("d_kb", self.kb_f)):
                    self.dma(g, None, self.dbg_bufs[nm], buf[:])
                return g.dma_start(out=self.dbg_bufs["d_ab"], in_=self.a_b[:])
            S("gpsimd", [f"{g_}_beap1", f"{g_}_ab"], dq, f"{g_}_dq")

        def tok_tr(g):
            kbt4 = self.kb_t.rearrange("p (g c q) -> p g c q", g=NG, q=128)
            vbt4 = self.h_f.rearrange("p (g c q) -> p g c q", g=NG, q=128)
            for dc in range(DC):
                self.dma(g, None, kbt4[:, :, dc, :], kbf[:, dc, :], transpose=True)
            for dc in range(DC - 1):
                self.dma(g, None, vbt4[:, :, dc, :], vf[:, dc, :], transpose=True)
            return g.dma_start_transpose(vbt4[:, :, DC - 1, :], vf[:, DC - 1, :])
        S("sync", [f"{g_}_beap1"], tok_tr, f"{g_}_tok")

        def grams(t):
            for pr_i in range(NPAIR):
                for dc in range(DC):
                    t.matmul(self.psG[:, pr_i * 128:(pr_i + 1) * 128],
                             kbf[:, dc, pr_i * 128:(pr_i + 1) * 128],
                             kf[:, dc, pr_i * 128:(pr_i + 1) * 128],
                             start=(dc == 0), stop=(dc == DC - 1))
                for dc in range(DC):
                    inst = t.matmul(self.psG[:, 256 + pr_i * 128:256 + (pr_i + 1) * 128],
                                    kf[:, dc, pr_i * 128:(pr_i + 1) * 128],
                                    kbf[:, dc, pr_i * 128:(pr_i + 1) * 128],
                                    start=(dc == 0), stop=(dc == DC - 1))
            return inst
        S("tensor", [f"{g_}_beap1", f"{g_}_ar"], grams, f"{g_}_gr")

        def masks(v):
            mk = lambda i: self.mask3[:, i * 128:(i + 1) * 128][:, None, :].broadcast_to([128, NPAIR, 128])
            g2 = self.psG[:, 0:256].rearrange("p (r q) -> p r q", q=128)
            gt2 = self.psG[:, 256:512].rearrange("p (r q) -> p r q", q=128)
            v.tensor_tensor(self.m_s.rearrange("p (r q) -> p r q", q=128), g2, mk(0),
                            ALU.mult)
            v.tensor_tensor(self.mn_s.rearrange("p (r q) -> p r q", q=128), gt2, mk(2),
                            ALU.mult)
            return v.tensor_tensor(self.mt_s.rearrange("p (r q) -> p r q", q=128), gt2,
                                   mk(1), ALU.mult)
        S("vector", [f"{g_}_gr"], masks, f"{g_}_msk")
        if dbg_on:
            S("gpsimd", [f"{g_}_msk"], lambda g: g.dma_start(
                out=self.dbg_bufs["d_m"], in_=self.m_s[:]), f"{g_}_dm")

        prev_lab = f"{g_}_msk"
        natof = {1: self.m_s, 2: self.pwn[0], 4: self.pwn[1], 8: self.pwn[0],
                 16: self.pwn[1]}
        mtof = {1: self.mt_s}
        for lv in (2, 4, 8, 16, 32):
            mtof[lv] = self.pw[-lv]
        cur = self.m_s
        nap = 0
        for step, lv in (("sq", 2), ("sq", 4), ("ap", 2), ("sq", 8), ("ap", 4),
                         ("sq", 16), ("ap", 8), ("sq", 32), ("ap", 16), ("ap", 32),
                         ("ap", 0)):
            if step == "sq":
                prev, prevt = natof[lv // 2], mtof[lv // 2]
                last = lv == 32

                def pwmm(t, prev=prev, prevt=prevt, last=last):
                    for pr_i in range(NPAIR):
                        sl = slice(pr_i * 128, (pr_i + 1) * 128)
                        if not last:
                            t.matmul(self.psT[:, sl], prevt[:, sl], prev[:, sl],
                                     start=True, stop=True)
                        inst = t.matmul(
                            self.psT[:, 256 + pr_i * 128:256 + (pr_i + 1) * 128],
                            prev[:, sl], prevt[:, sl], start=True, stop=True)
                    return inst
                S("tensor", [prev_lab], pwmm, f"{g_}_pw{lv}")

                def pwe(sc, lv=lv, last=last):
                    if not last:
                        sc.copy(natof[lv][:], self.psT[:, 0:256])
                    return sc.copy(mtof[lv][:], self.psT[:, 256:512])
                S("scalar", [f"{g_}_pw{lv}"], pwe, f"{g_}_pwe{lv}")
                prev_lab = f"{g_}_pwe{lv}"
            else:
                lhs = mtof[lv] if lv else self.mn_s

                def tcm(t, lhs=lhs, cur=cur):
                    for pr_i in range(NPAIR):
                        sl = slice(pr_i * 128, (pr_i + 1) * 128)
                        t.matmul(self.psT[:, sl], lhs[:, sl], cur[:, sl],
                                 start=True, stop=False)
                        inst = t.matmul(self.psT[:, sl], self.eye_f[:], cur[:, sl],
                                        start=False, stop=True)
                    return inst
                S("tensor", [prev_lab], tcm, f"{g_}_tc{nap}")
                S("scalar", [f"{g_}_tc{nap}"],
                  lambda sc: sc.copy(self.tc0[:], self.psT[:, 0:256]),
                  f"{g_}_tce{nap}")
                prev_lab = f"{g_}_tce{nap}"
                cur = self.tc0
                nap += 1

        def ttr(t):
            for pr_i in range(NPAIR):
                inst = t.transpose(self.psT[:, 256 + pr_i * 128:256 + (pr_i + 1) * 128],
                                   self.tc1[:, pr_i * 128:(pr_i + 1) * 128], self.eye_f[:])
            return inst
        S("tensor", [prev_lab], ttr, f"{g_}_ttr")

        def ttev(v):
            return v.scalar_tensor_tensor(
                self.tt_s.rearrange("p (r q) -> p r q", q=128),
                self.psT[:, 256:512].rearrange("p (r q) -> p r q", q=128), -3.0,
                self.eye_b[:, None, :].broadcast_to([128, NPAIR, 128]),
                ALU.mult, ALU.add)
        S("vector", [f"{g_}_ttr"], ttev, f"{g_}_tt")
        if dbg_on:
            S("gpsimd", [f"{g_}_tt"], lambda g: g.dma_start(
                out=self.dbg_bufs["d_tt"], in_=self.tt_s[:]), f"{g_}_dtt")

        kbt3 = self.kb_t.rearrange("p (g d) -> p g d", g=NG)
        vbt3 = self.h_f.rearrange("p (g d) -> p g d", g=NG)
        for ri in range(8):
            ten, dc0 = ri // 4, (ri % 4) * 2
            src3 = kbt3 if ten == 0 else vbt3
            ps = (self.psG, self.psC)[ri % 2]

            def wumm(t, src3=src3, dc0=dc0, ps=ps):
                for dq in range(2):
                    for pr_i in range(NPAIR):
                        inst = t.matmul(
                            ps[:, dq * 256 + pr_i * 128:dq * 256 + (pr_i + 1) * 128],
                            src3[:, pr_i, (dc0 + dq) * 128:(dc0 + dq + 1) * 128],
                            self.tt_s[:, pr_i * 128:(pr_i + 1) * 128],
                            start=True, stop=True)
                return inst
            w = [f"{g_}_tt", f"{g_}_tok"]
            if ri >= 2:
                w.append(f"{g_}_wue{ri - 2}")
            S("tensor", w, wumm, f"{g_}_wu{ri}")
            dstf = wf if ten == 0 else uf
            S("scalar", [f"{g_}_wu{ri}"], lambda sc, dstf=dstf, dc0=dc0, ps=ps: sc.copy(
                dstf[:, dc0:dc0 + 2, :].rearrange("p c t -> p (c t)"), ps[:, 0:512]),
              f"{g_}_wue{ri}")

        if dbg_on:
            def dwu(g):
                self.dma(g, None, self.dbg_bufs["d_w"], self.w_f[:])
                return g.dma_start(out=self.dbg_bufs["d_u"], in_=self.u_f[:])
            S("gpsimd", [f"{g_}_wue7"], dwu, f"{g_}_dwu")

        def p1p2(v):
            v.tensor_tensor(scf, kf, uf, ALU.mult)
            v.tensor_reduce(pp4[:, 0, :, :],
                            self.sc_f.rearrange("p (c n t) -> p c n t", c=DC, n=NCHU),
                            AX.X, ALU.add)
            v.tensor_tensor(scf, kf, wf, ALU.mult)
            v.tensor_reduce(pp4[:, 1, :, :],
                            self.sc_f.rearrange("p (c n t) -> p c n t", c=DC, n=NCHU),
                            AX.X, ALU.add)
            return v.tensor_scalar(pp4[:, 1, :, :], pp4[:, 1, :, :], -1.0, 1.0,
                                   ALU.mult, ALU.add)
        S("vector", [f"{g_}_wue7", f"{g_}_rkap1"], p1p2, f"{g_}_pp")
        if dbg_on:
            S("gpsimd", [f"{g_}_pp"], lambda g: g.dma_start(
                out=self.dbg_bufs["d_pp"], in_=self.pp[:]), f"{g_}_dpp")

        def scan(v):
            for dc in range(DC):
                inst = v.tensor_tensor_scan(sd3[:, dc, 1:NCHU + 1],
                                            pp4[:, 1, dc, :], pp4[:, 0, dc, :],
                                            sd3[:, dc, 0:1], ALU.mult, ALU.add)
            return inst
        S("vector", [f"{g_}_pp"], scan, f"{g_}_scan")

        def opass(v):
            ab3 = self.a_b[:, None, :].broadcast_to([128, DC, PT])
            v.tensor_tensor(scf, ab3, wf, ALU.mult)
            v.tensor_tensor(qf, qf, scf, ALU.subtract)
            v.tensor_tensor(uf, ab3, uf, ALU.mult)
            sdb = sd3[:, :, 0:NCHU][:, :, :, None].broadcast_to([128, DC, NCHU, CH])
            v.tensor_tensor(self.sc_f.rearrange("p (c n t) -> p c n t", c=DC, n=NCHU),
                            qf.rearrange("p c (n t) -> p c n t", n=NCHU), sdb, ALU.mult)
            v.tensor_tensor(uf, scf, uf, ALU.add)
            return v.tensor_copy(sd3[:, :, 0], sd3[:, :, NCHU])
        S("vector", [f"{g_}_scan", f"{g_}_ab", f"{g_}_rqap1"], opass, f"{g_}_o")
        if dbg_on:
            S("gpsimd", [f"{g_}_o"], lambda g: g.dma_start(
                out=self.dbg_bufs["d_o"], in_=self.u_f[:]), f"{g_}_do")

        S("vector", [f"{g_}_o"], lambda v: v.tensor_tensor(prb, uf, uf, ALU.mult),
          f"{g_}_po")
        rowmm(self.psR[0:NH, 0:PT], [f"{g_}_po"], f"{g_}_osq")

        def orstd(sc):
            sc.activation(self.ro[:], self.psR[0:NH, 0:PT], AF.Ln,
                          bias=EPS, scale=np.float32(1.0 / HD))
            return sc.activation(self.ro[:], self.ro[:], AF.Exp, scale=np.float32(-0.5))
        S("scalar", [f"{g_}_osq"], orstd, f"{g_}_ro")

        mk_bcap("ro", self.ro[:], [f"{g_}_ro", f"{g_}_abm"], ap_gen([(uf, uf)]))

        for gi in range(NG):
            for os_i in range(2):
                def wom(t, gi=gi, os_i=os_i):
                    for kc in range(DC):
                        inst = t.matmul(
                            (self.psA, self.psB)[os_i][:, 0:512],
                            uf[:, kc, gi * 128:(gi + 1) * 128],
                            self.wql[:, kc * D + os_i * 512:kc * D + (os_i + 1) * 512],
                            start=(kc == 0), stop=(kc == DC - 1))
                    return inst
                w = [f"{g_}_roap1", f"{g_}_w3"]
                if gi > 0:
                    w.append(f"{g_}_x2_{gi - 1}_{os_i}")
                S("tensor", w, wom, f"{g_}_wo{gi}_{os_i}")

                def x2ev(v, gi=gi, os_i=os_i):
                    ps = (self.psA, self.psB)[os_i]
                    gab = self.modt[:, 4 * D + os_i * 512:4 * D + (os_i + 1) * 512]
                    v.tensor_tensor(self.sc_f[:, 0:512], ps[:, 0:512], gab, ALU.mult)
                    return v.tensor_tensor(x3[:, gi, os_i * 512:(os_i + 1) * 512],
                                           self.sc_f[:, 0:512],
                                           x3[:, gi, os_i * 512:(os_i + 1) * 512],
                                           ALU.add)
                S("vector", [f"{g_}_wo{gi}_{os_i}"], x2ev, f"{g_}_x2_{gi}_{os_i}")

        if dbg_on:
            S("gpsimd", [f"{g_}_x2_{NG - 1}_1"], lambda g: g.dma_start(
                out=self.dbg_bufs["d_x2"], in_=self.x_t[:]), f"{g_}_dx2")

        self.ev[f"{g_}B_lnin"] = self.ev[f"{g_}_x2_{NG - 1}_1"]
        ln(f"{g_}B", 2, 3)

        S("vector", [f"{g_}B_h"], lambda v: self.scales_dve(
            v, self.h_t[:], self.ss_t, NG * D), f"{g_}_h2s")
        S("scalar", [f"{g_}_h2s"], lambda sc: self.recip_act(
            sc, self.rs_t, self.ss_t, NG * D // 16), f"{g_}_h2r")
        def h2lad(v):
            for ch in range(2):
                inst = self.ladder(v, self.h_t[:, ch * 1024:(ch + 1) * 1024],
                                   self.hq_t[:, ch * 1024:(ch + 1) * 1024],
                                   self.ss_t[:, ch * 64:], self.rs_t[:, ch * 64:], 1024)
            return inst
        S("vector", [f"{g_}_h2r"], h2lad, f"{g_}_h2q")
        if dbg_on:
            S("gpsimd", [f"{g_}_h2q"], lambda g: g.dma_start(
                out=self.dbg_bufs["d_h2q"], in_=self.hq_t[:]), f"{g_}_dh2q")

        hq3 = self.hq_t.rearrange("p (g d) -> p g d", g=NG)
        hqf = self.hq_f.rearrange("p (c t) -> p c t", c=DC)
        def hqf_tr(g):
            for gg in range(NG - 1):
                self.dma(g, None, hqf[:, :, gg * 128:(gg + 1) * 128],
                         hq3[:, gg, :].rearrange("p (c q) -> p c q", q=128),
                         transpose=True)
            return g.dma_start_transpose(
                hqf[:, :, (NG - 1) * 128:NG * 128],
                hq3[:, NG - 1, :].rearrange("p (c q) -> p c q", q=128))
        S("sync", [f"{g_}_h2q"], hqf_tr, f"{g_}_hqf")

        h3f = self.h3_f.rearrange("p (c t) -> p c t", c=KC3)
        for gi in range(NG):
            gt = f"{g_}_f{gi}"
            for osi, (o0, ow) in enumerate(OSL):
                nt = ow // 128

                def ldw(sy, o0=o0, ow=ow, nt=nt):
                    w1 = self.wfs[:, 0:DC * ow].rearrange("p (c a q) -> p c a q",
                                                          a=nt, q=128)
                    w2 = self.wfs2[:, 0:DC * ow].rearrange("p (c a q) -> p c a q",
                                                           a=nt, q=128)
                    inst = None
                    for a in range(nt):
                        self.dma(sy, None, w1[:, :, a, :],
                                 self.swg[o0 + a * 128:o0 + (a + 1) * 128, :],
                                 transpose=True)
                        if a < nt - 1:
                            self.dma(sy, None, w2[:, :, a, :],
                                     self.swu[o0 + a * 128:o0 + (a + 1) * 128, :],
                                     transpose=True)
                        else:
                            inst = sy.dma_start_transpose(
                                w2[:, :, a, :],
                                self.swu[o0 + a * 128:o0 + (a + 1) * 128, :])
                    return inst
                w = [f"{g_}_hqf"]
                if osi >= 1:
                    w.append(f"{gt}_gmm{osi - 1}")
                elif gi == 1:
                    w.append(f"{g_}_f0_gmm{len(OSL) - 1}")
                S("sync", w, ldw, f"{gt}_wu{osi}")

                def gumm(t, o0=o0, ow=ow, nt=nt, gi=gi, osi=osi):
                    wv = self.wfs[:, 0:DC * ow].rearrange("p (c a q) -> p c (a q)",
                                                          a=nt, q=128)
                    wv2 = self.wfs2[:, 0:DC * ow].rearrange("p (c a q) -> p c (a q)",
                                                            a=nt, q=128)
                    for kc in range(DC):
                        t.matmul(self.psA[:, 0:ow], hqf[:, kc, gi * 128:(gi + 1) * 128],
                                 wv[:, kc, :], start=(kc == 0), stop=(kc == DC - 1))
                    for kc in range(DC):
                        inst = t.matmul(self.psB[:, 0:ow],
                                        hqf[:, kc, gi * 128:(gi + 1) * 128],
                                        wv2[:, kc, :], start=(kc == 0), stop=(kc == DC - 1))
                    return inst
                w = [f"{gt}_wu{osi}"]
                if osi >= 1:
                    w += [f"{gt}_gu{osi - 1}", f"{gt}_sg{osi - 1}"]
                elif gi == 0:
                    w.append(f"{g_}_x2_{NG - 1}_1")
                else:
                    w.append(f"{g_}_f0_hm")
                S("tensor", w, gumm, f"{gt}_gmm{osi}")
                S("scalar", [f"{gt}_gmm{osi}"], lambda sc, o0=o0, ow=ow: sc.activation(
                    self.sg_t[:, o0:o0 + ow], self.psA[:, 0:ow], AF.Silu), f"{gt}_sg{osi}")
                S("vector", [f"{gt}_gmm{osi}"], lambda v, o0=o0, ow=ow: v.tensor_copy(
                    self.up_t[:, o0:o0 + ow], self.psB[:, 0:ow]), f"{gt}_gu{osi}")

            S("vector", [f"{gt}_gu{len(OSL) - 1}", f"{gt}_sg{len(OSL) - 1}"],
              lambda v: v.tensor_tensor(self.sg_t[:], self.sg_t[:], self.up_t[:],
                                        ALU.mult), f"{gt}_hm")
            if dbg_on and gi == 0:
                S("gpsimd", [f"{gt}_hm"], lambda g: g.dma_start(
                    out=self.dbg_bufs["d_hm"], in_=self.sg_t[:]), f"{gt}_dhm")
            S("vector", [f"{gt}_hm"], lambda v: self.scales_dve(
                v, self.sg_t[:], self.ss_t, I3), f"{gt}_hms")
            S("scalar", [f"{gt}_hms"], lambda sc: self.recip_act(
                sc, self.rs_t, self.ss_t, NB3), f"{gt}_hmr")
            def hmlad(v):
                for f0, F in ((0, 1024), (1024, 1024), (2048, 768)):
                    inst = self.ladder(v, self.sg_t[:, f0:f0 + F],
                                       self.up_t[:, f0:f0 + F],
                                       self.ss_t[:, f0 // 16:], self.rs_t[:, f0 // 16:],
                                       F)
                return inst
            S("vector", [f"{gt}_hmr"], hmlad, f"{gt}_hmq")
            if dbg_on and gi == 0:
                S("gpsimd", [f"{gt}_hmq"], lambda g: g.dma_start(
                    out=self.dbg_bufs["d_hmq"], in_=self.up_t[:]), f"{gt}_dhmq")

            w = [f"{gt}_hmq"] + ([f"{g_}_f0_dmm3"] if gi == 1 else [])
            S("sync", w, lambda g: g.dma_start_transpose(
                h3f, self.up_t.rearrange("p (c q) -> p c q", q=128)), f"{gt}_h3f")

            for osi in range(4):
                def ldwd(sy, osi=osi):
                    wv = self.wds[:, 0:KC3 * 256].rearrange("p (c a q) -> p c a q",
                                                            a=2, q=128)
                    self.dma(sy, None, wv[:, :, 0, :],
                             self.swd[osi * 256:osi * 256 + 128, :], transpose=True)
                    return sy.dma_start_transpose(
                        wv[:, :, 1, :], self.swd[osi * 256 + 128:(osi + 1) * 256, :])
                w = [f"{gt}_h3f"]
                if osi > 0:
                    w.append(f"{gt}_dmm{osi - 1}")
                elif gi > 0:
                    w.append(f"{g_}_f0_dmm3")
                S("sync", w, ldwd, f"{gt}_wd{osi}")

                def dnm(t, osi=osi):
                    wv = self.wds[:, 0:KC3 * 256].rearrange("p (c a q) -> p c (a q)",
                                                            a=2, q=128)
                    for kc in range(KC3):
                        inst = t.matmul(self.psC[:, 0:256], h3f[:, kc, :], wv[:, kc, :],
                                        start=(kc == 0), stop=(kc == KC3 - 1))
                    return inst
                w = [f"{gt}_wd{osi}"]
                if osi >= 1:
                    w.append(f"{gt}_dn{osi - 1}")
                elif gi == 0:
                    w.append(f"{g_}_wue7")
                else:
                    w.append(f"{g_}_f0_dn3")
                S("tensor", w, dnm, f"{gt}_dmm{osi}")

                def outev(v, osi=osi, gi=gi):
                    gbb = self.modt[:, 5 * D + osi * 256:5 * D + (osi + 1) * 256]
                    v.tensor_tensor(self.sc_f[:, 0:256], self.psC[:, 0:256], gbb, ALU.mult)
                    gq, rr = divmod(osi * 256, 1024)
                    return v.tensor_tensor(x3[:, gi, osi * 256:(osi + 1) * 256],
                                           self.sc_f[:, 0:256],
                                           x3[:, gi, osi * 256:(osi + 1) * 256], ALU.add)
                S("vector", [f"{gt}_dmm{osi}"], outev, f"{gt}_dn{osi}")

        S("gpsimd", [f"{g_}_f0_dn3", f"{g_}_f1_dn3"], lambda g: g.dma_start(
            out=self.dout[s, t0:t0 + PT, :].rearrange("(g p) d -> p g d", p=128),
            in_=x3), f"{g_}_out")

    def build(self):
        with contextlib.ExitStack() as es:
            self.declare()
            self.alloc(es)
            self.p0()
            for s in range(self.nseq):
                self.seq_setup(s)
                for pi in range(self.npiece):
                    self.piece(s, pi)
            fin = [self.cur(s) for s in
                   ("vv", "aa", "tt", *SDPOOL, *GDPOOL)]
            pr = self

            def final(sy):
                pr.wait(sy, *fin)
            self.block.sync(final)
        return self.nc


def _consts():
    eye = np.eye(128, dtype=np.float32)
    m_tril = np.zeros((128, 128), np.float32)
    m_triu = np.zeros((128, 128), np.float32)
    for blk in range(2):
        for i in range(64):
            jj = np.arange(64)
            m_tril[blk * 64 + i, blk * 64 + jj[jj < i]] = 1.0
            m_triu[blk * 64 + i, blk * 64 + jj[jj > i]] = 1.0
    mask3 = np.stack([m_tril, m_triu, -m_triu])
    hsel = np.zeros((NH, NH * 128), np.float32)
    for h in range(NH):
        hsel[h, h * 128:(h + 1) * 128] = 1.0
    colsel = np.zeros((128, NH * NH), np.float32)
    for h in range(NH):
        colsel[:, h * NH + h] = 1.0
    ones4 = np.ones((NH, 1), np.float32)
    return eye, mask3, hsel, colsel, ones4


def _diag_const(conv_q, conv_k, conv_v):
    cd = np.zeros((12, DC, 128, 128), np.float32)
    for ti, cw in enumerate((conv_q, conv_k, conv_v)):
        w = np.asarray(cw, np.float32)[:, 0, :]
        for tau in range(4):
            for dc in range(DC):
                np.fill_diagonal(cd[ti * 4 + tau, dc], w[dc * 128:(dc + 1) * 128, tau])
    return cd


_NC_CACHE = {}


def _in_maps(L0_attn_x, L0_attn_Wq, L0_attn_Wk, L0_attn_Wv, L0_attn_Wb,
             L0_attn_conv_q, L0_attn_conv_k, L0_attn_conv_v, L0_attn_g,
             L0_attn_Wo, L0_ffn_w_gate, L0_ffn_w_up, L0_ffn_w_down,
             L0_c, L0_wmod):
    x = np.ascontiguousarray(np.asarray(L0_attn_x, np.float32))
    eye, mask3, hsel, colsel, ones4 = _consts()
    cdiag = _diag_const(L0_attn_conv_q, L0_attn_conv_k, L0_attn_conv_v)
    g = np.asarray(L0_attn_g, np.float32)
    gcol = np.ascontiguousarray(g.reshape(2, 128).T)
    base = {
        "wmod": np.ascontiguousarray(np.asarray(L0_wmod, np.float32)),
        "Wq": np.ascontiguousarray(np.asarray(L0_attn_Wq, np.float32)),
        "Wk": np.ascontiguousarray(np.asarray(L0_attn_Wk, np.float32)),
        "Wv": np.ascontiguousarray(np.asarray(L0_attn_Wv, np.float32)),
        "Wo": np.ascontiguousarray(np.asarray(L0_attn_Wo, np.float32)),
        "Wb": np.ascontiguousarray(np.asarray(L0_attn_Wb, np.float32)),
        "wg": np.ascontiguousarray(np.asarray(L0_ffn_w_gate, np.float32)),
        "wu": np.ascontiguousarray(np.asarray(L0_ffn_w_up, np.float32)),
        "wd": np.ascontiguousarray(np.asarray(L0_ffn_w_down, np.float32)),
        "cdiag": cdiag, "gcol": gcol, "eye": eye,
        "eyeb": eye.astype(BF), "mask3": mask3.astype(BF),
        "hsel": hsel, "colsel": colsel.astype(BF),
        "ones4": ones4,
    }
    cc = np.asarray(L0_c, np.float32)
    in_maps = []
    for ci in range(NCORE):
        m = dict(base)
        m["x"] = np.ascontiguousarray(x[ci * BPC:(ci + 1) * BPC])
        m["cc"] = np.ascontiguousarray(cc[ci * BPC:(ci + 1) * BPC])
        am = np.zeros((1, BPC * CH), np.float32)
        for sloc in range(BPC):
            gb = ci * BPC + sloc
            am[0, sloc * CH:sloc * CH + gb + 1] = 1.0
        m["amask"] = am
        in_maps.append(m)
    return in_maps


def kernel(L0_attn_x, L0_attn_Wq, L0_attn_Wk, L0_attn_Wv, L0_attn_Wb,
           L0_attn_conv_q, L0_attn_conv_k, L0_attn_conv_v, L0_attn_g, L0_attn_Wo,
           L0_ffn_w_gate, L0_ffn_w_up, L0_ffn_w_down, L0_c, L0_wmod):
    in_maps = _in_maps(L0_attn_x, L0_attn_Wq, L0_attn_Wk, L0_attn_Wv,
                       L0_attn_Wb, L0_attn_conv_q, L0_attn_conv_k,
                       L0_attn_conv_v, L0_attn_g, L0_attn_Wo, L0_ffn_w_gate,
                       L0_ffn_w_up, L0_ffn_w_down, L0_c, L0_wmod)
    if "nc" not in _NC_CACHE:
        _NC_CACHE["nc"] = Bld(window=0).build()
    nc = _NC_CACHE["nc"]
    res = run_bass_kernel_spmd(nc, in_maps, core_ids=list(range(NCORE)))
    return np.concatenate([r["out"] for r in res.results], axis=0)
